# version 24
# baseline (speedup 1.0000x reference)
import numpy as np
import ml_dtypes
import concourse.bass as bass
import concourse.mybir as mybir
from concourse.bass import IndirectOffsetOnAxis
from concourse.bass_utils import run_bass_kernel_spmd


F32 = mybir.dt.float32
BF16 = mybir.dt.bfloat16
U32 = mybir.dt.uint32
AF = mybir.ActivationFunctionType
ALU = mybir.AluOpType
AX = mybir.AxisListType

P = 128
D = 1024
S = 4096
NT = S // P
NDC = D // P
NPAIR = 8
DH = 64
NE = 8
EH = 2048
SHID = 4096
CAP = 10240.0
CLOC = 1536
NSLOT = NE * CLOC
HUGE = 1 << 20
NCORES = 8
SB3 = 512
NSB3 = S // SB3
NHT = SHID // P
BISECT_ITERS = 18

STAGE = 3
DEFER_NORM = True

bf16 = ml_dtypes.bfloat16

OFF_A = 0
OFF_B = 64 * 1024
OFF_C = 128 * 1024
ARENA = 160 * 1024


class Plan:
    ENGS = ("sync", "gpsimd", "tensor", "vector", "scalar")

    def __init__(self, nc):
        self.nc = nc
        self.q = {e: [] for e in self.ENGS}
        self.sems = {}
        self.cnt = {}
        self.floors = {}
        self.bounds_reg = [None]

    def breg(self, engine):
        if self.bounds_reg[0] is None:
            self.bounds_reg[0] = engine.to_reg(NSLOT - 1)
        return self.bounds_reg[0]

    def sem(self, name):
        self.sems[name] = self.nc.alloc_semaphore(name)
        self.cnt[name] = 0
        return name

    def op(self, eng, fn, waits=(), inc=None, inc_k=1):
        w = tuple((s, v) for (s, v) in waits if v > 0)
        if inc is not None:
            self.cnt[inc] += inc_k
        after = self.cnt[inc] if inc else None
        sems = self.sems

        floors = self.floors.setdefault(eng, {})

        def emit(engine, fn=fn, w=w, inc=inc, inc_k=inc_k, floors=floors):
            live = [(s, v) for (s, v) in w if floors.get(s, 0) < v]
            for (s, v) in live:
                floors[s] = v
            for (s, v) in live[1:]:
                engine.wait_ge(sems[s], v)
            ins = fn(engine)
            if live:
                ins._wait_ge(sems[live[0][0]], live[0][1])
            if inc is not None:
                ins.then_inc(sems[inc], inc_k)

        self.q[eng].append(emit)
        return after

    def ch(self, eng, fn, waits=()):
        name = "ch_" + eng
        prev = self.cnt[name]
        return self.op(eng, fn, waits=list(waits) + [(name, prev)], inc=name)

    def finalize(self):
        nc = self.nc
        with nc.Block() as block:
            for eng in self.ENGS:
                fns = self.q[eng]
                if not fns:
                    continue

                def body(engine, fns=fns):
                    for f in fns:
                        f(engine)

                getattr(block, eng)(body)


def W(*pairs):
    return list(pairs)


def host_constants():
    c = {}
    c["ident_bf"] = np.eye(P, dtype=np.float32).astype(bf16)
    perm = np.zeros((P, P), np.float32)
    for h in range(2):
        for i in range(32):
            perm[h * 64 + i, h * 64 + 32 + i] = 1.0
            perm[h * 64 + 32 + i, h * 64 + i] = 1.0
    c["p128_bf"] = perm.astype(bf16)
    freqs = 1.0 / (10000.0 ** (np.arange(32, dtype=np.float64) / 32.0))
    ang = np.arange(S, dtype=np.float64)[None, :] * freqs[:, None]
    Cc = np.zeros((P, S), np.float64)
    Sg = np.zeros((P, S), np.float64)
    for l in range(P):
        i = l % 32
        Cc[l] = np.cos(ang[i])
        Sg[l] = np.sin(ang[i]) * (-1.0 if (l % 64) < 32 else 1.0)
    c["cosT"] = Cc.astype(np.float32).astype(bf16)
    c["sinT"] = Sg.astype(np.float32).astype(bf16)
    c["lstrict_bf"] = np.tril(np.ones((P, P), np.float32), -1).astype(bf16)
    dm = np.zeros((4, P, 1024), np.float32)
    for i in range(4):
        for p in range(P):
            keep = (np.arange(512) >= i * P + p).astype(np.float32)
            dm[i, p, 0:512] = keep
            dm[i, p, 512:1024] = keep
    c["dmask"] = dm.astype(bf16)
    c["huge_f"] = (np.float32(HUGE) + np.arange(P, dtype=np.float32))[:, None] * np.ones(
        (1, NT), np.float32)
    c["zer_bf"] = np.zeros((P, D), np.float32).astype(bf16)
    return c


def build():
    nc = bass.Bass("TRN2", target_bir_lowering=False, detect_race_conditions=False)
    pl = Plan(nc)
    C = pl.cnt

    x_d = nc.dram_tensor("x", [S, D], F32, kind="ExternalInput")
    wq_d = nc.dram_tensor("wq", [D, D], F32, kind="ExternalInput")
    wk_d = nc.dram_tensor("wk", [D, D], F32, kind="ExternalInput")
    wv_d = nc.dram_tensor("wv", [D, D], F32, kind="ExternalInput")
    wo_d = nc.dram_tensor("wo", [D, D], F32, kind="ExternalInput")
    bo_d = nc.dram_tensor("bo", [1, D], F32, kind="ExternalInput")
    pwt_d = nc.dram_tensor("pwt", [D, NE], F32, kind="ExternalInput")
    pb_d = nc.dram_tensor("pb", [1, NE], F32, kind="ExternalInput")
    eb_d = nc.dram_tensor("eb", [1, NE], F32, kind="ExternalInput")
    f1w_d = nc.dram_tensor("f1w", [NE, D, EH], F32, kind="ExternalInput")
    f1b_d = nc.dram_tensor("f1b", [NE, P, EH // P], F32, kind="ExternalInput")
    f2w_d = nc.dram_tensor("f2w", [NE, EH, D], F32, kind="ExternalInput")
    f2b_d = nc.dram_tensor("f2b", [NE, 1, D], F32, kind="ExternalInput")
    s1w_d = nc.dram_tensor("s1w", [D, SHID], F32, kind="ExternalInput")
    s1b_d = nc.dram_tensor("s1b", [P, NHT], F32, kind="ExternalInput")
    s2w_d = nc.dram_tensor("s2w", [SHID, D], F32, kind="ExternalInput")
    s2b_d = nc.dram_tensor("s2b", [1, D], F32, kind="ExternalInput")
    out_d = nc.dram_tensor("out", [S, D], F32, kind="ExternalOutput")

    xn_d = nc.dram_tensor("xn_d", [NT, P, D], BF16)
    xnT_d = nc.dram_tensor("xnT_d", [NDC, P, S], BF16)
    attn_d = nc.dram_tensor("attn_d", [NT, P, D], BF16)
    ctxT_d = nc.dram_tensor("ctxT_d", [NPAIR, P, S], BF16)
    h1_d = nc.dram_tensor("h1_d", [NSB3, NHT, P, SB3], BF16)
    xe_d = nc.dram_tensor("xe_d", [NSLOT, D], BF16)
    oe_d = nc.dram_tensor("oe_d", [NSLOT, D], BF16)
    rt_d = nc.dram_tensor("rt_d", [4, S], F32)
    rtall_d = nc.dram_tensor("rtall_d", [4 * NCORES, S], F32)

    hcst = host_constants()
    ident_c = nc.inline_tensor(hcst["ident_bf"], "ident_c")
    p128_c = nc.inline_tensor(hcst["p128_bf"], "p128_c")
    cos_c = nc.inline_tensor(hcst["cosT"], "cos_c")
    sin_c = nc.inline_tensor(hcst["sinT"], "sin_c")
    lstrict_c = nc.inline_tensor(hcst["lstrict_bf"], "lstrict_c")
    huge_c = nc.inline_tensor(hcst["huge_f"], "huge_c")
    dmask_c = nc.inline_tensor(hcst["dmask"].reshape(4 * P, 1024), "dmask_c")
    zer_c = nc.inline_tensor(hcst["zer_bf"], "zer_c")

    for s in ("ch_vector", "ch_scalar", "ch_gpsimd", "pe",
              "ldq", "gq", "stq", "cc", "xa", "xb", "wa", "wb",
              "spillA", "spillB", "spillC", "spillH", "moew", "asm",
              "att", "attb"):
        pl.sem(s)

    sbt = nc.alloc_sbuf_tensor
    ident_s = sbt("ident_s", [P, P], BF16)
    p128_s = sbt("p128_s", [P, P], BF16)
    lstrict_s = sbt("lstrict_s", [P, P], BF16)
    ones_bf = sbt("ones_bf", [1, P], BF16)
    ones_f = sbt("ones_f", [1, P], F32)
    onesc_bf = sbt("onesc_bf", [P, 1], BF16)
    ones_pp = sbt("ones_pp", [P, P], BF16)
    onesc_f = sbt("onesc_f", [P, 1], F32)
    bo_row = sbt("bo_row", [1, D], F32)
    bo_bc = sbt("bo_bc", [P, D], F32)
    s2b_row = sbt("s2b_row", [1, D], BF16)
    s1b_s = sbt("s1b_s", [P, NHT], F32)
    stab0 = sbt("stab0", [P, NT], U32)
    stab1 = sbt("stab1", [P, NT], U32)
    w0f = sbt("w0f", [P, NT], F32)
    w1f = sbt("w1f", [P, NT], F32)
    rl_t = [sbt("rl_t0", [P, D], BF16), sbt("rl_t1", [P, D], BF16)]
    g0_t = [sbt("g0_t0", [P, D], BF16), sbt("g0_t1", [P, D], BF16)]
    g1_t = [sbt("g1_t0", [P, D], BF16), sbt("g1_t1", [P, D], BF16)]

    arena = sbt("arena", [P, ARENA], mybir.dt.uint8)

    def carve(off_bytes, shape, dtype):
        nbytes = int(np.prod(shape[1:])) * mybir.dt.size(dtype)
        assert off_bytes + nbytes <= ARENA, (off_bytes, nbytes)
        v = arena[:, off_bytes:off_bytes + nbytes].bitcast(dtype)
        if shape[0] != P:
            v = v[0:shape[0]]
        if len(shape) > 2:
            pat = " ".join(f"d{i}" for i in range(1, len(shape)))
            v = v.rearrange(f"p ({pat}) -> p {pat}",
                            **{f"d{i}": shape[i] for i in range(1, len(shape))})
        return v

    def region(base, specs):
        out = {}
        off = base
        for name, shape, dtype in specs:
            out[name] = carve(off, shape, dtype)
            off += int(np.prod(shape[1:])) * mybir.dt.size(dtype)
        return out, off

    pbig4 = nc.alloc_psum_tensor("pbig4", [P, 2048], F32)
    pbig = nc.alloc_psum_tensor("pbig", [P, 1536], F32)
    pb_ = [pbig4[:, i * 512:(i + 1) * 512] for i in range(4)]
    pb_.extend([pbig[:, 0:512], pbig[:, 512:1024], pbig[:, 1024:1536]])
    pbt = nc.alloc_psum_tensor("pbt", [P, P], BF16)


    pl.op("sync", lambda e: e.dma_start(out=ident_s[:, :], in_=ident_c[:, :]), inc="ldq", inc_k=16)
    pl.op("sync", lambda e: e.dma_start(out=p128_s[:, :], in_=p128_c[:, :]), inc="ldq", inc_k=16)
    pl.op("sync", lambda e: e.dma_start(out=lstrict_s[:, :], in_=lstrict_c[:, :]), inc="ldq", inc_k=16)
    pl.op("sync", lambda e: e.dma_start(out=bo_row[:, :], in_=bo_d[:, :]), inc="ldq", inc_k=16)
    pl.op("sync", lambda e: e.dma_start(out=s1b_s[:, :], in_=s1b_d[:, :]), inc="ldq", inc_k=16)
    pl.op("gpsimd", lambda e: e.dma_start(out=s2b_row[:, :], in_=s2b_d[:, :]), inc="gq", inc_k=16)
    pl.ch("vector", lambda e: e.memset(ones_bf[:, :], 1.0))
    pl.ch("vector", lambda e: e.memset(ones_f[:, :], 1.0))
    pl.ch("vector", lambda e: e.memset(onesc_bf[:, :], 1.0))
    pl.ch("vector", lambda e: e.memset(ones_pp[:, :], 1.0))
    pl.ch("vector", lambda e: e.memset(onesc_f[:, :], 1.0))
    ld0 = C["ldq"]
    gq0 = C["gq"]
    vch_c = C["ch_vector"]
    v_bo = 0
    for h in range(2):
        pe_bo = pl.op("tensor", lambda e, h=h: e.matmul(
            pb_[6][:, :], ones_f[:, :], bo_row[:, h * 512:(h + 1) * 512],
            start=True, stop=True),
            waits=W(("ldq", ld0), ("ch_vector", vch_c), ("ch_vector", v_bo)), inc="pe")
        v_bo = pl.ch("vector", lambda e, h=h: e.tensor_copy(
            bo_bc[:, h * 512:(h + 1) * 512], pb_[6][:, :]), waits=W(("pe", pe_bo)))
    vch_bo = C["ch_vector"]

    P0c, p0c_end = region(OFF_C, [
        ("xt0", [P, D], F32),
        ("xt1", [P, D], F32),
        ("sqs", [P, D], F32),
        ("lnt", [P, 16], F32),
        ("xn0", [P, D], BF16),
        ("xn1", [P, D], BF16),
    ])
    xnt_bufs = [P0c["xn0"], P0c["xn1"]]
    xts = [P0c["xt0"], P0c["xt1"]]
    lnt = P0c["lnt"]
    xnT = carve(OFF_B, [P, NDC, S], BF16)

    xpar = ["xa", "xb"]
    vdone = [0, 0]
    sdone = [0, 0]
    xn_spill = [0, 0]
    tr_pe = [0, 0]
    vtr = 0
    for j in range(NT):
        par = j % 2
        pl.op("sync", lambda e, j=j, par=par: e.dma_start(
            out=xts[par][:, :], in_=x_d[j * P:(j + 1) * P, :]),
            waits=W(("ch_vector", vdone[par]), ("ch_scalar", sdone[par])),
            inc=xpar[par], inc_k=16)
        ldv = C[xpar[par]]
        ssq = pl.ch("scalar", lambda e, par=par: e.activation(
            P0c["sqs"][:, :], xts[par][:, :], AF.Square, scale=1.0 / 32.0,
            accum_out=lnt[:, 8:9]), waits=W((xpar[par], ldv)))
        sdone[par] = ssq
        pl.ch("vector", lambda e, par=par: e.reduce_sum(
            lnt[:, 0:1], xts[par][:, :], AX.X), waits=W((xpar[par], ldv)))
        pl.ch("vector", lambda e: e.tensor_scalar_mul(lnt[:, 1:2], lnt[:, 0:1], 1.0 / D))
        pl.ch("vector", lambda e: e.tensor_mul(lnt[:, 2:3], lnt[:, 1:2], lnt[:, 1:2]))
        pl.ch("vector", lambda e: e.tensor_sub(lnt[:, 3:4], lnt[:, 8:9], lnt[:, 2:3]),
              waits=W(("ch_scalar", ssq)))
        pl.ch("vector", lambda e: e.tensor_scalar_add(lnt[:, 4:5], lnt[:, 3:4], 1e-5))
        sq = pl.ch("scalar", lambda e: e.sqrt(lnt[:, 5:6], lnt[:, 4:5]),
                   waits=W(("ch_vector", C["ch_vector"])))
        pl.ch("vector", lambda e: e.reciprocal(lnt[:, 6:7], lnt[:, 5:6]),
              waits=W(("ch_scalar", sq)))
        vn = pl.ch("vector", lambda e, j=j, par=par: e.tensor_scalar(
            xnt_bufs[par][:, :], xts[par][:, :], lnt[:, 1:2], lnt[:, 6:7],
            ALU.subtract, ALU.mult),
            waits=W(("spillC", xn_spill[par]), ("pe", tr_pe[par])))
        vdone[par] = vn
        pl.op("sync", lambda e, j=j, par=par: e.dma_start(
            out=xn_d[j], in_=xnt_bufs[par][:, :]),
            waits=W(("ch_vector", vn)), inc="spillC", inc_k=16)
        xn_spill[par] = C["spillC"]
        for dc in range(NDC):
            pt = pl.op("tensor", lambda e, j=j, dc=dc, par=par: e.transpose(
                pbt[:, :], xnt_bufs[par][:, dc * P:(dc + 1) * P], ident_s[:, :]),
                waits=W(("ch_vector", vn), ("ldq", ld0), ("ch_vector", vtr)), inc="pe")
            vtr = pl.ch("vector", lambda e, j=j, dc=dc: e.tensor_copy(
                xnT[:, dc, j * P:(j + 1) * P], pbt[:, :]), waits=W(("pe", pt)))
        tr_pe[par] = C["pe"]
    vch_p0 = C["ch_vector"]
    spill_xn = C["spillC"]
    for dc in range(NDC):
        pl.op("sync", lambda e, dc=dc: e.dma_start(out=xnT_d[dc], in_=xnT[:, dc, :]),
              waits=W(("ch_vector", vch_p0)), inc="spillA", inc_k=16)
    spill_xnT = C["spillA"]

    s1w_s = carve(OFF_A, [P, NDC, SHID], BF16)
    h1c = [g0_t[0][:, 0:SB3], g0_t[1][:, 0:SB3]]
    for dc in range(NDC):
        pl.op("gpsimd", lambda e, dc=dc: e.dma_start(
            out=s1w_s[:, dc, :], in_=s1w_d[dc * P:(dc + 1) * P, :]),
            inc="moew", inc_k=16)
    s1w_ld = C["moew"]
    p3a_st = {"i": 0, "sgels": [0, 0, 0, 0], "spills": [0, 0]}

    def p3a_fill(n):
        for _ in range(n):
            i = p3a_st["i"]
            if i >= NSB3 * NHT:
                return
            p3a_st["i"] = i + 1
            sbi, ht = divmod(i, NHT)
            bank = pb_[i % 4]
            w4 = p3a_st["sgels"][i % 4]
            for dc in range(NDC):
                pl.op("tensor", lambda e, sbi=sbi, ht=ht, dc=dc, bank=bank: e.matmul(
                    bank[:, 0:SB3], s1w_s[:, dc, ht * P:(ht + 1) * P],
                    xnT[:, dc, sbi * SB3:(sbi + 1) * SB3],
                    start=(dc == 0), stop=(dc == NDC - 1)),
                    waits=W(("moew", s1w_ld), ("ch_vector", vch_p0),
                            ("ch_scalar", w4 if dc == 0 else 0)),
                    inc="pe" if dc == NDC - 1 else None)
            pe_h = C["pe"]
            cb = i % 2
            sgel = pl.ch("scalar", lambda e, cb=cb, ht=ht, bank=bank: e.activation(
                h1c[cb][:, :], bank[:, 0:SB3], AF.Gelu, bias=s1b_s[:, ht:ht + 1]),
                waits=W(("pe", pe_h), ("ldq", ld0),
                        ("spillH", p3a_st["spills"][cb])))
            p3a_st["sgels"][i % 4] = sgel
            pl.op("sync", lambda e, sbi=sbi, ht=ht, cb=cb: e.dma_start(
                out=h1_d[sbi, ht], in_=h1c[cb][:, :]),
                waits=W(("ch_scalar", sgel)), inc="spillH", inc_k=16)
            p3a_st["spills"][cb] = C["spillH"]

    rt_b6v = [0]
    rt_b5v = [0]
    env = dict(locals())

    if STAGE >= 3:
        build_routing(pl, nc, env)
    p3a_fill(NSB3 * NHT)
    env["p3a_pe"] = C["pe"]
    env["p3a_sc"] = C["ch_scalar"]
    env["spill_h1"] = C["spillH"]
    env["v_routing"] = C["ch_vector"]

    if STAGE >= 2:
        build_attention(pl, nc, env)
    else:
        vza = pl.ch("vector", lambda e: e.memset(P0c["sqs"][:, :], 0.0),
                    waits=W(("ch_scalar", C["ch_scalar"])))
        ztile = P0c["sqs"].bitcast(BF16)
        for j in range(NT):
            pl.op("sync", lambda e, j=j: e.dma_start(out=attn_d[j], in_=ztile[:, 0:D]),
                  waits=W(("ch_vector", vza)), inc="spillB", inc_k=16)
    if env.get("rt_gen") is not None:
        for _ in env["rt_gen"]:
            pass
    env["spill_attn"] = C["spillB"]
    env["gq_scat"] = C["gq"]

    if STAGE >= 3:
        build_moe(pl, nc, env)
    env["p2_pe"] = C["pe"]
    env["p2_v"] = C["ch_vector"]
    env["p2_s"] = C["ch_scalar"]
    env["p2_gq"] = C["gq"]

    build_p3(pl, nc, env)

    pl.finalize()
    return nc


def build_routing(pl, nc, g):
    C = pl.cnt
    carve = g["carve"]
    pb_ = g["pb_"]
    xnT = g["xnT"]; xn_d = g["xn_d"]; xe_d = g["xe_d"]
    pwt_d = g["pwt_d"]; pb_d = g["pb_d"]; eb_d = g["eb_d"]
    rt_d = g["rt_d"]; rtall_d = g["rtall_d"]; huge_c = g["huge_c"]
    ones_bf = g["ones_bf"]; ones_f = g["ones_f"]
    onesc_bf = g["onesc_bf"]; onesc_f = g["onesc_f"]
    lstrict_s = g["lstrict_s"]
    stab0 = g["stab0"]; stab1 = g["stab1"]; w0f = g["w0f"]; w1f = g["w1f"]
    vch_p0 = g["vch_p0"]; spill_xn = g["spill_xn"]
    fill = g.get("p3a_fill") or (lambda n: None)
    zer_c = g["zer_c"]

    R1 = {}
    off = OFF_C
    for name, shape, dtype in [
        ("pwt", [P, NDC, NE], BF16), ("prow", [1, NE], BF16),
        ("ebrow", [1, NE], F32), ("eb_bc", [P, NE], F32),
        ("z", [P, NT, NE], F32), ("i8f", [P, NE], F32),
        ("w1", [P, NT], F32), ("w2", [P, NT], F32),
        ("e1", [P, NT], F32), ("e2", [P, NT], F32),
        ("wall", [P, 512], F32), ("eall", [P, 512], F32),
        ("mask8", [P, NE, 512], BF16), ("gtmp", [P, 512], F32),
        ("cnts", [P, NE], F32), ("rows", [1, 64], F32),
        ("hibc", [P, NE], F32),
        ("m0", [P, NT], F32), ("m1", [P, NT], F32),
        ("pos0", [P, NT], F32), ("pos1", [P, NT], F32),
        ("acc0a", [P, NT], F32), ("acc0b", [P, NT], F32),
        ("acc1a", [P, NT], F32), ("acc1b", [P, NT], F32),
        ("k0", [P, NT], F32), ("k1", [P, NT], F32),
        ("sc1", [P, NT], F32), ("sc2", [P, NT], F32),
        ("rsum", [P, 4], F32), ("rsbf", [P, 4], BF16),
        ("hugef", [P, NT], F32), ("zrow", [P, NT], F32),
        ("gtu", [1, NE], U32), ("m2u", [P, NT], U32),
        ("v8", [P, NE], F32), ("i8", [P, NE], U32),
        ("zs", [P, NT, NE], F32),
        ("v12", [P, NT, 2], F32), ("e12", [P, NT, 2], U32),
        ("ebr", [P, NT * NE], F32),
    ]:
        R1[name] = carve(off, shape, dtype)
        off += int(np.prod(shape[1:])) * mybir.dt.size(dtype)
    assert off <= ARENA, off

    rel1 = [("ch_vector", vch_p0), ("ch_scalar", C["ch_scalar"])]
    for dc in range(NDC):
        pl.op("gpsimd", lambda e, dc=dc: e.dma_start(
            out=R1["pwt"][:, dc, :], in_=pwt_d[dc * P:(dc + 1) * P, :]),
            waits=rel1, inc="gq", inc_k=16)
    pl.op("gpsimd", lambda e: e.dma_start(out=R1["prow"][:, :], in_=pb_d[:, :]),
          inc="gq", inc_k=16)
    pl.op("sync", lambda e: e.dma_start(out=R1["ebrow"][:, :], in_=eb_d[:, :]),
          waits=rel1, inc="ldq", inc_k=16)
    pl.op("sync", lambda e: e.dma_start(out=R1["hugef"][:, :], in_=huge_c[:, :]),
          waits=rel1, inc="ldq", inc_k=16)
    gq_rt = C["gq"]; ld_rt = C["ldq"]

    peb = pl.op("tensor", lambda e: e.matmul(
        pb_[6][:, 0:NE], ones_f[:, :], R1["ebrow"][:, :], start=True, stop=True),
        waits=W(("ldq", ld_rt), ("ch_vector", g["vch_bo"])), inc="pe")
    v_eb = pl.ch("vector", lambda e: e.tensor_copy(R1["eb_bc"][:, :], pb_[6][:, 0:NE]),
                 waits=W(("pe", peb)))
    for j in range(NT):
        pl.ch("vector", lambda e, j=j: e.tensor_copy(
            R1["ebr"][:, j * NE:(j + 1) * NE], R1["eb_bc"][:, :]))
    v_eb = C["ch_vector"]
    pl.ch("vector", lambda e: e.memset(R1["zrow"][:, :], 0.0))

    for j in range(NT):
        for dc in range(NDC):
            pl.op("tensor", lambda e, j=j, dc=dc: e.matmul(
                pb_[5][:, 0:NE], xnT[:, dc, j * P:(j + 1) * P], R1["pwt"][:, dc, :],
                start=(dc == 0), stop=False),
                waits=W(("gq", gq_rt),
                        ("ch_vector", C["ch_vector"] if dc == 0 else 0)),
                inc=None)
        pz = pl.op("tensor", lambda e: e.matmul(
            pb_[5][:, 0:NE], ones_bf[:, :], R1["prow"][:, :], start=False, stop=True),
            inc="pe")
        pl.ch("vector", lambda e, j=j: e.tensor_copy(R1["z"][:, j, :], pb_[5][:, 0:NE]),
              waits=W(("pe", pz)))
    vz = C["ch_vector"]
    fill(24)
    sgm = pl.ch("scalar", lambda e: e.activation(
        R1["zs"].rearrange("p a b -> p (a b)"), R1["z"].rearrange("p a b -> p (a b)"),
        AF.Sigmoid), waits=W(("ch_vector", vz)))
    vadd = pl.ch("vector", lambda e: e.tensor_add(
        R1["zs"].rearrange("p a b -> p (a b)"), R1["zs"].rearrange("p a b -> p (a b)"),
        R1["ebr"][:, :]), waits=W(("ch_scalar", sgm), ("ch_vector", v_eb)))
    pl.ch("vector", lambda e: e.tensor_scalar_max(
        R1["zs"].rearrange("p a b -> p (a b)"),
        R1["zs"].rearrange("p a b -> p (a b)"), 1e-12))
    for j in range(NT):
        pl.ch("vector", lambda e, j=j: e.max(R1["v8"][:, :], R1["zs"][:, j, :]))
        pl.ch("vector", lambda e, j=j: e.max_index(
            R1["i8"][:, :], R1["v8"][:, :], R1["zs"][:, j, :]))
        pl.ch("vector", lambda e, j=j: e.tensor_copy(
            R1["v12"][:, j, :], R1["v8"][:, 0:2]))
        pl.ch("vector", lambda e, j=j: e.tensor_copy(
            R1["e12"][:, j, :], R1["i8"][:, 0:2]))
    pl.ch("vector", lambda e: e.tensor_add(
        R1["w2"][:, :], R1["v12"][:, :, 0], R1["v12"][:, :, 1]))
    pl.ch("vector", lambda e: e.reciprocal(R1["w1"][:, :], R1["w2"][:, :]))
    pl.ch("vector", lambda e: e.tensor_mul(
        R1["w2"][:, :], R1["v12"][:, :, 1], R1["w1"][:, :]))
    pl.ch("vector", lambda e: e.tensor_mul(
        R1["w1"][:, :], R1["v12"][:, :, 0], R1["w1"][:, :]))
    pl.ch("vector", lambda e: e.tensor_copy(R1["e1"][:, :], R1["e12"][:, :, 0]))
    pl.ch("vector", lambda e: e.tensor_copy(R1["e2"][:, :], R1["e12"][:, :, 1]))
    v_top2 = C["ch_vector"]

    for fi, nm in enumerate(("w1", "w2", "e1", "e2")):
        pl.op("gpsimd", lambda e, fi=fi, nm=nm: e.dma_start(
            out=rt_d[fi].rearrange("(p j) -> p j", p=P), in_=R1[nm][:, :]),
            waits=W(("ch_vector", v_top2)), inc="gq", inc_k=16)
    gq_rtw = C["gq"]
    pl.op("gpsimd", lambda e: e.collective_compute(
        "AllGather", ALU.bypass, replica_groups=[list(range(NCORES))],
        ins=[rt_d[:, :]], outs=[rtall_d[:, :]]),
        waits=W(("gq", gq_rtw)), inc="cc", inc_k=1)
    fill(16)
    for cc_ in range(NCORES):
        for sl in range(2):
            col = sl * 256 + cc_ * 32
            pl.op("gpsimd", lambda e, cc_=cc_, sl=sl, col=col: e.dma_start(
                out=R1["wall"][:, col:col + 32],
                in_=rtall_d[4 * cc_ + sl].rearrange("(p j) -> p j", p=P)),
                waits=W(("cc", 1)), inc="ldq", inc_k=16)
            pl.op("gpsimd", lambda e, cc_=cc_, sl=sl, col=col: e.dma_start(
                out=R1["eall"][:, col:col + 32],
                in_=rtall_d[4 * cc_ + 2 + sl].rearrange("(p j) -> p j", p=P)),
                waits=W(("cc", 1)), inc="ldq", inc_k=16)
    ld_we = C["ldq"]

    for ex in range(NE):
        pl.ch("vector", lambda e, ex=ex: e.tensor_single_scalar(
            R1["mask8"][:, ex, :], R1["eall"][:, :], float(ex), ALU.is_equal),
            waits=W(("ldq", ld_we), ("spillC", spill_xn)))
    v_masks = C["ch_vector"]

    R = R1["rows"]
    lo_s, hi_s, lo_a, hi_a = 0, 8, 40, 48
    pl.ch("vector", lambda e: e.memset(R[:, lo_s:lo_s + 8], 0.0))
    pl.ch("vector", lambda e: e.memset(R[:, hi_s:hi_s + 8], 1.0))

    for b in range(NSLOT // P):
        pl.op("gpsimd", lambda e, b=b: e.dma_start(
            out=xe_d[b * P:(b + 1) * P, :], in_=zer_c[:, :]),
            inc="gq", inc_k=16)
    ld_pref = C["gq"]

    g6 = g["rt_b6v"]
    g5 = g["rt_b5v"]

    def routing_tail():
        cur = 0
        for it in range(BISECT_ITERS):
            lo0 = lo_s if cur == 0 else lo_a
            hi0 = hi_s if cur == 0 else hi_a
            lo1 = lo_a if cur == 0 else lo_s
            hi1 = hi_a if cur == 0 else hi_s
            pl.ch("vector", lambda e, lo0=lo0, hi0=hi0: e.tensor_add(
                R[:, 16:24], R[:, lo0:lo0 + 8], R[:, hi0:hi0 + 8]))
            pl.ch("vector", lambda e: e.tensor_scalar_mul(R[:, 16:24], R[:, 16:24], 0.5))
            pm = pl.op("tensor", lambda e: e.matmul(
                pb_[6][:, 0:NE], ones_f[:, :], R[:, 16:24], start=True, stop=True),
                waits=W(("ch_vector", C["ch_vector"])), inc="pe")
            g6[0] = pl.ch("vector", lambda e: e.tensor_copy(
                R1["hibc"][:, :], pb_[6][:, 0:NE]), waits=W(("pe", pm)))
            yield
            for ex in range(NE):
                pl.ch("vector", lambda e, ex=ex: e.tensor_scalar(
                    R1["gtmp"][:, :], R1["wall"][:, :], R1["hibc"][:, ex:ex + 1],
                    None, ALU.is_ge), waits=W(("ch_vector", v_masks)))
                pl.ch("vector", lambda e, ex=ex: e.tensor_mul(
                    R1["gtmp"][:, :], R1["gtmp"][:, :], R1["mask8"][:, ex, :]))
                pl.ch("vector", lambda e, ex=ex: e.reduce_sum(
                    R1["cnts"][:, ex:ex + 1], R1["gtmp"][:, :], AX.X))
            pcm = pl.op("tensor", lambda e: e.matmul(
                pb_[5][0:1, 0:NE], onesc_f[:, :], R1["cnts"][:, :],
                start=True, stop=True),
                waits=W(("ch_vector", C["ch_vector"]),
                        ("ch_scalar", C["ch_scalar"])), inc="pe")
            g5[0] = pl.ch("vector", lambda e: e.tensor_copy(
                R[:, 24:32], pb_[5][0:1, 0:NE]), waits=W(("pe", pcm)))
            pl.ch("vector", lambda e: e.tensor_single_scalar(
                R1["gtu"][:, :], R[:, 24:32], CAP, ALU.is_gt))
            pl.ch("vector", lambda e, lo0=lo0, lo1=lo1: e.select(
                R[:, lo1:lo1 + 8], R1["gtu"][:, :], R[:, 16:24], R[:, lo0:lo0 + 8],
                add_drain=True))
            pl.ch("vector", lambda e, hi0=hi0, hi1=hi1: e.select(
                R[:, hi1:hi1 + 8], R1["gtu"][:, :], R[:, hi0:hi0 + 8], R[:, 16:24],
                add_drain=True))
            cur = 1 - cur
            yield
        hi_fin = hi_s if cur == 0 else hi_a
        ph = pl.op("tensor", lambda e, hi_fin=hi_fin: e.matmul(
            pb_[6][:, 0:NE], ones_f[:, :], R[:, hi_fin:hi_fin + 8],
            start=True, stop=True),
            waits=W(("ch_vector", C["ch_vector"])), inc="pe")
        v_hibc = pl.ch("vector", lambda e: e.tensor_copy(
            R1["hibc"][:, :], pb_[6][:, 0:NE]), waits=W(("pe", ph)))
        g6[0] = v_hibc
        pl.ch("vector", lambda e: e.tensor_copy(R1["acc0a"][:, :], R1["hugef"][:, :]),
              waits=W(("ldq", ld_rt)))
        pl.ch("vector", lambda e: e.tensor_copy(R1["acc1a"][:, :], R1["hugef"][:, :]))
        pl.ch("vector", lambda e: e.memset(R1["k0"][:, :], 0.0))
        pl.ch("vector", lambda e: e.memset(R1["k1"][:, :], 0.0))
        yield
        acc = [[R1["acc0a"], R1["acc0b"]], [R1["acc1a"], R1["acc1b"]]]
        accur = [0, 0]
        for ex in range(NE):
            for sl in range(2):
                wsl = R1["w1"] if sl == 0 else R1["w2"]
                esl = R1["e1"] if sl == 0 else R1["e2"]
                msl = R1["m0"] if sl == 0 else R1["m1"]
                pl.ch("vector", lambda e, esl=esl, ex=ex: e.tensor_single_scalar(
                    R1["sc1"][:, :], esl[:, :], float(ex), ALU.is_equal),
                    waits=W(("ch_vector", v_hibc)))
                pl.ch("vector", lambda e, wsl=wsl, ex=ex: e.tensor_scalar(
                    R1["sc2"][:, :], wsl[:, :], R1["hibc"][:, ex:ex + 1], None,
                    ALU.is_ge))
                pl.ch("vector", lambda e, msl=msl: e.tensor_mul(
                    msl[:, :], R1["sc1"][:, :], R1["sc2"][:, :]))
            pl.ch("vector", lambda e: e.reduce_sum(
                R1["rsum"][:, 0:1], R1["m0"][:, :], AX.X))
            pl.ch("vector", lambda e: e.reduce_sum(
                R1["rsum"][:, 1:2], R1["m1"][:, :], AX.X))
            vrs = pl.ch("vector", lambda e: e.tensor_copy(
                R1["rsbf"][:, 0:2], R1["rsum"][:, 0:2]))
            pt0 = pl.op("tensor", lambda e: e.matmul(
                pb_[6][:, 0:2], lstrict_s[:, :], R1["rsbf"][:, 0:2],
                start=True, stop=True),
                waits=W(("ch_vector", C["ch_vector"])), inc="pe")
            pl.ch("vector", lambda e: e.tensor_copy(
                R1["rsum"][:, 2:4], pb_[6][:, 0:2]), waits=W(("pe", pt0)))
            pt1 = pl.op("tensor", lambda e: e.matmul(
                pb_[6][0:1, 4:5], R1["rsbf"][:, 0:1], onesc_bf[:, :],
                start=True, stop=True),
                waits=W(("ch_vector", vrs)), inc="pe")
            vt0 = pl.ch("vector", lambda e: e.tensor_copy(
                R1["rsbf"][0:1, 2:3], pb_[6][0:1, 4:5]), waits=W(("pe", pt1)))
            pt2 = pl.op("tensor", lambda e: e.matmul(
                pb_[6][:, 6:7], ones_bf[:, :], R1["rsbf"][0:1, 2:3],
                start=True, stop=True),
                waits=W(("ch_vector", vt0)), inc="pe")
            g6[0] = pl.ch("vector", lambda e: e.tensor_copy(
                R1["rsum"][:, 0:1], pb_[6][:, 6:7]), waits=W(("pe", pt2)))
            for sl in range(2):
                msl = R1["m0"] if sl == 0 else R1["m1"]
                pos = R1["pos0"] if sl == 0 else R1["pos1"]
                pl.ch("vector", lambda e, msl=msl: e.tensor_tensor_scan(
                    R1["sc1"][:, :], R1["zrow"][:, :], msl[:, :], 0.0,
                    ALU.add, ALU.add))
                pl.ch("vector", lambda e, msl=msl, pos=pos: e.tensor_sub(
                    pos[:, :], R1["sc1"][:, :], msl[:, :]))
                pl.ch("vector", lambda e, pos=pos, sl=sl: e.tensor_scalar_add(
                    pos[:, :], pos[:, :], R1["rsum"][:, 2 + sl:3 + sl]))
            pl.ch("vector", lambda e: e.tensor_scalar_add(
                R1["pos1"][:, :], R1["pos1"][:, :], R1["rsum"][:, 0:1]))
            for sl in range(2):
                msl = R1["m0"] if sl == 0 else R1["m1"]
                pos = R1["pos0"] if sl == 0 else R1["pos1"]
                ksl = R1["k0"] if sl == 0 else R1["k1"]
                a = acc[sl]
                cu = accur[sl]
                pl.ch("vector", lambda e, pos=pos: e.tensor_single_scalar(
                    R1["sc1"][:, :], pos[:, :], float(CLOC), ALU.is_lt))
                pl.ch("vector", lambda e, msl=msl: e.tensor_mul(
                    R1["sc1"][:, :], R1["sc1"][:, :], msl[:, :]))
                pl.ch("vector", lambda e, pos=pos, ex=ex: e.tensor_scalar_add(
                    R1["sc2"][:, :], pos[:, :], float(ex * CLOC)))
                pl.ch("vector", lambda e: e.tensor_copy(R1["m2u"][:, :], R1["sc1"][:, :]))
                pl.ch("vector", lambda e, a=a, cu=cu: e.select(
                    a[1 - cu][:, :], R1["m2u"][:, :], R1["sc2"][:, :], a[cu][:, :],
                    add_drain=True))
                pl.ch("vector", lambda e, ksl=ksl: e.tensor_add(
                    ksl[:, :], ksl[:, :], R1["sc1"][:, :]))
                accur[sl] = 1 - cu
            yield
        pl.ch("vector", lambda e: e.tensor_copy(stab0[:, :], acc[0][accur[0]][:, :]))
        pl.ch("vector", lambda e: e.tensor_copy(stab1[:, :], acc[1][accur[1]][:, :]))
        pl.ch("vector", lambda e: e.tensor_mul(w0f[:, :], R1["w1"][:, :], R1["k0"][:, :]))
        pl.ch("vector", lambda e: e.tensor_mul(w1f[:, :], R1["w2"][:, :], R1["k1"][:, :]))
        v_stab = C["ch_vector"]
        rls = g["rl_t"]
        rl_used = [0, 0]
        for j in range(NT):
            par = j % 2
            rsem = "wa" if par == 0 else "wb"
            pl.op("gpsimd", lambda e, j=j, par=par: e.dma_start(
                out=rls[par][:, :], in_=xn_d[j]),
                waits=W(("spillC", spill_xn), ("gq", rl_used[par])),
                inc=rsem, inc_k=16)
            ld_rl = C[rsem]
            for sl in range(2):
                st = stab0 if sl == 0 else stab1
                pl.op("gpsimd", lambda e, j=j, st=st, par=par: e.indirect_dma_start(
                    xe_d[:, :], IndirectOffsetOnAxis(st[:, j:j + 1], 0),
                    rls[par][:, :], None,
                    bounds_check=pl.breg(e), oob_is_err=False),
                    waits=W(("ch_vector", v_stab), ("gq", ld_pref), (rsem, ld_rl)),
                    inc="gq", inc_k=16)
            rl_used[par] = C["gq"]
        yield

    g["rt_gen"] = routing_tail()


def build_attention(pl, nc, g):
    C = pl.cnt
    carve = g["carve"]
    pb_ = g["pb_"]; pbig4 = g["pbig4"]; pbig = g["pbig"]
    dmask_c = g["dmask_c"]
    xnT = g["xnT"]
    wq_d = g["wq_d"]; wk_d = g["wk_d"]; wv_d = g["wv_d"]; wo_d = g["wo_d"]
    ctxT_d = g["ctxT_d"]; attn_d = g["attn_d"]
    cos_c = g["cos_c"]; sin_c = g["sin_c"]
    p128_s = g["p128_s"]; ones_f = g["ones_f"]; ones_bf = g["ones_bf"]
    ones_pp = g["ones_pp"]
    spill_xn = g["spill_xn"]; vch_p0 = g["vch_p0"]
    v_routing = g["v_routing"]

    A2 = {}
    offs = {}
    off = OFF_A
    for name, shape, dtype in [
        ("cosT", [P, S], BF16), ("sinT", [P, S], BF16),
        ("qraw0", [P, 512], BF16), ("qraw1", [P, 512], BF16),
        ("q2r", [P, S], BF16), ("k2r", [P, S], BF16),
        ("v2", [P, NT, 132], BF16),
        ("asb0", [P, 1024], BF16), ("asb1", [P, 1024], BF16),
        ("am0", [P, 1024], BF16), ("am1", [P, 1024], BF16),
        ("rt0", [P, 512], BF16), ("rt1", [P, 512], BF16),
        ("rcpb", [P, 1024], BF16),
        ("ctxe", [P, 1024], BF16),
        ("craw", [P, 1024], BF16),
        ("sums", [P, 1024], F32),
    ]:
        A2[name] = carve(off, shape, dtype)
        offs[name] = off
        off += int(np.prod(shape[1:])) * mybir.dt.size(dtype)
    assert off <= OFF_B, off
    amsf = carve(offs["am0"], [P, 1024], F32)
    offc = OFF_C + 18 * 1024
    for name, shape, dtype in [
        ("dmask", [P, 4, 1024], BF16),
        ("wqs", [P, NDC, P], BF16), ("wks", [P, NDC, P], BF16),
        ("wvs", [P, NDC, P], BF16),
    ]:
        A2[name] = carve(offc, shape, dtype)
        offc += int(np.prod(shape[1:])) * mybir.dt.size(dtype)
    assert offc <= ARENA, offc

    relC = [("spillC", spill_xn), ("ch_vector", vch_p0),
            ("ch_vector", v_routing)]
    relA = relC + [("pe", g.get("p3a_pe", 0))]
    rt_gen = g.get("rt_gen")
    g5 = g["rt_b5v"]
    g6 = g["rt_b6v"]
    pl.op("sync", lambda e: e.dma_start(out=A2["cosT"][:, :], in_=cos_c[:, :]),
          waits=relA, inc="ldq", inc_k=16)
    pl.op("sync", lambda e: e.dma_start(out=A2["sinT"][:, :], in_=sin_c[:, :]),
          waits=relA, inc="ldq", inc_k=16)
    pl.op("sync", lambda e: e.dma_start(
        out=A2["dmask"][:, :, :],
        in_=dmask_c[:, :].rearrange("(a p) b -> p a b", p=P)),
        waits=relC, inc="ldq", inc_k=16)
    ld_cs = C["ldq"]
    vone = pl.ch("vector", lambda e: e.memset(A2["v2"][:, :, 64:66], 1.0), waits=relA)
    vone = pl.ch("vector", lambda e: e.memset(A2["v2"][:, :, 130:132], 1.0))
    v_v2ones = C["ch_vector"]

    qraws = [A2["qraw0"], A2["qraw1"]]
    kraws = qraws
    asb = [A2["asb0"], A2["asb1"]]
    ams = [A2["am0"], A2["am1"]]
    rts = [A2["rt0"], A2["rt1"]]
    NBLK = S // 512

    gen_pe_prev = 0
    norm_v_prev = 0
    exp_cons = [g.get("p3a_sc", 0), g.get("p3a_sc", 0)]
    ctx_read = [0, 0]
    pool_read = [0, 0]
    st_ctx = 0
    pending_norm = None
    carry_tails = []
    carry_norm = None
    bank45_free = [0]
    craw_rd_v = [0]
    sums_rd_v = [0]
    norm_pmm_pe = [0]

    for hp in range(NPAIR):
        for dc in range(NDC):
            pl.op("gpsimd", lambda e, hp=hp, dc=dc: e.dma_start(
                out=A2["wqs"][:, dc, :],
                in_=wq_d[dc * P:(dc + 1) * P, hp * P:(hp + 1) * P]),
                waits=relC + [("pe", gen_pe_prev)], inc="att", inc_k=16)
            pl.op("gpsimd", lambda e, hp=hp, dc=dc: e.dma_start(
                out=A2["wks"][:, dc, :],
                in_=wk_d[dc * P:(dc + 1) * P, hp * P:(hp + 1) * P]),
                inc="att", inc_k=16)
            pl.op("gpsimd", lambda e, hp=hp, dc=dc: e.dma_start(
                out=A2["wvs"][:, dc, :],
                in_=wv_d[dc * P:(dc + 1) * P, hp * P:(hp + 1) * P]),
                inc="att", inc_k=16)
        ld_w = C["att"]

        vev_prev = norm_v_prev
        sin_prev = norm_v_prev
        for which, wsl, raws, dst in (
                (0, A2["wqs"], qraws, A2["q2r"]), (1, A2["wks"], kraws, A2["k2r"])):
            for blk in range(NBLK):
                par = blk % 2
                for dc in range(NDC):
                    pl.op("tensor", lambda e, wsl=wsl, dc=dc, blk=blk: e.matmul(
                        pb_[4][:, :], wsl[:, dc, :],
                        xnT[:, dc, blk * 512:(blk + 1) * 512],
                        start=(dc == 0), stop=(dc == NDC - 1)),
                        waits=W(("att", ld_w),
                                ("ch_vector", vev_prev if dc == 0 else 0),
                                ("ch_scalar", bank45_free[0] if dc == 0 else 0)),
                        inc="pe" if dc == NDC - 1 else None)
                pe_q = C["pe"]
                vev = pl.ch("vector", lambda e, raws=raws, par=par: e.tensor_copy(
                    raws[par][:, :], pb_[4][:, :]), waits=W(("pe", pe_q)))
                vev_prev = vev
                pp = pl.op("tensor", lambda e, raws=raws, par=par: e.matmul(
                    pb_[5][:, :], p128_s[:, :], raws[par][:, :],
                    start=True, stop=True),
                    waits=W(("ch_vector", vev), ("ldq", g["ld0"]),
                            ("ch_vector", sin_prev),
                            ("ch_scalar", bank45_free[0])), inc="pe")
                sin_prev = pl.ch("vector", lambda e, blk=blk, par=par: e.tensor_mul(
                    rts[par][:, :], A2["sinT"][:, blk * 512:(blk + 1) * 512],
                    pb_[5][:, :]), waits=W(("pe", pp), ("ldq", ld_cs)))
                pl.ch("vector", lambda e, blk=blk, par=par, raws=raws: e.tensor_mul(
                    rts[1 - par][:, :], A2["cosT"][:, blk * 512:(blk + 1) * 512],
                    raws[par][:, :]))
                pl.ch("vector", lambda e, blk=blk, par=par, dst=dst: e.tensor_add(
                    dst[:, blk * 512:(blk + 1) * 512], rts[par][:, :],
                    rts[1 - par][:, :]))
        for j in range(NT):
            for dc in range(NDC):
                pl.op("tensor", lambda e, j=j, dc=dc: e.matmul(
                    pb_[4][:, 0:P], xnT[:, dc, j * P:(j + 1) * P],
                    A2["wvs"][:, dc, :],
                    start=(dc == 0), stop=(dc == NDC - 1)),
                    waits=W(("att", ld_w),
                            ("ch_vector", C["ch_vector"] if dc == 0 else 0)),
                    inc="pe" if dc == NDC - 1 else None)
            pe_v = C["pe"]
            pl.ch("vector", lambda e, j=j: e.tensor_copy(
                A2["v2"][:, j, 0:64], pb_[4][:, 0:64]),
                waits=W(("pe", pe_v), ("ch_vector", v_v2ones)))
            pl.ch("vector", lambda e, j=j: e.tensor_copy(
                A2["v2"][:, j, 66:130], pb_[4][:, 64:128]))
        v_gen = C["ch_vector"]
        gen_pe_prev = C["pe"]

        for J in range(NBLK):
            csw = [bank45_free[0]]
            ntile = 4 * J + 4
            tinfo = {}

            def emit_ctx(t, ntile=ntile, tinfo=tinfo, csw=csw):
                diag, par, sc_exp, pool_c = tinfo[t]
                for head in range(2):
                    src = (ams[par] if diag else asb[par])[:, head * 512:(head + 1) * 512]
                    bank = (pb_[4], pb_[5])[head]
                    voff = 0 if head == 0 else 66
                    pl.op("tensor", lambda e, t=t, head=head, src=src, bank=bank,
                          voff=voff: e.matmul(
                        bank[0:65, :], A2["v2"][:, t, voff:voff + 65], src,
                        start=(t == 0), stop=(t == ntile - 1)),
                        waits=W(("ch_vector", pool_c) if diag else
                                ("ch_scalar", sc_exp),
                                ("ch_scalar", csw[0] if t == 0 else 0),
                                ("ch_vector", g5[0] if t == 0 else 0)),
                        inc="pe" if head == 1 else None)
                if diag:
                    pool_read[par] = C["pe"]
                ctx_read[par] = C["pe"]

            for t in range(ntile):
                par = t % 2
                for head in range(2):
                    poff = 64 * head
                    bank = pbig4[:, par * 1024 + head * 512: par * 1024 + (head + 1) * 512]
                    pl.op("tensor", lambda e, hp=hp, t=t, J=J, head=head, bank=bank,
                          poff=poff: e.matmul(
                        bank,
                        A2["k2r"][poff:poff + 64, t * P:(t + 1) * P],
                        A2["q2r"][poff:poff + 64, J * 512:(J + 1) * 512],
                        start=True, stop=True, tile_position=(poff, 0)),
                        waits=W(("ch_scalar", exp_cons[par]),
                                ("ch_vector", v_gen)),
                        inc="pe" if head == 1 else None)
                pm = C["pe"]
                if t == 2 and carry_norm is not None:
                    carry_norm()()
                    carry_norm = None
                if t >= 2:
                    emit_ctx(t - 2)
                exp_cons[par] = pl.ch(
                    "scalar", lambda e, par=par: e.activation(
                        asb[par][:, :], pbig4[:, par * 1024:(par + 1) * 1024],
                        AF.Exp, scale=0.125),
                    waits=W(("pe", pm), ("pe", ctx_read[par])))
                sc_exp = exp_cons[par]
                diag = t >= 4 * J
                if diag:
                    di = t - 4 * J
                    pl.ch("vector", lambda e, par=par, di=di: e.tensor_mul(
                        ams[par][:, :], asb[par][:, :], A2["dmask"][:, di, :]),
                        waits=W(("ch_scalar", sc_exp), ("pe", pool_read[par]),
                                ("ldq", ld_cs)))
                tinfo[t] = (diag, par, sc_exp, C["ch_vector"])
            emit_ctx(ntile - 2)
            emit_ctx(ntile - 1)
            pe_ctx = C["pe"]

            craw_rd = craw_rd_v[0]
            s1 = pl.ch("scalar", lambda e: e.copy(
                A2["craw"][0:64, 0:512], pb_[4][0:64, :]),
                waits=W(("pe", pe_ctx), ("ch_vector", craw_rd)))
            pl.ch("scalar", lambda e: e.copy(
                A2["craw"][0:64, 512:1024], pb_[5][0:64, :]))
            s_free = pl.ch("scalar", lambda e: e.copy(
                A2["sums"][64:65, 0:1024], pbig[64:65, 0:1024]),
                waits=W(("ch_vector", sums_rd_v[0])))
            bank45_free[0] = s_free
            pl.ch("vector", lambda e: e.reciprocal(
                amsf[64:65, 0:512], A2["sums"][64:65, 0:512]),
                waits=W(("ch_scalar", s_free)))
            pl.ch("vector", lambda e: e.reciprocal(
                amsf[64:65, 512:1024], A2["sums"][64:65, 512:1024]))
            rcp_cast = pl.ch("vector", lambda e: e.tensor_copy(
                A2["rcpb"][64:65, 0:1024], amsf[64:65, 0:1024]),
                waits=W(("pe", norm_pmm_pe[0])))
            sums_rd_v[0] = rcp_cast

            def make_norm(rcp_cast=rcp_cast, hp=hp, J=J, s1=s1):
                def norm(csw_next=None):
                    nonlocal st_ctx, norm_v_prev
                    st_prev = st_ctx
                    tt = 0
                    for head in range(2):
                        pmm = pl.op("tensor", lambda e, head=head: e.matmul(
                            pb_[6][:, :], ones_pp[64:65, :],
                            A2["rcpb"][64:65, head * 512:(head + 1) * 512],
                            start=True, stop=True),
                            waits=W(("ch_vector", max(rcp_cast, tt, g6[0]))),
                            inc="pe")
                        tt = pl.ch("vector", lambda e, head=head: e.tensor_mul(
                            A2["ctxe"][0:64, head * 512:(head + 1) * 512],
                            A2["craw"][0:64, head * 512:(head + 1) * 512],
                            pb_[6][0:64, :]),
                            waits=W(("pe", pmm), ("attb", st_prev)))
                    v_norm = C["ch_vector"]
                    craw_rd_v[0] = v_norm
                    norm_pmm_pe[0] = C["pe"]
                    st_ctx = pl.op("sync", lambda e, hp=hp, J=J: e.dma_start(
                        out=ctxT_d[hp, 0:64, J * 512:(J + 1) * 512],
                        in_=A2["ctxe"][0:64, 0:512]),
                        waits=W(("ch_vector", v_norm)), inc="attb", inc_k=16)
                    st_ctx = pl.op("sync", lambda e, hp=hp, J=J: e.dma_start(
                        out=ctxT_d[hp, 64:128, J * 512:(J + 1) * 512],
                        in_=A2["ctxe"][0:64, 512:1024]),
                        inc="attb", inc_k=16)
                    norm_v_prev = v_norm
                    if csw_next is not None:
                        pass
                return norm

            carry_norm = make_norm
            if rt_gen is not None:
                next(rt_gen, None)
                if J >= 6:
                    next(rt_gen, None)
        if carry_norm is not None:
            carry_norm()()
            carry_norm = None
    ctx_stored = st_ctx
    g["pe_attn_jend"] = C["pe"]
    g["v_attn_jend"] = C["ch_vector"]
    g["gp_attn_jend"] = C["ch_gpsimd"]
    if STAGE >= 3:
        xeT0 = carve(OFF_C, [P, NDC, CLOC], BF16)
        for dc in range(NDC):
            pl.op("sync", lambda e, dc=dc: e.dma_start_transpose(
                xeT0[:, dc, :], g["xe_d"][0:CLOC, dc * P:(dc + 1) * P]),
                waits=W(("gq", C["gq"]), ("ch_vector", g["v_attn_jend"]),
                        ("ch_gpsimd", g["gp_attn_jend"])),
                inc="att", inc_k=16)
        g["ld_xe0"] = C["att"]

    off = OFF_A + 16 * 1024
    O2 = {}
    for name, shape, dtype in [
        ("woS", [P, NDC, D], BF16),
        ("ctxJ0", [P, NPAIR, 512], BF16),
        ("ctxJ1", [P, NPAIR, 512], BF16),
        ("ae0", [P, D], BF16),
        ("ae1", [P, D], BF16),
    ]:
        O2[name] = carve(off, shape, dtype)
        off += int(np.prod(shape[1:])) * mybir.dt.size(dtype)
    assert off <= OFF_B

    relO = [("pe", C["pe"]), ("ch_vector", C["ch_vector"]),
            ("ch_scalar", C["ch_scalar"]), ("ch_gpsimd", C["ch_gpsimd"])]
    for dc in range(NDC):
        pl.op("gpsimd", lambda e, dc=dc: e.dma_start(
            out=O2["woS"][:, dc, :], in_=wo_d[dc * P:(dc + 1) * P, :]),
            waits=relO, inc="att", inc_k=16)
    ld_wo = C["att"]
    ctxJ = [O2["ctxJ0"], O2["ctxJ1"]]
    aes = [O2["ae0"], O2["ae1"]]
    ctxJ_pe = [0, 0]
    ae_st = [0, 0]
    for J in range(NBLK):
        par = J % 2
        cjsem = "xa" if par == 0 else "xb"
        for pr in range(NPAIR):
            pl.op("sync", lambda e, J=J, pr=pr, par=par: e.dma_start(
                out=ctxJ[par][:, pr, :], in_=ctxT_d[pr, :, J * 512:(J + 1) * 512]),
                waits=relO + [("attb", ctx_stored), ("pe", ctxJ_pe[par])],
                inc=cjsem, inc_k=16)
        ld_cj = C[cjsem]
        for st in range(4):
            j = J * 4 + st
            for half in range(2):
                bank = pb_[half]
                for pr in range(NPAIR):
                    pl.op("tensor", lambda e, st=st, pr=pr, half=half, par=par,
                          bank=bank: e.matmul(
                        bank[:, :], ctxJ[par][:, pr, st * P:(st + 1) * P],
                        O2["woS"][:, pr, half * 512:(half + 1) * 512],
                        start=(pr == 0), stop=(pr == NPAIR - 1)),
                        waits=W(("att", ld_wo), (cjsem, ld_cj),
                                ("ch_vector", C["ch_vector"] if pr == 0 else 0)),
                        inc="pe" if pr == NPAIR - 1 else None)
            pe_o = C["pe"]
            apar = j % 2
            for half in range(2):
                pl.ch("vector", lambda e, half=half, apar=apar: e.tensor_copy(
                    aes[apar][:, half * 512:(half + 1) * 512], pb_[half][:, :]),
                    waits=W(("pe", pe_o), ("spillB", ae_st[apar])))
            v_ae = C["ch_vector"]
            pl.op("sync", lambda e, j=j, apar=apar: e.dma_start(
                out=attn_d[j], in_=aes[apar][:, :]),
                waits=W(("ch_vector", v_ae)), inc="spillB", inc_k=16)
            ae_st[apar] = C["spillB"]
        ctxJ_pe[par] = C["pe"]


def build_moe(pl, nc, g):
    C = pl.cnt
    carve = g["carve"]
    pb_ = g["pb_"]; pbig = g["pbig"]
    xe_d = g["xe_d"]; oe_d = g["oe_d"]
    f1w_d = g["f1w_d"]; f1b_d = g["f1b_d"]; f2w_d = g["f2w_d"]; f2b_d = g["f2b_d"]
    ones_bf = g["ones_bf"]
    gq_scat = g["gq_scat"]

    f2ws = [carve(OFF_A, [P, EH // P, D], BF16),
            carve(OFF_A + 32 * 1024, [P, EH // P, D], BF16)]
    hTs = [carve(OFF_B, [P, EH // P, 512], BF16),
           carve(OFF_B + 16 * 1024, [P, EH // P, 512], BF16)]
    f1w = carve(OFF_B + 32 * 1024, [P, NDC, EH], BF16)
    M2 = {"xeT": carve(OFF_C, [P, NDC, CLOC], BF16)}
    sbt = nc.alloc_sbuf_tensor
    oes = [sbt("moe_oe0", [P, D], BF16), sbt("moe_oe1", [P, D], BF16)]
    f1b_s = sbt("moe_f1b", [P, EH // P], F32)
    f2b_s = [sbt("moe_f2b0", [1, D], BF16), sbt("moe_f2b1", [1, D], BF16)]

    relM = [("pe", C["pe"]), ("ch_vector", C["ch_vector"]),
            ("ch_scalar", C["ch_scalar"]), ("ch_gpsimd", C["ch_gpsimd"]),
            ("gq", gq_scat)]
    NTH = CLOC // 512
    fc1_done = {}
    gelu_done = {}
    fc2_done = {}
    f1b_used = 0
    oe_st = [0, 0]
    st_oe = 0
    fc1bank_sc = [0, 0]

    def fc1_third(ex, th):
        gth = ex * NTH + th
        hb = hTs[gth % 2]
        ld_xe = ld_xe_e[ex]
        ld_f1 = ld_f1_e[ex]
        for ht in range(EH // P):
            bank = pbig[:, (ht % 2) * 512:(ht % 2 + 1) * 512]
            for dc in range(NDC):
                pl.op("tensor", lambda e, ht=ht, dc=dc, th=th, bank=bank: e.matmul(
                    bank, f1w[:, dc, ht * P:(ht + 1) * P],
                    M2["xeT"][:, dc, th * 512:(th + 1) * 512],
                    start=(dc == 0), stop=(dc == NDC - 1)),
                    waits=W(("att", ld_xe), ("moew", ld_f1),
                            ("ch_scalar", fc1bank_sc[ht % 2] if dc == 0 else 0)),
                    inc="pe" if dc == NDC - 1 else None)
            pe_h = C["pe"]
            fc1bank_sc[ht % 2] = pl.ch(
                "scalar", lambda e, ht=ht, bank=bank, hb=hb: e.activation(
                    hb[:, ht, :], bank, AF.Gelu, bias=f1b_s[:, ht:ht + 1]),
                waits=W(("pe", pe_h), ("moew", f1b_used),
                        ("pe", fc2_done.get(gth - 2, 0))))
        fc1_done[gth] = C["pe"]
        gelu_done[gth] = C["ch_scalar"]

    def fc2_third(ex, th):
        nonlocal st_oe
        gth = ex * NTH + th
        hb = hTs[gth % 2]
        f2w = f2ws[ex % 2]
        ld_f2 = ld_f2_e[ex]
        for st in range(4):
            for half in range(2):
                bank = pb_[half]
                for hc in range(EH // P):
                    pl.op("tensor", lambda e, st=st, hc=hc, half=half, bank=bank,
                          hb=hb: e.matmul(
                        bank[:, :], hb[:, hc, st * P:(st + 1) * P],
                        f2w[:, hc, half * 512:(half + 1) * 512],
                        start=(hc == 0), stop=False),
                        waits=W(("ch_scalar", gelu_done[gth]), ("moew", ld_f2),
                                ("ch_vector", oe_free_v[0] if hc == 0 else 0)),
                        inc=None)
                pl.op("tensor", lambda e, half=half, bank=bank, ex=ex: e.matmul(
                    bank[:, :], ones_bf[:, :],
                    f2b_s[ex % 2][:, half * 512:(half + 1) * 512],
                    start=False, stop=True),
                    waits=W(("moew", ld_f2)),
                    inc="pe" if half == 1 else None)
            pe_o = C["pe"]
            par = st % 2
            for half in range(2):
                pl.ch("vector", lambda e, half=half, par=par: e.tensor_copy(
                    oes[par][:, half * 512:(half + 1) * 512], pb_[half][:, :]),
                    waits=W(("pe", pe_o), ("spillC", oe_st[par])))
            oe_free_v[0] = C["ch_vector"]
            st_oe = pl.op("sync", lambda e, ex=ex, th=th, st=st, par=par: e.dma_start(
                out=oe_d[ex * CLOC + th * 512 + st * P:
                         ex * CLOC + th * 512 + (st + 1) * P, :],
                in_=oes[par][:, :]),
                waits=W(("ch_vector", oe_free_v[0])), inc="spillC", inc_k=16)
            oe_st[par] = st_oe
        fc2_done[gth] = C["pe"]

    oe_free_v = [0]
    ld_xe_e = {}
    ld_f1_e = {}
    ld_f2_e = {}

    relM0_xe = [("gq", gq_scat), ("ch_vector", g["v_attn_jend"]),
                ("ch_gpsimd", g["gp_attn_jend"])]
    relM0_f1 = [("gq", gq_scat), ("pe", g["pe_attn_jend"])]

    def emit_loads(ex):
        nonlocal f1b_used
        prev_fc1 = fc1_done.get(ex * NTH - 1, 0)
        if ex == 0:
            ld_xe_e[0] = g["ld_xe0"]
        else:
            for dc in range(NDC):
                pl.op("sync", lambda e, ex=ex, dc=dc: e.dma_start_transpose(
                    M2["xeT"][:, dc, :],
                    xe_d[ex * CLOC:(ex + 1) * CLOC, dc * P:(dc + 1) * P]),
                    waits=relM + [("pe", prev_fc1)], inc="att", inc_k=16)
            ld_xe_e[ex] = C["att"]
        for dc in range(NDC):
            pl.op("gpsimd", lambda e, ex=ex, dc=dc: e.dma_start(
                out=f1w[:, dc, :], in_=f1w_d[ex, dc * P:(dc + 1) * P, :]),
                waits=(relM0_f1 if ex == 0 else relM + [("pe", prev_fc1)]),
                inc="moew", inc_k=16)
        pl.op("gpsimd", lambda e, ex=ex: e.dma_start(
            out=f1b_s[:, :], in_=f1b_d[ex]),
            waits=(relM0_f1 if ex == 0 else
                   relM + [("ch_scalar", gelu_done.get(ex * NTH - 1, 0))]),
            inc="moew", inc_k=16)
        ld_f1_e[ex] = C["moew"]
        f1b_used = ld_f1_e[ex]
        prev_fc2 = fc2_done.get((ex - 2) * NTH + NTH - 1, 0)
        for hc in range(EH // P):
            pl.op("gpsimd", lambda e, ex=ex, hc=hc: e.dma_start(
                out=f2ws[ex % 2][:, hc, :], in_=f2w_d[ex, hc * P:(hc + 1) * P, :]),
                waits=relM + [("pe", prev_fc2)], inc="moew", inc_k=16)
        pl.op("gpsimd", lambda e, ex=ex: e.dma_start(
            out=f2b_s[ex % 2][:, :], in_=f2b_d[ex]),
            waits=relM + [("pe", fc2_done.get((ex - 2) * NTH + NTH - 1, 0))],
            inc="moew", inc_k=16)
        ld_f2_e[ex] = C["moew"]

    emit_loads(0)
    fc1_third(0, 0)
    pend = [(0, 0)]
    for ex in range(NE):
        for th in range(NTH):
            if (ex, th) == (0, 0):
                continue
            fc1_third(ex, th)
            fc2_third(*pend.pop(0))
            pend.append((ex, th))
            if th == NTH - 1 and ex + 1 < NE:
                emit_loads(ex + 1)
    fc2_third(*pend.pop(0))
    g["oe_st"] = st_oe


def build_p3(pl, nc, g):
    C = pl.cnt
    region = g["region"]
    pb_ = g["pb_"]
    xnT_d = g["xnT_d"]; attn_d = g["attn_d"]; x_d = g["x_d"]; out_d = g["out_d"]
    s1w_d = g["s1w_d"]; s2w_d = g["s2w_d"]; oe_d = g["oe_d"]; h1_d = g["h1_d"]
    ones_bf = g["ones_bf"]; s2b_row = g["s2b_row"]; s1b_s = g["s1b_s"]
    bo_bc = g["bo_bc"]; vch_bo = g["vch_bo"]
    stab0 = g["stab0"]; stab1 = g["stab1"]; w0f = g["w0f"]; w1f = g["w1f"]
    spill_xnT = g["spill_xnT"]; spill_attn = g["spill_attn"]
    gq0 = g["gq0"]; ld0 = g["ld0"]
    oe_done = g.get("oe_st", 0)
    p2_pe = g["p2_pe"]; p2_v = g["p2_v"]; p2_s = g["p2_s"]; p2_gq = g["p2_gq"]

    spill_h1 = g["spill_h1"]
    P3b, _ = region(OFF_B, [("s2w", [P, NHT, D], BF16)])
    P3h, _ = region(OFF_A, [
        ("h1b", [P, NHT, SB3], BF16),
        ("xa0", [P, D], F32),
        ("xa1", [P, D], F32),
        ("at0", [P, D], BF16),
        ("at1", [P, D], BF16),
        ("oa0", [P, D], F32),
        ("oa1", [P, D], F32),
        ("oa2", [P, D], F32),
    ])
    for hcx in range(NHT):
        pl.op("gpsimd", lambda e, hcx=hcx: e.dma_start(
            out=P3b["s2w"][:, hcx, :], in_=s2w_d[hcx * P:(hcx + 1) * P, :]),
            waits=W(("spillH", spill_h1), ("pe", p2_pe), ("gq", p2_gq)),
            inc="moew", inc_k=16)
    s2w_ld = C["moew"]

    xas = [P3h["xa0"], P3h["xa1"]]
    ats = [P3h["at0"], P3h["at1"]]
    g0s = g["g0_t"]
    g1s = g["g1_t"]
    oas = [P3h["oa0"], P3h["oa1"], P3h["oa2"]]

    for t_ in g["g0_t"] + g["g1_t"]:
        pl.ch("vector", lambda e, t_=t_: e.memset(t_[:, :], 0.0),
              waits=W(("spillH", spill_h1)))
    v_gz = C["ch_vector"]

    NJT = SB3 // P
    asm_consumed = [0, 0]
    oa_store = [0, 0, 0]
    peA_mark = p2_pe
    peB_mark = p2_pe
    st_total = 0
    oi = 0
    bank23_free_v = 0
    for sbi in range(NSB3):
        hsem = "wa" if sbi % 2 == 0 else "wb"
        for ht in range(NHT // 2):
            pl.op("sync", lambda e, sbi=sbi, ht=ht: e.dma_start(
                out=P3h["h1b"][:, ht, :], in_=h1_d[sbi, ht]),
                waits=W(("spillH", spill_h1), ("pe", peA_mark)),
                inc=hsem, inc_k=16)
        ld_h1A = C[hsem]
        for ht in range(NHT // 2, NHT):
            pl.op("sync", lambda e, sbi=sbi, ht=ht: e.dma_start(
                out=P3h["h1b"][:, ht, :], in_=h1_d[sbi, ht]),
                waits=W(("spillH", spill_h1), ("pe", peB_mark)),
                inc=hsem, inc_k=16)
        ld_h1B = C[hsem]
        for jt in range(NJT):
            j = sbi * NJT + jt
            apar = j % 2
            pl.op("sync", lambda e, j=j, apar=apar: e.dma_start(
                out=xas[apar][:, :], in_=x_d[j * P:(j + 1) * P, :]),
                waits=W(("ch_vector", asm_consumed[apar]), ("spillH", spill_h1),
                        ("pe", p2_pe)),
                inc="asm", inc_k=16)
            pl.op("sync", lambda e, j=j, apar=apar: e.dma_start(
                out=ats[apar][:, :], in_=attn_d[j]),
                waits=W(("spillB", spill_attn), ("pe", p2_pe)),
                inc="asm", inc_k=16)
            asm_ld = C["asm"]
            gq_g = 0
            if STAGE >= 3:
                pl.op("gpsimd", lambda e, j=j, apar=apar: e.indirect_dma_start(
                    g0s[apar][:, :], None,
                    oe_d[:, :], IndirectOffsetOnAxis(stab0[:, j:j + 1], 0),
                    bounds_check=pl.breg(e), oob_is_err=False),
                    waits=W(("ch_vector", max(asm_consumed[apar], v_gz)),
                            ("spillC", oe_done)),
                    inc="gq", inc_k=16)
                pl.op("gpsimd", lambda e, j=j, apar=apar: e.indirect_dma_start(
                    g1s[apar][:, :], None,
                    oe_d[:, :], IndirectOffsetOnAxis(stab1[:, j:j + 1], 0),
                    bounds_check=pl.breg(e), oob_is_err=False),
                    inc="gq", inc_k=16)
                gq_g = C["gq"]

            for hcx in range(NHT):
                for half in range(2):
                    isA_last = (jt == NJT - 1 and hcx == NHT // 2 - 1 and half == 1)
                    pl.op("tensor", lambda e, jt=jt, hcx=hcx, half=half: e.matmul(
                        pb_[2 + half][:, :], P3h["h1b"][:, hcx, jt * P:(jt + 1) * P],
                        P3b["s2w"][:, hcx, half * 512:(half + 1) * 512],
                        start=(hcx == 0), stop=False),
                        waits=W(("moew", s2w_ld),
                                (hsem, ld_h1A if hcx == 0 else
                                 (ld_h1B if hcx == NHT // 2 else 0)),
                                ("ch_vector", bank23_free_v if hcx == 0 else 0)),
                        inc="pe" if isA_last else None)
                    if isA_last:
                        peA_mark = C["pe"]
            for half in range(2):
                pl.op("tensor", lambda e, half=half: e.matmul(
                    pb_[2 + half][:, :], ones_bf[:, :],
                    s2b_row[:, half * 512:(half + 1) * 512], start=False, stop=True),
                    waits=W(("gq", gq0)), inc="pe")
            pe_sh = C["pe"]
            if jt == NJT - 1:
                peB_mark = pe_sh

            dst = oas[oi % 3]; oi += 1
            for half in range(2):
                pl.ch("vector", lambda e, half=half, apar=apar, dst=dst: e.tensor_add(
                    dst[:, half * 512:(half + 1) * 512], pb_[2 + half][:, :],
                    xas[apar][:, half * 512:(half + 1) * 512]),
                    waits=W(("pe", pe_sh), ("asm", asm_ld),
                            ("stq", oa_store[oas.index(dst)])))
            bank23_free_v = C["ch_vector"]
            cur = dst
            nxt = oas[oi % 3]; oi += 1
            pl.ch("vector", lambda e, cur=cur, nxt=nxt: e.tensor_add(
                nxt[:, :], cur[:, :], bo_bc[:, :]),
                waits=W(("ch_vector", vch_bo), ("stq", oa_store[oas.index(nxt)])))
            cur = nxt
            if STAGE >= 2:
                nxt = oas[oi % 3]; oi += 1
                pl.ch("vector", lambda e, cur=cur, nxt=nxt, apar=apar: e.tensor_add(
                    nxt[:, :], cur[:, :], ats[apar][:, :]),
                    waits=W(("stq", oa_store[oas.index(nxt)])))
                cur = nxt
            if STAGE >= 3:
                nxt = oas[oi % 3]; oi += 1
                pl.ch("vector", lambda e, cur=cur, nxt=nxt, apar=apar, j=j: e.scalar_tensor_tensor(
                    nxt[:, :], g0s[apar][:, :], w0f[:, j:j + 1], cur[:, :],
                    ALU.mult, ALU.add),
                    waits=W(("gq", gq_g), ("stq", oa_store[oas.index(nxt)])))
                cur = nxt
                nxt = oas[oi % 3]; oi += 1
                pl.ch("vector", lambda e, cur=cur, nxt=nxt, apar=apar, j=j: e.scalar_tensor_tensor(
                    nxt[:, :], g1s[apar][:, :], w1f[:, j:j + 1], cur[:, :],
                    ALU.mult, ALU.add),
                    waits=W(("stq", oa_store[oas.index(nxt)])))
                cur = nxt
            v_asm = C["ch_vector"]
            asm_consumed[apar] = v_asm
            st_total = pl.op("sync", lambda e, j=j, cur=cur: e.dma_start(
                out=out_d[j * P:(j + 1) * P, :], in_=cur[:, :]),
                waits=W(("ch_vector", v_asm)), inc="stq", inc_k=16)
            oa_store[oas.index(cur)] = st_total

    pl.op("sync", lambda e: e.nop(), waits=W(("stq", st_total)))


def kernel(L0_attn_x, L0_attn_Wq, L0_attn_Wk, L0_attn_Wv, L0_attn_Wo, L0_attn_bo,
           L0_ffn_proj_w, L0_ffn_proj_b, L0_ffn_expert_bias,
           L0_ffn_fc1_w, L0_ffn_fc1_b, L0_ffn_fc2_w, L0_ffn_fc2_b,
           L0_ffn_sh1_w, L0_ffn_sh1_b, L0_ffn_sh2_w, L0_ffn_sh2_b):
    x = np.ascontiguousarray(np.asarray(L0_attn_x, np.float32))

    idx = np.zeros(D, np.int64)
    for h in range(16):
        for half in range(2):
            for i in range(32):
                idx[h * 64 + half * 32 + i] = h * 64 + 2 * i + half
    wq = np.ascontiguousarray(np.asarray(L0_attn_Wq, np.float32)[:, idx])
    wk = np.ascontiguousarray(np.asarray(L0_attn_Wk, np.float32)[:, idx])
    wv = np.ascontiguousarray(np.asarray(L0_attn_Wv, np.float32))
    wo = np.ascontiguousarray(np.asarray(L0_attn_Wo, np.float32))
    bo = np.asarray(L0_attn_bo, np.float32).reshape(1, D)
    pwt = np.ascontiguousarray(np.asarray(L0_ffn_proj_w, np.float32).T)
    pbv = np.asarray(L0_ffn_proj_b, np.float32).reshape(1, NE)
    eb = np.asarray(L0_ffn_expert_bias, np.float32).reshape(1, NE)
    f1w = np.ascontiguousarray(np.asarray(L0_ffn_fc1_w, np.float32).transpose(0, 2, 1))
    f1b = np.ascontiguousarray(
        np.asarray(L0_ffn_fc1_b, np.float32).reshape(NE, EH // P, P).transpose(0, 2, 1))
    f2w = np.ascontiguousarray(np.asarray(L0_ffn_fc2_w, np.float32).transpose(0, 2, 1))
    f2b = np.ascontiguousarray(np.asarray(L0_ffn_fc2_b, np.float32).reshape(NE, 1, D))
    s1w = np.ascontiguousarray(np.asarray(L0_ffn_sh1_w, np.float32).T)
    s1b = np.ascontiguousarray(np.asarray(L0_ffn_sh1_b, np.float32).reshape(NHT, P).T)
    s2w = np.ascontiguousarray(np.asarray(L0_ffn_sh2_w, np.float32).T)
    s2b = np.asarray(L0_ffn_sh2_b, np.float32).reshape(1, D)

    nc = build()
    shared = dict(wq=wq, wk=wk, wv=wv, wo=wo, bo=bo, pwt=pwt, pb=pbv, eb=eb,
                  f1w=f1w, f1b=f1b, f2w=f2w, f2b=f2b,
                  s1w=s1w, s1b=s1b, s2w=s2w, s2b=s2b)
    in_maps = [dict(shared, x=np.ascontiguousarray(x[c])) for c in range(NCORES)]
    res = run_bass_kernel_spmd(nc, in_maps, core_ids=list(range(NCORES)))
    out = np.stack([res.results[c]["out"] for c in range(NCORES)], axis=0)
    return out.astype(np.float32)


# version 25
# speedup vs baseline: 1.1576x; 1.1576x over previous
import numpy as np
import ml_dtypes
import concourse.bass as bass
import concourse.mybir as mybir
from concourse.bass import IndirectOffsetOnAxis
from concourse.bass_utils import run_bass_kernel_spmd


F32 = mybir.dt.float32
BF16 = mybir.dt.bfloat16
U32 = mybir.dt.uint32
AF = mybir.ActivationFunctionType
ALU = mybir.AluOpType
AX = mybir.AxisListType

P = 128
D = 1024
S = 4096
NT = S // P
NDC = D // P
NPAIR = 8
DH = 64
NE = 8
EH = 2048
SHID = 4096
CAP = 10240.0
CLOC = 1536
NSLOT = NE * CLOC
HUGE = 1 << 20
NCORES = 8
SB3 = 512
NSB3 = S // SB3
NHT = SHID // P
BISECT_ITERS = 18

STAGE = 3
DEFER_NORM = True

bf16 = ml_dtypes.bfloat16

OFF_A = 0
OFF_B = 64 * 1024
OFF_C = 128 * 1024
ARENA = 160 * 1024


class Plan:
    ENGS = ("sync", "gpsimd", "tensor", "vector", "scalar")

    def __init__(self, nc):
        self.nc = nc
        self.q = {e: [] for e in self.ENGS}
        self.sems = {}
        self.cnt = {}
        self.floors = {}
        self.bounds_reg = [None]

    def breg(self, engine):
        if self.bounds_reg[0] is None:
            self.bounds_reg[0] = engine.to_reg(NSLOT - 1)
        return self.bounds_reg[0]

    def sem(self, name):
        self.sems[name] = self.nc.alloc_semaphore(name)
        self.cnt[name] = 0
        return name

    def op(self, eng, fn, waits=(), inc=None, inc_k=1):
        w = tuple((s, v) for (s, v) in waits if v > 0)
        if inc is not None:
            self.cnt[inc] += inc_k
        after = self.cnt[inc] if inc else None
        sems = self.sems

        floors = self.floors.setdefault(eng, {})

        def emit(engine, fn=fn, w=w, inc=inc, inc_k=inc_k, floors=floors):
            live = [(s, v) for (s, v) in w if floors.get(s, 0) < v]
            for (s, v) in live:
                floors[s] = v
            for (s, v) in live[1:]:
                engine.wait_ge(sems[s], v)
            ins = fn(engine)
            if live:
                ins._wait_ge(sems[live[0][0]], live[0][1])
            if inc is not None:
                ins.then_inc(sems[inc], inc_k)

        self.q[eng].append(emit)
        return after

    def ch(self, eng, fn, waits=()):
        name = "ch_" + eng
        prev = self.cnt[name]
        return self.op(eng, fn, waits=list(waits) + [(name, prev)], inc=name)

    def finalize(self):
        nc = self.nc
        with nc.Block() as block:
            for eng in self.ENGS:
                fns = self.q[eng]
                if not fns:
                    continue

                def body(engine, fns=fns):
                    for f in fns:
                        f(engine)

                getattr(block, eng)(body)


def W(*pairs):
    return list(pairs)


def host_constants():
    c = {}
    c["ident_bf"] = np.eye(P, dtype=np.float32).astype(bf16)
    perm = np.zeros((P, P), np.float32)
    for h in range(2):
        for i in range(32):
            perm[h * 64 + i, h * 64 + 32 + i] = 1.0
            perm[h * 64 + 32 + i, h * 64 + i] = 1.0
    c["p128_bf"] = perm.astype(bf16)
    freqs = 1.0 / (10000.0 ** (np.arange(32, dtype=np.float64) / 32.0))
    ang = np.arange(S, dtype=np.float64)[None, :] * freqs[:, None]
    Cc = np.zeros((P, S), np.float64)
    Sg = np.zeros((P, S), np.float64)
    for l in range(P):
        i = l % 32
        Cc[l] = np.cos(ang[i])
        Sg[l] = np.sin(ang[i]) * (-1.0 if (l % 64) < 32 else 1.0)
    c["cosT"] = Cc.astype(np.float32).astype(bf16)
    c["sinT"] = Sg.astype(np.float32).astype(bf16)
    c["lstrict_bf"] = np.tril(np.ones((P, P), np.float32), -1).astype(bf16)
    dm = np.zeros((4, P, 1024), np.float32)
    for i in range(4):
        for p in range(P):
            keep = (np.arange(512) >= i * P + p).astype(np.float32)
            dm[i, p, 0:512] = keep
            dm[i, p, 512:1024] = keep
    c["dmask"] = dm.astype(bf16)
    c["huge_f"] = (np.float32(HUGE) + np.arange(P, dtype=np.float32))[:, None] * np.ones(
        (1, NT), np.float32)
    c["zer_bf"] = np.zeros((P, D), np.float32).astype(bf16)
    return c


def build():
    nc = bass.Bass("TRN2", target_bir_lowering=False, detect_race_conditions=False)
    pl = Plan(nc)
    C = pl.cnt

    x_d = nc.dram_tensor("x", [S, D], F32, kind="ExternalInput")
    wq_d = nc.dram_tensor("wq", [D, D], F32, kind="ExternalInput")
    wk_d = nc.dram_tensor("wk", [D, D], F32, kind="ExternalInput")
    wv_d = nc.dram_tensor("wv", [D, D], F32, kind="ExternalInput")
    wo_d = nc.dram_tensor("wo", [D, D], F32, kind="ExternalInput")
    bo_d = nc.dram_tensor("bo", [1, D], F32, kind="ExternalInput")
    pwt_d = nc.dram_tensor("pwt", [D, NE], F32, kind="ExternalInput")
    pb_d = nc.dram_tensor("pb", [1, NE], F32, kind="ExternalInput")
    eb_d = nc.dram_tensor("eb", [1, NE], F32, kind="ExternalInput")
    f1w_d = nc.dram_tensor("f1w", [NE, D, EH], F32, kind="ExternalInput")
    f1b_d = nc.dram_tensor("f1b", [NE, P, EH // P], F32, kind="ExternalInput")
    f2w_d = nc.dram_tensor("f2w", [NE, EH, D], F32, kind="ExternalInput")
    f2b_d = nc.dram_tensor("f2b", [NE, 1, D], F32, kind="ExternalInput")
    s1w_d = nc.dram_tensor("s1w", [D, SHID], F32, kind="ExternalInput")
    s1b_d = nc.dram_tensor("s1b", [P, NHT], F32, kind="ExternalInput")
    s2w_d = nc.dram_tensor("s2w", [SHID, D], F32, kind="ExternalInput")
    s2b_d = nc.dram_tensor("s2b", [1, D], F32, kind="ExternalInput")
    out_d = nc.dram_tensor("out", [S, D], F32, kind="ExternalOutput")

    xn_d = nc.dram_tensor("xn_d", [NT, P, D], BF16)
    xnT_d = nc.dram_tensor("xnT_d", [NDC, P, S], BF16)
    attn_d = nc.dram_tensor("attn_d", [NT, P, D], BF16)
    ctxT_d = nc.dram_tensor("ctxT_d", [NPAIR, P, S], BF16)
    h1_d = nc.dram_tensor("h1_d", [NSB3, NHT, P, SB3], BF16)
    xe_d = nc.dram_tensor("xe_d", [NSLOT, D], BF16)
    oe_d = nc.dram_tensor("oe_d", [NSLOT, D], BF16)
    rt_d = nc.dram_tensor("rt_d", [4, S], F32)
    rtall_d = nc.dram_tensor("rtall_d", [4 * NCORES, S], F32)

    hcst = host_constants()
    ident_c = nc.inline_tensor(hcst["ident_bf"], "ident_c")
    p128_c = nc.inline_tensor(hcst["p128_bf"], "p128_c")
    cos_c = nc.inline_tensor(hcst["cosT"], "cos_c")
    sin_c = nc.inline_tensor(hcst["sinT"], "sin_c")
    lstrict_c = nc.inline_tensor(hcst["lstrict_bf"], "lstrict_c")
    huge_c = nc.inline_tensor(hcst["huge_f"], "huge_c")
    dmask_c = nc.inline_tensor(hcst["dmask"].reshape(4 * P, 1024), "dmask_c")
    zer_c = nc.inline_tensor(hcst["zer_bf"], "zer_c")

    for s in ("ch_vector", "ch_scalar", "ch_gpsimd", "pe",
              "ldq", "gq", "stq", "cc", "xa", "xb", "wa", "wb",
              "spillA", "spillB", "spillC", "spillH", "moew", "asm",
              "att", "attb"):
        pl.sem(s)

    sbt = nc.alloc_sbuf_tensor
    ident_s = sbt("ident_s", [P, P], BF16)
    p128_s = sbt("p128_s", [P, P], BF16)
    lstrict_s = sbt("lstrict_s", [P, P], BF16)
    ones_bf = sbt("ones_bf", [1, P], BF16)
    ones_f = sbt("ones_f", [1, P], F32)
    onesc_bf = sbt("onesc_bf", [P, 1], BF16)
    ones_pp = sbt("ones_pp", [P, P], BF16)
    onesc_f = sbt("onesc_f", [P, 1], F32)
    bo_row = sbt("bo_row", [1, D], F32)
    bo_bc = sbt("bo_bc", [P, D], F32)
    s2b_row = sbt("s2b_row", [1, D], BF16)
    s1b_s = sbt("s1b_s", [P, NHT], F32)
    stab0 = sbt("stab0", [P, NT], U32)
    stab1 = sbt("stab1", [P, NT], U32)
    w0f = sbt("w0f", [P, NT], F32)
    w1f = sbt("w1f", [P, NT], F32)
    rl_t = [sbt("rl_t0", [P, D], BF16), sbt("rl_t1", [P, D], BF16)]
    g0_t = [sbt("g0_t0", [P, D], BF16), sbt("g0_t1", [P, D], BF16)]
    g1_t = [sbt("g1_t0", [P, D], BF16), sbt("g1_t1", [P, D], BF16)]

    arena = sbt("arena", [P, ARENA], mybir.dt.uint8)

    def carve(off_bytes, shape, dtype):
        nbytes = int(np.prod(shape[1:])) * mybir.dt.size(dtype)
        assert off_bytes + nbytes <= ARENA, (off_bytes, nbytes)
        v = arena[:, off_bytes:off_bytes + nbytes].bitcast(dtype)
        if shape[0] != P:
            v = v[0:shape[0]]
        if len(shape) > 2:
            pat = " ".join(f"d{i}" for i in range(1, len(shape)))
            v = v.rearrange(f"p ({pat}) -> p {pat}",
                            **{f"d{i}": shape[i] for i in range(1, len(shape))})
        return v

    def region(base, specs):
        out = {}
        off = base
        for name, shape, dtype in specs:
            out[name] = carve(off, shape, dtype)
            off += int(np.prod(shape[1:])) * mybir.dt.size(dtype)
        return out, off

    pbig4 = nc.alloc_psum_tensor("pbig4", [P, 2048], F32)
    pbig = nc.alloc_psum_tensor("pbig", [P, 1536], F32)
    pb_ = [pbig4[:, i * 512:(i + 1) * 512] for i in range(4)]
    pb_.extend([pbig[:, 0:512], pbig[:, 512:1024], pbig[:, 1024:1536]])
    pbt = nc.alloc_psum_tensor("pbt", [P, P], BF16)


    pl.op("sync", lambda e: e.dma_start(out=ident_s[:, :], in_=ident_c[:, :]), inc="ldq", inc_k=16)
    pl.op("sync", lambda e: e.dma_start(out=p128_s[:, :], in_=p128_c[:, :]), inc="ldq", inc_k=16)
    pl.op("sync", lambda e: e.dma_start(out=lstrict_s[:, :], in_=lstrict_c[:, :]), inc="ldq", inc_k=16)
    pl.op("sync", lambda e: e.dma_start(out=bo_row[:, :], in_=bo_d[:, :]), inc="ldq", inc_k=16)
    pl.op("sync", lambda e: e.dma_start(out=s1b_s[:, :], in_=s1b_d[:, :]), inc="ldq", inc_k=16)
    pl.op("gpsimd", lambda e: e.dma_start(out=s2b_row[:, :], in_=s2b_d[:, :]), inc="gq", inc_k=16)
    pl.ch("vector", lambda e: e.memset(ones_bf[:, :], 1.0))
    pl.ch("vector", lambda e: e.memset(ones_f[:, :], 1.0))
    pl.ch("vector", lambda e: e.memset(onesc_bf[:, :], 1.0))
    pl.ch("vector", lambda e: e.memset(ones_pp[:, :], 1.0))
    pl.ch("vector", lambda e: e.memset(onesc_f[:, :], 1.0))
    ld0 = C["ldq"]
    gq0 = C["gq"]
    vch_c = C["ch_vector"]
    v_bo = 0
    for h in range(2):
        pe_bo = pl.op("tensor", lambda e, h=h: e.matmul(
            pb_[6][:, :], ones_f[:, :], bo_row[:, h * 512:(h + 1) * 512],
            start=True, stop=True),
            waits=W(("ldq", ld0), ("ch_vector", vch_c), ("ch_vector", v_bo)), inc="pe")
        v_bo = pl.ch("vector", lambda e, h=h: e.tensor_copy(
            bo_bc[:, h * 512:(h + 1) * 512], pb_[6][:, :]), waits=W(("pe", pe_bo)))
    vch_bo = C["ch_vector"]

    P0c, p0c_end = region(OFF_C, [
        ("xt0", [P, D], F32),
        ("xt1", [P, D], F32),
        ("sqs", [P, D], F32),
        ("lnt", [P, 16], F32),
        ("xn0", [P, D], BF16),
        ("xn1", [P, D], BF16),
    ])
    xnt_bufs = [P0c["xn0"], P0c["xn1"]]
    xts = [P0c["xt0"], P0c["xt1"]]
    lnt = P0c["lnt"]
    xnT = carve(OFF_B, [P, NDC, S], BF16)

    xpar = ["xa", "xb"]
    vdone = [0, 0]
    sdone = [0, 0]
    xn_spill = [0, 0]
    tr_pe = [0, 0]
    vtr = 0
    for j in range(NT):
        par = j % 2
        pl.op("sync", lambda e, j=j, par=par: e.dma_start(
            out=xts[par][:, :], in_=x_d[j * P:(j + 1) * P, :]),
            waits=W(("ch_vector", vdone[par]), ("ch_scalar", sdone[par])),
            inc=xpar[par], inc_k=16)
        ldv = C[xpar[par]]
        ssq = pl.ch("scalar", lambda e, par=par: e.activation(
            P0c["sqs"][:, :], xts[par][:, :], AF.Square, scale=1.0 / 32.0,
            accum_out=lnt[:, 8:9]), waits=W((xpar[par], ldv)))
        sdone[par] = ssq
        pl.ch("vector", lambda e, par=par: e.reduce_sum(
            lnt[:, 0:1], xts[par][:, :], AX.X), waits=W((xpar[par], ldv)))
        pl.ch("vector", lambda e: e.tensor_scalar_mul(lnt[:, 1:2], lnt[:, 0:1], 1.0 / D))
        pl.ch("vector", lambda e: e.tensor_mul(lnt[:, 2:3], lnt[:, 1:2], lnt[:, 1:2]))
        pl.ch("vector", lambda e: e.tensor_sub(lnt[:, 3:4], lnt[:, 8:9], lnt[:, 2:3]),
              waits=W(("ch_scalar", ssq)))
        pl.ch("vector", lambda e: e.tensor_scalar_add(lnt[:, 4:5], lnt[:, 3:4], 1e-5))
        sq = pl.ch("scalar", lambda e: e.sqrt(lnt[:, 5:6], lnt[:, 4:5]),
                   waits=W(("ch_vector", C["ch_vector"])))
        pl.ch("vector", lambda e: e.reciprocal(lnt[:, 6:7], lnt[:, 5:6]),
              waits=W(("ch_scalar", sq)))
        vn = pl.ch("vector", lambda e, j=j, par=par: e.tensor_scalar(
            xnt_bufs[par][:, :], xts[par][:, :], lnt[:, 1:2], lnt[:, 6:7],
            ALU.subtract, ALU.mult),
            waits=W(("spillC", xn_spill[par]), ("pe", tr_pe[par])))
        vdone[par] = vn
        pl.op("sync", lambda e, j=j, par=par: e.dma_start(
            out=xn_d[j], in_=xnt_bufs[par][:, :]),
            waits=W(("ch_vector", vn)), inc="spillC", inc_k=16)
        xn_spill[par] = C["spillC"]
        for dc in range(NDC):
            pt = pl.op("tensor", lambda e, j=j, dc=dc, par=par: e.transpose(
                pbt[:, :], xnt_bufs[par][:, dc * P:(dc + 1) * P], ident_s[:, :]),
                waits=W(("ch_vector", vn), ("ldq", ld0), ("ch_vector", vtr)), inc="pe")
            vtr = pl.ch("vector", lambda e, j=j, dc=dc: e.tensor_copy(
                xnT[:, dc, j * P:(j + 1) * P], pbt[:, :]), waits=W(("pe", pt)))
        tr_pe[par] = C["pe"]
    vch_p0 = C["ch_vector"]
    spill_xn = C["spillC"]
    for dc in range(NDC):
        pl.op("sync", lambda e, dc=dc: e.dma_start(out=xnT_d[dc], in_=xnT[:, dc, :]),
              waits=W(("ch_vector", vch_p0)), inc="spillA", inc_k=16)
    spill_xnT = C["spillA"]

    s1w_s = carve(OFF_A, [P, NDC, SHID], BF16)
    h1c = [g0_t[0][:, 0:SB3], g0_t[1][:, 0:SB3]]
    for dc in range(NDC):
        pl.op("gpsimd", lambda e, dc=dc: e.dma_start(
            out=s1w_s[:, dc, :], in_=s1w_d[dc * P:(dc + 1) * P, :]),
            inc="moew", inc_k=16)
    s1w_ld = C["moew"]
    p3a_st = {"i": 0, "sgels": [0, 0, 0, 0], "spills": [0, 0]}

    def p3a_fill(n):
        for _ in range(n):
            i = p3a_st["i"]
            if i >= NSB3 * NHT:
                return
            p3a_st["i"] = i + 1
            sbi, ht = divmod(i, NHT)
            bank = pb_[i % 4]
            w4 = p3a_st["sgels"][i % 4]
            for dc in range(NDC):
                pl.op("tensor", lambda e, sbi=sbi, ht=ht, dc=dc, bank=bank: e.matmul(
                    bank[:, 0:SB3], s1w_s[:, dc, ht * P:(ht + 1) * P],
                    xnT[:, dc, sbi * SB3:(sbi + 1) * SB3],
                    start=(dc == 0), stop=(dc == NDC - 1)),
                    waits=W(("moew", s1w_ld), ("ch_vector", vch_p0),
                            ("ch_scalar", w4 if dc == 0 else 0)),
                    inc="pe" if dc == NDC - 1 else None)
            pe_h = C["pe"]
            cb = i % 2
            sgel = pl.ch("scalar", lambda e, cb=cb, ht=ht, bank=bank: e.activation(
                h1c[cb][:, :], bank[:, 0:SB3], AF.Gelu, bias=s1b_s[:, ht:ht + 1]),
                waits=W(("pe", pe_h), ("ldq", ld0),
                        ("spillH", p3a_st["spills"][cb])))
            p3a_st["sgels"][i % 4] = sgel
            pl.op("sync", lambda e, sbi=sbi, ht=ht, cb=cb: e.dma_start(
                out=h1_d[sbi, ht], in_=h1c[cb][:, :]),
                waits=W(("ch_scalar", sgel)), inc="spillH", inc_k=16)
            p3a_st["spills"][cb] = C["spillH"]

    env = dict(locals())

    if STAGE >= 3:
        build_routing(pl, nc, env)
    p3a_fill(NSB3 * NHT)
    env["p3a_pe"] = C["pe"]
    env["p3a_sc"] = C["ch_scalar"]
    env["spill_h1"] = C["spillH"]
    env["gq_scat"] = C["gq"]
    env["v_routing"] = C["ch_vector"]

    if STAGE >= 2:
        build_attention(pl, nc, env)
    else:
        vza = pl.ch("vector", lambda e: e.memset(P0c["sqs"][:, :], 0.0),
                    waits=W(("ch_scalar", C["ch_scalar"])))
        ztile = P0c["sqs"].bitcast(BF16)
        for j in range(NT):
            pl.op("sync", lambda e, j=j: e.dma_start(out=attn_d[j], in_=ztile[:, 0:D]),
                  waits=W(("ch_vector", vza)), inc="spillB", inc_k=16)
    env["spill_attn"] = C["spillB"]

    if STAGE >= 3:
        build_moe(pl, nc, env)
    env["p2_pe"] = C["pe"]
    env["p2_v"] = C["ch_vector"]
    env["p2_s"] = C["ch_scalar"]
    env["p2_gq"] = C["gq"]

    build_p3(pl, nc, env)

    pl.finalize()
    return nc


def build_routing(pl, nc, g):
    C = pl.cnt
    carve = g["carve"]
    pb_ = g["pb_"]
    xnT = g["xnT"]; xn_d = g["xn_d"]; xe_d = g["xe_d"]
    pwt_d = g["pwt_d"]; pb_d = g["pb_d"]; eb_d = g["eb_d"]
    rt_d = g["rt_d"]; rtall_d = g["rtall_d"]; huge_c = g["huge_c"]
    ones_bf = g["ones_bf"]; ones_f = g["ones_f"]
    onesc_bf = g["onesc_bf"]; onesc_f = g["onesc_f"]
    lstrict_s = g["lstrict_s"]
    stab0 = g["stab0"]; stab1 = g["stab1"]; w0f = g["w0f"]; w1f = g["w1f"]
    vch_p0 = g["vch_p0"]; spill_xn = g["spill_xn"]
    fill = g.get("p3a_fill") or (lambda n: None)
    zer_c = g["zer_c"]

    R1 = {}
    off = OFF_C
    for name, shape, dtype in [
        ("pwt", [P, NDC, NE], BF16), ("prow", [1, NE], BF16),
        ("ebrow", [1, NE], F32), ("eb_bc", [P, NE], F32),
        ("z", [P, NT, NE], F32), ("zs", [P, NT, NE], F32),
        ("v8", [P, NE], F32), ("i8", [P, NE], U32), ("i8f", [P, NE], F32),
        ("v12", [P, NT, 2], F32), ("e12", [P, NT, 2], U32),
        ("ebr", [P, NT * NE], F32),
        ("w1", [P, NT], F32), ("w2", [P, NT], F32),
        ("e1", [P, NT], F32), ("e2", [P, NT], F32),
        ("wall", [P, 512], F32), ("eall", [P, 512], F32),
        ("mask8", [P, NE, 512], BF16), ("gtmp", [P, 512], F32),
        ("cnts", [P, NE], F32), ("rows", [1, 64], F32),
        ("hibc", [P, NE], F32),
        ("m0", [P, NT], F32), ("m1", [P, NT], F32),
        ("pos0", [P, NT], F32), ("pos1", [P, NT], F32),
        ("acc0a", [P, NT], F32), ("acc0b", [P, NT], F32),
        ("acc1a", [P, NT], F32), ("acc1b", [P, NT], F32),
        ("k0", [P, NT], F32), ("k1", [P, NT], F32),
        ("sc1", [P, NT], F32), ("sc2", [P, NT], F32),
        ("rsum", [P, 4], F32), ("rsbf", [P, 4], BF16),
        ("hugef", [P, NT], F32), ("zrow", [P, NT], F32),
        ("gtu", [1, NE], U32), ("m2u", [P, NT], U32),
    ]:
        R1[name] = carve(off, shape, dtype)
        off += int(np.prod(shape[1:])) * mybir.dt.size(dtype)
    assert off <= ARENA, off

    rel1 = [("ch_vector", vch_p0), ("ch_scalar", C["ch_scalar"])]
    for dc in range(NDC):
        pl.op("gpsimd", lambda e, dc=dc: e.dma_start(
            out=R1["pwt"][:, dc, :], in_=pwt_d[dc * P:(dc + 1) * P, :]),
            waits=rel1, inc="gq", inc_k=16)
    pl.op("gpsimd", lambda e: e.dma_start(out=R1["prow"][:, :], in_=pb_d[:, :]),
          inc="gq", inc_k=16)
    pl.op("sync", lambda e: e.dma_start(out=R1["ebrow"][:, :], in_=eb_d[:, :]),
          waits=rel1, inc="ldq", inc_k=16)
    pl.op("sync", lambda e: e.dma_start(out=R1["hugef"][:, :], in_=huge_c[:, :]),
          waits=rel1, inc="ldq", inc_k=16)
    gq_rt = C["gq"]; ld_rt = C["ldq"]

    peb = pl.op("tensor", lambda e: e.matmul(
        pb_[6][:, 0:NE], ones_f[:, :], R1["ebrow"][:, :], start=True, stop=True),
        waits=W(("ldq", ld_rt), ("ch_vector", g["vch_bo"])), inc="pe")
    v_eb = pl.ch("vector", lambda e: e.tensor_copy(R1["eb_bc"][:, :], pb_[6][:, 0:NE]),
                 waits=W(("pe", peb)))
    for j in range(NT):
        pl.ch("vector", lambda e, j=j: e.tensor_copy(
            R1["ebr"][:, j * NE:(j + 1) * NE], R1["eb_bc"][:, :]))
    v_eb = C["ch_vector"]
    pl.ch("vector", lambda e: e.memset(R1["zrow"][:, :], 0.0))

    for j in range(NT):
        for dc in range(NDC):
            pl.op("tensor", lambda e, j=j, dc=dc: e.matmul(
                pb_[5][:, 0:NE], xnT[:, dc, j * P:(j + 1) * P], R1["pwt"][:, dc, :],
                start=(dc == 0), stop=False),
                waits=W(("gq", gq_rt),
                        ("ch_vector", C["ch_vector"] if dc == 0 else 0)),
                inc=None)
        pz = pl.op("tensor", lambda e: e.matmul(
            pb_[5][:, 0:NE], ones_bf[:, :], R1["prow"][:, :], start=False, stop=True),
            inc="pe")
        pl.ch("vector", lambda e, j=j: e.tensor_copy(R1["z"][:, j, :], pb_[5][:, 0:NE]),
              waits=W(("pe", pz)))
    vz = C["ch_vector"]
    sgm = pl.ch("scalar", lambda e: e.activation(
        R1["zs"].rearrange("p a b -> p (a b)"), R1["z"].rearrange("p a b -> p (a b)"),
        AF.Sigmoid), waits=W(("ch_vector", vz)))
    vadd = pl.ch("vector", lambda e: e.tensor_add(
        R1["zs"].rearrange("p a b -> p (a b)"), R1["zs"].rearrange("p a b -> p (a b)"),
        R1["ebr"][:, :]), waits=W(("ch_scalar", sgm), ("ch_vector", v_eb)))
    pl.ch("vector", lambda e: e.tensor_scalar_max(
        R1["zs"].rearrange("p a b -> p (a b)"),
        R1["zs"].rearrange("p a b -> p (a b)"), 1e-12))
    for j in range(NT):
        pl.ch("vector", lambda e, j=j: e.max(R1["v8"][:, :], R1["zs"][:, j, :]))
        pl.ch("vector", lambda e, j=j: e.max_index(
            R1["i8"][:, :], R1["v8"][:, :], R1["zs"][:, j, :]))
        pl.ch("vector", lambda e, j=j: e.tensor_copy(
            R1["v12"][:, j, :], R1["v8"][:, 0:2]))
        pl.ch("vector", lambda e, j=j: e.tensor_copy(
            R1["e12"][:, j, :], R1["i8"][:, 0:2]))
    pl.ch("vector", lambda e: e.tensor_add(
        R1["w2"][:, :], R1["v12"][:, :, 0], R1["v12"][:, :, 1]))
    pl.ch("vector", lambda e: e.reciprocal(R1["w1"][:, :], R1["w2"][:, :]))
    pl.ch("vector", lambda e: e.tensor_mul(
        R1["w2"][:, :], R1["v12"][:, :, 1], R1["w1"][:, :]))
    pl.ch("vector", lambda e: e.tensor_mul(
        R1["w1"][:, :], R1["v12"][:, :, 0], R1["w1"][:, :]))
    pl.ch("vector", lambda e: e.tensor_copy(R1["e1"][:, :], R1["e12"][:, :, 0]))
    pl.ch("vector", lambda e: e.tensor_copy(R1["e2"][:, :], R1["e12"][:, :, 1]))
    v_top2 = C["ch_vector"]

    for fi, nm in enumerate(("w1", "w2", "e1", "e2")):
        pl.op("gpsimd", lambda e, fi=fi, nm=nm: e.dma_start(
            out=rt_d[fi].rearrange("(p j) -> p j", p=P), in_=R1[nm][:, :]),
            waits=W(("ch_vector", v_top2)), inc="gq", inc_k=16)
    gq_rtw = C["gq"]
    pl.op("gpsimd", lambda e: e.collective_compute(
        "AllGather", ALU.bypass, replica_groups=[list(range(NCORES))],
        ins=[rt_d[:, :]], outs=[rtall_d[:, :]]),
        waits=W(("gq", gq_rtw)), inc="cc", inc_k=1)
    fill(16)
    for cc_ in range(NCORES):
        for sl in range(2):
            col = sl * 256 + cc_ * 32
            pl.op("gpsimd", lambda e, cc_=cc_, sl=sl, col=col: e.dma_start(
                out=R1["wall"][:, col:col + 32],
                in_=rtall_d[4 * cc_ + sl].rearrange("(p j) -> p j", p=P)),
                waits=W(("cc", 1)), inc="ldq", inc_k=16)
            pl.op("gpsimd", lambda e, cc_=cc_, sl=sl, col=col: e.dma_start(
                out=R1["eall"][:, col:col + 32],
                in_=rtall_d[4 * cc_ + 2 + sl].rearrange("(p j) -> p j", p=P)),
                waits=W(("cc", 1)), inc="ldq", inc_k=16)
    ld_we = C["ldq"]

    for ex in range(NE):
        pl.ch("vector", lambda e, ex=ex: e.tensor_single_scalar(
            R1["mask8"][:, ex, :], R1["eall"][:, :], float(ex), ALU.is_equal),
            waits=W(("ldq", ld_we), ("spillC", spill_xn)))
    v_masks = C["ch_vector"]

    R = R1["rows"]
    lo_s, hi_s, lo_a, hi_a = 0, 8, 40, 48
    pl.ch("vector", lambda e: e.memset(R[:, lo_s:lo_s + 8], 0.0))
    pl.ch("vector", lambda e: e.memset(R[:, hi_s:hi_s + 8], 1.0))
    cur = 0
    for it in range(BISECT_ITERS):
        lo0 = lo_s if cur == 0 else lo_a
        hi0 = hi_s if cur == 0 else hi_a
        lo1 = lo_a if cur == 0 else lo_s
        hi1 = hi_a if cur == 0 else hi_s
        pl.ch("vector", lambda e, lo0=lo0, hi0=hi0: e.tensor_add(
            R[:, 16:24], R[:, lo0:lo0 + 8], R[:, hi0:hi0 + 8]))
        vmid = pl.ch("vector", lambda e: e.tensor_scalar_mul(R[:, 16:24], R[:, 16:24], 0.5))
        pm = pl.op("tensor", lambda e: e.matmul(
            pb_[6][:, 0:NE], ones_f[:, :], R[:, 16:24], start=True, stop=True),
            waits=W(("ch_vector", vmid)), inc="pe")
        vbc = pl.ch("vector", lambda e: e.tensor_copy(R1["hibc"][:, :], pb_[6][:, 0:NE]),
                    waits=W(("pe", pm)))
        fill(8)
        for ex in range(NE):
            pl.ch("vector", lambda e, ex=ex: e.tensor_scalar(
                R1["gtmp"][:, :], R1["wall"][:, :], R1["hibc"][:, ex:ex + 1], None,
                ALU.is_ge), waits=W(("ch_vector", max(vbc, v_masks))))
            pl.ch("vector", lambda e, ex=ex: e.tensor_mul(
                R1["gtmp"][:, :], R1["gtmp"][:, :], R1["mask8"][:, ex, :]))
            pl.ch("vector", lambda e, ex=ex: e.reduce_sum(
                R1["cnts"][:, ex:ex + 1], R1["gtmp"][:, :], AX.X))
        vcnt = C["ch_vector"]
        pcm = pl.op("tensor", lambda e: e.matmul(
            pb_[5][0:1, 0:NE], onesc_f[:, :], R1["cnts"][:, :], start=True, stop=True),
            waits=W(("ch_vector", vcnt)), inc="pe")
        pl.ch("vector", lambda e: e.tensor_copy(R[:, 24:32], pb_[5][0:1, 0:NE]),
              waits=W(("pe", pcm)))
        pl.ch("vector", lambda e: e.tensor_single_scalar(
            R1["gtu"][:, :], R[:, 24:32], CAP, ALU.is_gt))
        pl.ch("vector", lambda e, lo0=lo0, lo1=lo1: e.select(
            R[:, lo1:lo1 + 8], R1["gtu"][:, :], R[:, 16:24], R[:, lo0:lo0 + 8],
            add_drain=True))
        pl.ch("vector", lambda e, hi0=hi0, hi1=hi1: e.select(
            R[:, hi1:hi1 + 8], R1["gtu"][:, :], R[:, hi0:hi0 + 8], R[:, 16:24],
            add_drain=True))
        fill(2)
        cur = 1 - cur
    hi_fin = hi_s if cur == 0 else hi_a
    vhi = C["ch_vector"]
    ph = pl.op("tensor", lambda e, hi_fin=hi_fin: e.matmul(
        pb_[6][:, 0:NE], ones_f[:, :], R[:, hi_fin:hi_fin + 8], start=True, stop=True),
        waits=W(("ch_vector", vhi)), inc="pe")
    v_hibc = pl.ch("vector", lambda e: e.tensor_copy(R1["hibc"][:, :], pb_[6][:, 0:NE]),
                   waits=W(("pe", ph)))
    fill(4)

    pl.ch("vector", lambda e: e.tensor_copy(R1["acc0a"][:, :], R1["hugef"][:, :]),
          waits=W(("ldq", ld_rt)))
    pl.ch("vector", lambda e: e.tensor_copy(R1["acc1a"][:, :], R1["hugef"][:, :]))
    pl.ch("vector", lambda e: e.memset(R1["k0"][:, :], 0.0))
    pl.ch("vector", lambda e: e.memset(R1["k1"][:, :], 0.0))
    acc = [[R1["acc0a"], R1["acc0b"]], [R1["acc1a"], R1["acc1b"]]]
    accur = [0, 0]
    for ex in range(NE):
        for sl in range(2):
            wsl = R1["w1"] if sl == 0 else R1["w2"]
            esl = R1["e1"] if sl == 0 else R1["e2"]
            msl = R1["m0"] if sl == 0 else R1["m1"]
            pl.ch("vector", lambda e, esl=esl, ex=ex: e.tensor_single_scalar(
                R1["sc1"][:, :], esl[:, :], float(ex), ALU.is_equal),
                waits=W(("ch_vector", v_hibc)))
            pl.ch("vector", lambda e, wsl=wsl, ex=ex: e.tensor_scalar(
                R1["sc2"][:, :], wsl[:, :], R1["hibc"][:, ex:ex + 1], None,
                ALU.is_ge))
            pl.ch("vector", lambda e, msl=msl: e.tensor_mul(
                msl[:, :], R1["sc1"][:, :], R1["sc2"][:, :]))
        pl.ch("vector", lambda e: e.reduce_sum(R1["rsum"][:, 0:1], R1["m0"][:, :], AX.X))
        pl.ch("vector", lambda e: e.reduce_sum(R1["rsum"][:, 1:2], R1["m1"][:, :], AX.X))
        vrs = pl.ch("vector", lambda e: e.tensor_copy(R1["rsbf"][:, 0:2], R1["rsum"][:, 0:2]))
        pt0 = pl.op("tensor", lambda e: e.matmul(
            pb_[6][:, 0:2], lstrict_s[:, :], R1["rsbf"][:, 0:2], start=True, stop=True),
            waits=W(("ch_vector", vrs)), inc="pe")
        pl.ch("vector", lambda e: e.tensor_copy(R1["rsum"][:, 2:4], pb_[6][:, 0:2]),
              waits=W(("pe", pt0)))
        pt1 = pl.op("tensor", lambda e: e.matmul(
            pb_[6][0:1, 4:5], R1["rsbf"][:, 0:1], onesc_bf[:, :], start=True, stop=True),
            waits=W(("ch_vector", vrs)), inc="pe")
        vt0 = pl.ch("vector", lambda e: e.tensor_copy(R1["rsbf"][0:1, 2:3], pb_[6][0:1, 4:5]),
                    waits=W(("pe", pt1)))
        pt2 = pl.op("tensor", lambda e: e.matmul(
            pb_[6][:, 6:7], ones_bf[:, :], R1["rsbf"][0:1, 2:3], start=True, stop=True),
            waits=W(("ch_vector", vt0)), inc="pe")
        pl.ch("vector", lambda e: e.tensor_copy(R1["rsum"][:, 0:1], pb_[6][:, 6:7]),
              waits=W(("pe", pt2)))
        fill(3)
        for sl in range(2):
            msl = R1["m0"] if sl == 0 else R1["m1"]
            pos = R1["pos0"] if sl == 0 else R1["pos1"]
            pl.ch("vector", lambda e, msl=msl: e.tensor_tensor_scan(
                R1["sc1"][:, :], R1["zrow"][:, :], msl[:, :], 0.0, ALU.add, ALU.add))
            pl.ch("vector", lambda e, msl=msl, pos=pos: e.tensor_sub(
                pos[:, :], R1["sc1"][:, :], msl[:, :]))
            pl.ch("vector", lambda e, pos=pos, sl=sl: e.tensor_scalar_add(
                pos[:, :], pos[:, :], R1["rsum"][:, 2 + sl:3 + sl]))
        pl.ch("vector", lambda e: e.tensor_scalar_add(
            R1["pos1"][:, :], R1["pos1"][:, :], R1["rsum"][:, 0:1]))
        for sl in range(2):
            msl = R1["m0"] if sl == 0 else R1["m1"]
            pos = R1["pos0"] if sl == 0 else R1["pos1"]
            ksl = R1["k0"] if sl == 0 else R1["k1"]
            a = acc[sl]
            cu = accur[sl]
            pl.ch("vector", lambda e, pos=pos: e.tensor_single_scalar(
                R1["sc1"][:, :], pos[:, :], float(CLOC), ALU.is_lt))
            pl.ch("vector", lambda e, msl=msl: e.tensor_mul(
                R1["sc1"][:, :], R1["sc1"][:, :], msl[:, :]))
            pl.ch("vector", lambda e, pos=pos, ex=ex: e.tensor_scalar_add(
                R1["sc2"][:, :], pos[:, :], float(ex * CLOC)))
            pl.ch("vector", lambda e: e.tensor_copy(R1["m2u"][:, :], R1["sc1"][:, :]))
            pl.ch("vector", lambda e, a=a, cu=cu: e.select(
                a[1 - cu][:, :], R1["m2u"][:, :], R1["sc2"][:, :], a[cu][:, :],
                add_drain=True))
            pl.ch("vector", lambda e, ksl=ksl: e.tensor_add(
                ksl[:, :], ksl[:, :], R1["sc1"][:, :]))
            accur[sl] = 1 - cu
    pl.ch("vector", lambda e: e.tensor_copy(stab0[:, :], acc[0][accur[0]][:, :]))
    pl.ch("vector", lambda e: e.tensor_copy(stab1[:, :], acc[1][accur[1]][:, :]))
    pl.ch("vector", lambda e: e.tensor_mul(w0f[:, :], R1["w1"][:, :], R1["k0"][:, :]))
    pl.ch("vector", lambda e: e.tensor_mul(w1f[:, :], R1["w2"][:, :], R1["k1"][:, :]))
    v_stab = C["ch_vector"]

    for b in range(NSLOT // P):
        pl.op("gpsimd", lambda e, b=b: e.dma_start(
            out=xe_d[b * P:(b + 1) * P, :], in_=zer_c[:, :]),
            inc="gq", inc_k=16)
    ld_pref = C["gq"]
    rls = g["rl_t"]
    rl_used = [0, 0]
    for j in range(NT):
        par = j % 2
        rsem = "wa" if par == 0 else "wb"
        pl.op("sync", lambda e, j=j, par=par: e.dma_start(
            out=rls[par][:, :], in_=xn_d[j]),
            waits=W(("spillC", spill_xn), ("gq", rl_used[par])),
            inc=rsem, inc_k=16)
        ld_rl = C[rsem]
        for sl in range(2):
            st = stab0 if sl == 0 else stab1
            pl.op("gpsimd", lambda e, j=j, st=st, par=par: e.indirect_dma_start(
                xe_d[:, :], IndirectOffsetOnAxis(st[:, j:j + 1], 0),
                rls[par][:, :], None,
                bounds_check=pl.breg(e), oob_is_err=False),
                waits=W(("ch_vector", v_stab), ("gq", ld_pref), (rsem, ld_rl)),
                inc="gq", inc_k=16)
        rl_used[par] = C["gq"]


def build_attention(pl, nc, g):
    C = pl.cnt
    carve = g["carve"]
    pb_ = g["pb_"]; pbig4 = g["pbig4"]; pbig = g["pbig"]
    dmask_c = g["dmask_c"]
    xnT = g["xnT"]
    wq_d = g["wq_d"]; wk_d = g["wk_d"]; wv_d = g["wv_d"]; wo_d = g["wo_d"]
    ctxT_d = g["ctxT_d"]; attn_d = g["attn_d"]
    cos_c = g["cos_c"]; sin_c = g["sin_c"]
    p128_s = g["p128_s"]; ones_f = g["ones_f"]; ones_bf = g["ones_bf"]
    ones_pp = g["ones_pp"]
    spill_xn = g["spill_xn"]; vch_p0 = g["vch_p0"]
    v_routing = g["v_routing"]; gq_scat = g["gq_scat"]

    A2 = {}
    offs = {}
    off = OFF_A
    for name, shape, dtype in [
        ("cosT", [P, S], BF16), ("sinT", [P, S], BF16),
        ("qraw0", [P, 512], BF16), ("qraw1", [P, 512], BF16),
        ("q2r", [P, S], BF16), ("k2r", [P, S], BF16),
        ("v2", [P, NT, 132], BF16),
        ("asb0", [P, 1024], BF16), ("asb1", [P, 1024], BF16),
        ("am0", [P, 1024], BF16), ("am1", [P, 1024], BF16),
        ("rt0", [P, 512], BF16), ("rt1", [P, 512], BF16),
        ("rcpb", [P, 1024], BF16),
        ("ctxe", [P, 1024], BF16),
        ("craw", [P, 1024], BF16),
        ("sums", [P, 1024], F32),
    ]:
        A2[name] = carve(off, shape, dtype)
        offs[name] = off
        off += int(np.prod(shape[1:])) * mybir.dt.size(dtype)
    assert off <= OFF_B, off
    amsf = carve(offs["am0"], [P, 1024], F32)
    offc = OFF_C + 18 * 1024
    for name, shape, dtype in [
        ("dmask", [P, 4, 1024], BF16),
        ("wqs", [P, NDC, P], BF16), ("wks", [P, NDC, P], BF16),
        ("wvs", [P, NDC, P], BF16),
    ]:
        A2[name] = carve(offc, shape, dtype)
        offc += int(np.prod(shape[1:])) * mybir.dt.size(dtype)
    assert offc <= ARENA, offc

    relC = [("spillC", spill_xn), ("ch_vector", vch_p0),
            ("ch_vector", v_routing), ("gq", gq_scat)]
    relA = relC + [("pe", g.get("p3a_pe", 0))]
    pl.op("sync", lambda e: e.dma_start(out=A2["cosT"][:, :], in_=cos_c[:, :]),
          waits=relA, inc="ldq", inc_k=16)
    pl.op("sync", lambda e: e.dma_start(out=A2["sinT"][:, :], in_=sin_c[:, :]),
          waits=relA, inc="ldq", inc_k=16)
    pl.op("sync", lambda e: e.dma_start(
        out=A2["dmask"][:, :, :],
        in_=dmask_c[:, :].rearrange("(a p) b -> p a b", p=P)),
        waits=relC, inc="ldq", inc_k=16)
    ld_cs = C["ldq"]
    vone = pl.ch("vector", lambda e: e.memset(A2["v2"][:, :, 64:66], 1.0), waits=relA)
    vone = pl.ch("vector", lambda e: e.memset(A2["v2"][:, :, 130:132], 1.0))
    v_v2ones = C["ch_vector"]

    qraws = [A2["qraw0"], A2["qraw1"]]
    kraws = qraws
    asb = [A2["asb0"], A2["asb1"]]
    ams = [A2["am0"], A2["am1"]]
    rts = [A2["rt0"], A2["rt1"]]
    NBLK = S // 512

    gen_pe_prev = 0
    norm_v_prev = 0
    exp_cons = [g.get("p3a_sc", 0), g.get("p3a_sc", 0)]
    ctx_read = [0, 0]
    pool_read = [0, 0]
    st_ctx = 0
    pending_norm = None
    carry_tails = []
    carry_norm = None
    bank45_free = [0]
    craw_rd_v = [0]
    sums_rd_v = [0]
    norm_pmm_pe = [0]

    for hp in range(NPAIR):
        for dc in range(NDC):
            pl.op("gpsimd", lambda e, hp=hp, dc=dc: e.dma_start(
                out=A2["wqs"][:, dc, :],
                in_=wq_d[dc * P:(dc + 1) * P, hp * P:(hp + 1) * P]),
                waits=relC + [("pe", gen_pe_prev)], inc="att", inc_k=16)
            pl.op("gpsimd", lambda e, hp=hp, dc=dc: e.dma_start(
                out=A2["wks"][:, dc, :],
                in_=wk_d[dc * P:(dc + 1) * P, hp * P:(hp + 1) * P]),
                inc="att", inc_k=16)
            pl.op("gpsimd", lambda e, hp=hp, dc=dc: e.dma_start(
                out=A2["wvs"][:, dc, :],
                in_=wv_d[dc * P:(dc + 1) * P, hp * P:(hp + 1) * P]),
                inc="att", inc_k=16)
        ld_w = C["att"]

        vev_prev = norm_v_prev
        sin_prev = norm_v_prev
        for which, wsl, raws, dst in (
                (0, A2["wqs"], qraws, A2["q2r"]), (1, A2["wks"], kraws, A2["k2r"])):
            for blk in range(NBLK):
                par = blk % 2
                for dc in range(NDC):
                    pl.op("tensor", lambda e, wsl=wsl, dc=dc, blk=blk: e.matmul(
                        pb_[4][:, :], wsl[:, dc, :],
                        xnT[:, dc, blk * 512:(blk + 1) * 512],
                        start=(dc == 0), stop=(dc == NDC - 1)),
                        waits=W(("att", ld_w),
                                ("ch_vector", vev_prev if dc == 0 else 0),
                                ("ch_scalar", bank45_free[0] if dc == 0 else 0)),
                        inc="pe" if dc == NDC - 1 else None)
                pe_q = C["pe"]
                vev = pl.ch("vector", lambda e, raws=raws, par=par: e.tensor_copy(
                    raws[par][:, :], pb_[4][:, :]), waits=W(("pe", pe_q)))
                vev_prev = vev
                pp = pl.op("tensor", lambda e, raws=raws, par=par: e.matmul(
                    pb_[5][:, :], p128_s[:, :], raws[par][:, :],
                    start=True, stop=True),
                    waits=W(("ch_vector", vev), ("ldq", g["ld0"]),
                            ("ch_vector", sin_prev),
                            ("ch_scalar", bank45_free[0])), inc="pe")
                sin_prev = pl.ch("vector", lambda e, blk=blk, par=par: e.tensor_mul(
                    rts[par][:, :], A2["sinT"][:, blk * 512:(blk + 1) * 512],
                    pb_[5][:, :]), waits=W(("pe", pp), ("ldq", ld_cs)))
                pl.ch("vector", lambda e, blk=blk, par=par, raws=raws: e.tensor_mul(
                    rts[1 - par][:, :], A2["cosT"][:, blk * 512:(blk + 1) * 512],
                    raws[par][:, :]))
                pl.ch("vector", lambda e, blk=blk, par=par, dst=dst: e.tensor_add(
                    dst[:, blk * 512:(blk + 1) * 512], rts[par][:, :],
                    rts[1 - par][:, :]))
        for j in range(NT):
            for dc in range(NDC):
                pl.op("tensor", lambda e, j=j, dc=dc: e.matmul(
                    pb_[4][:, 0:P], xnT[:, dc, j * P:(j + 1) * P],
                    A2["wvs"][:, dc, :],
                    start=(dc == 0), stop=(dc == NDC - 1)),
                    waits=W(("att", ld_w),
                            ("ch_vector", C["ch_vector"] if dc == 0 else 0)),
                    inc="pe" if dc == NDC - 1 else None)
            pe_v = C["pe"]
            pl.ch("vector", lambda e, j=j: e.tensor_copy(
                A2["v2"][:, j, 0:64], pb_[4][:, 0:64]),
                waits=W(("pe", pe_v), ("ch_vector", v_v2ones)))
            pl.ch("vector", lambda e, j=j: e.tensor_copy(
                A2["v2"][:, j, 66:130], pb_[4][:, 64:128]))
        v_gen = C["ch_vector"]
        gen_pe_prev = C["pe"]

        for J in range(NBLK):
            csw = [bank45_free[0]]
            ntile = 4 * J + 4
            tinfo = {}

            def emit_ctx(t, ntile=ntile, tinfo=tinfo, csw=csw):
                diag, par, sc_exp, pool_c = tinfo[t]
                for head in range(2):
                    src = (ams[par] if diag else asb[par])[:, head * 512:(head + 1) * 512]
                    bank = (pb_[4], pb_[5])[head]
                    voff = 0 if head == 0 else 66
                    pl.op("tensor", lambda e, t=t, head=head, src=src, bank=bank,
                          voff=voff: e.matmul(
                        bank[0:65, :], A2["v2"][:, t, voff:voff + 65], src,
                        start=(t == 0), stop=(t == ntile - 1)),
                        waits=W(("ch_vector", pool_c) if diag else
                                ("ch_scalar", sc_exp),
                                ("ch_scalar", csw[0] if t == 0 else 0)),
                        inc="pe" if head == 1 else None)
                if diag:
                    pool_read[par] = C["pe"]
                ctx_read[par] = C["pe"]

            for t in range(ntile):
                par = t % 2
                for head in range(2):
                    poff = 64 * head
                    bank = pbig4[:, par * 1024 + head * 512: par * 1024 + (head + 1) * 512]
                    pl.op("tensor", lambda e, hp=hp, t=t, J=J, head=head, bank=bank,
                          poff=poff: e.matmul(
                        bank,
                        A2["k2r"][poff:poff + 64, t * P:(t + 1) * P],
                        A2["q2r"][poff:poff + 64, J * 512:(J + 1) * 512],
                        start=True, stop=True, tile_position=(poff, 0)),
                        waits=W(("ch_scalar", exp_cons[par]),
                                ("ch_vector", v_gen)),
                        inc="pe" if head == 1 else None)
                pm = C["pe"]
                if t == 2 and carry_norm is not None:
                    carry_norm()()
                    carry_norm = None
                if t >= 2:
                    emit_ctx(t - 2)
                exp_cons[par] = pl.ch(
                    "scalar", lambda e, par=par: e.activation(
                        asb[par][:, :], pbig4[:, par * 1024:(par + 1) * 1024],
                        AF.Exp, scale=0.125),
                    waits=W(("pe", pm), ("pe", ctx_read[par])))
                sc_exp = exp_cons[par]
                diag = t >= 4 * J
                if diag:
                    di = t - 4 * J
                    pl.ch("vector", lambda e, par=par, di=di: e.tensor_mul(
                        ams[par][:, :], asb[par][:, :], A2["dmask"][:, di, :]),
                        waits=W(("ch_scalar", sc_exp), ("pe", pool_read[par]),
                                ("ldq", ld_cs)))
                tinfo[t] = (diag, par, sc_exp, C["ch_vector"])
            emit_ctx(ntile - 2)
            emit_ctx(ntile - 1)
            pe_ctx = C["pe"]

            craw_rd = craw_rd_v[0]
            s1 = pl.ch("scalar", lambda e: e.copy(
                A2["craw"][0:64, 0:512], pb_[4][0:64, :]),
                waits=W(("pe", pe_ctx), ("ch_vector", craw_rd)))
            pl.ch("scalar", lambda e: e.copy(
                A2["craw"][0:64, 512:1024], pb_[5][0:64, :]))
            s_free = pl.ch("scalar", lambda e: e.copy(
                A2["sums"][64:65, 0:1024], pbig[64:65, 0:1024]),
                waits=W(("ch_vector", sums_rd_v[0])))
            bank45_free[0] = s_free
            pl.ch("vector", lambda e: e.reciprocal(
                amsf[64:65, 0:512], A2["sums"][64:65, 0:512]),
                waits=W(("ch_scalar", s_free)))
            pl.ch("vector", lambda e: e.reciprocal(
                amsf[64:65, 512:1024], A2["sums"][64:65, 512:1024]))
            rcp_cast = pl.ch("vector", lambda e: e.tensor_copy(
                A2["rcpb"][64:65, 0:1024], amsf[64:65, 0:1024]),
                waits=W(("pe", norm_pmm_pe[0])))
            sums_rd_v[0] = rcp_cast

            def make_norm(rcp_cast=rcp_cast, hp=hp, J=J, s1=s1):
                def norm(csw_next=None):
                    nonlocal st_ctx, norm_v_prev
                    st_prev = st_ctx
                    tt = 0
                    for head in range(2):
                        pmm = pl.op("tensor", lambda e, head=head: e.matmul(
                            pb_[6][:, :], ones_pp[64:65, :],
                            A2["rcpb"][64:65, head * 512:(head + 1) * 512],
                            start=True, stop=True),
                            waits=W(("ch_vector", rcp_cast), ("ch_vector", tt)),
                            inc="pe")
                        tt = pl.ch("vector", lambda e, head=head: e.tensor_mul(
                            A2["ctxe"][0:64, head * 512:(head + 1) * 512],
                            A2["craw"][0:64, head * 512:(head + 1) * 512],
                            pb_[6][0:64, :]),
                            waits=W(("pe", pmm), ("attb", st_prev)))
                    v_norm = C["ch_vector"]
                    craw_rd_v[0] = v_norm
                    norm_pmm_pe[0] = C["pe"]
                    st_ctx = pl.op("sync", lambda e, hp=hp, J=J: e.dma_start(
                        out=ctxT_d[hp, 0:64, J * 512:(J + 1) * 512],
                        in_=A2["ctxe"][0:64, 0:512]),
                        waits=W(("ch_vector", v_norm)), inc="attb", inc_k=16)
                    st_ctx = pl.op("sync", lambda e, hp=hp, J=J: e.dma_start(
                        out=ctxT_d[hp, 64:128, J * 512:(J + 1) * 512],
                        in_=A2["ctxe"][0:64, 512:1024]),
                        inc="attb", inc_k=16)
                    norm_v_prev = v_norm
                    if csw_next is not None:
                        pass
                return norm

            carry_norm = make_norm
        if carry_norm is not None:
            carry_norm()()
            carry_norm = None
    ctx_stored = st_ctx

    off = OFF_A + 16 * 1024
    O2 = {}
    for name, shape, dtype in [
        ("woS", [P, NDC, D], BF16),
        ("ctxJ0", [P, NPAIR, 512], BF16),
        ("ctxJ1", [P, NPAIR, 512], BF16),
        ("ae0", [P, D], BF16),
        ("ae1", [P, D], BF16),
    ]:
        O2[name] = carve(off, shape, dtype)
        off += int(np.prod(shape[1:])) * mybir.dt.size(dtype)
    assert off <= OFF_B

    relO = [("pe", C["pe"]), ("ch_vector", C["ch_vector"]),
            ("ch_scalar", C["ch_scalar"]), ("ch_gpsimd", C["ch_gpsimd"])]
    for dc in range(NDC):
        pl.op("gpsimd", lambda e, dc=dc: e.dma_start(
            out=O2["woS"][:, dc, :], in_=wo_d[dc * P:(dc + 1) * P, :]),
            waits=relO, inc="att", inc_k=16)
    ld_wo = C["att"]
    ctxJ = [O2["ctxJ0"], O2["ctxJ1"]]
    aes = [O2["ae0"], O2["ae1"]]
    ctxJ_pe = [0, 0]
    ae_st = [0, 0]
    for J in range(NBLK):
        par = J % 2
        cjsem = "xa" if par == 0 else "xb"
        for pr in range(NPAIR):
            pl.op("sync", lambda e, J=J, pr=pr, par=par: e.dma_start(
                out=ctxJ[par][:, pr, :], in_=ctxT_d[pr, :, J * 512:(J + 1) * 512]),
                waits=relO + [("attb", ctx_stored), ("pe", ctxJ_pe[par])],
                inc=cjsem, inc_k=16)
        ld_cj = C[cjsem]
        for st in range(4):
            j = J * 4 + st
            for half in range(2):
                bank = pb_[half]
                for pr in range(NPAIR):
                    pl.op("tensor", lambda e, st=st, pr=pr, half=half, par=par,
                          bank=bank: e.matmul(
                        bank[:, :], ctxJ[par][:, pr, st * P:(st + 1) * P],
                        O2["woS"][:, pr, half * 512:(half + 1) * 512],
                        start=(pr == 0), stop=(pr == NPAIR - 1)),
                        waits=W(("att", ld_wo), (cjsem, ld_cj),
                                ("ch_vector", C["ch_vector"] if pr == 0 else 0)),
                        inc="pe" if pr == NPAIR - 1 else None)
            pe_o = C["pe"]
            apar = j % 2
            for half in range(2):
                pl.ch("vector", lambda e, half=half, apar=apar: e.tensor_copy(
                    aes[apar][:, half * 512:(half + 1) * 512], pb_[half][:, :]),
                    waits=W(("pe", pe_o), ("spillB", ae_st[apar])))
            v_ae = C["ch_vector"]
            pl.op("sync", lambda e, j=j, apar=apar: e.dma_start(
                out=attn_d[j], in_=aes[apar][:, :]),
                waits=W(("ch_vector", v_ae)), inc="spillB", inc_k=16)
            ae_st[apar] = C["spillB"]
        ctxJ_pe[par] = C["pe"]


def build_moe(pl, nc, g):
    C = pl.cnt
    carve = g["carve"]
    pb_ = g["pb_"]; pbig = g["pbig"]
    xe_d = g["xe_d"]; oe_d = g["oe_d"]
    f1w_d = g["f1w_d"]; f1b_d = g["f1b_d"]; f2w_d = g["f2w_d"]; f2b_d = g["f2b_d"]
    ones_bf = g["ones_bf"]
    gq_scat = g["gq_scat"]

    f2ws = [carve(OFF_A, [P, EH // P, D], BF16),
            carve(OFF_A + 32 * 1024, [P, EH // P, D], BF16)]
    hTs = [carve(OFF_B, [P, EH // P, 512], BF16),
           carve(OFF_B + 16 * 1024, [P, EH // P, 512], BF16)]
    f1w = carve(OFF_B + 32 * 1024, [P, NDC, EH], BF16)
    M2 = {"xeT": carve(OFF_C, [P, NDC, CLOC], BF16)}
    sbt = nc.alloc_sbuf_tensor
    oes = [sbt("moe_oe0", [P, D], BF16), sbt("moe_oe1", [P, D], BF16)]
    f1b_s = sbt("moe_f1b", [P, EH // P], F32)
    f2b_s = [sbt("moe_f2b0", [1, D], BF16), sbt("moe_f2b1", [1, D], BF16)]

    relM = [("pe", C["pe"]), ("ch_vector", C["ch_vector"]),
            ("ch_scalar", C["ch_scalar"]), ("ch_gpsimd", C["ch_gpsimd"]),
            ("gq", gq_scat)]
    NTH = CLOC // 512
    fc1_done = {}
    gelu_done = {}
    fc2_done = {}
    f1b_used = 0
    oe_st = [0, 0]
    st_oe = 0
    fc1bank_sc = [0, 0]

    def fc1_third(ex, th):
        gth = ex * NTH + th
        hb = hTs[gth % 2]
        ld_xe = ld_xe_e[ex]
        ld_f1 = ld_f1_e[ex]
        for ht in range(EH // P):
            bank = pbig[:, (ht % 2) * 512:(ht % 2 + 1) * 512]
            for dc in range(NDC):
                pl.op("tensor", lambda e, ht=ht, dc=dc, th=th, bank=bank: e.matmul(
                    bank, f1w[:, dc, ht * P:(ht + 1) * P],
                    M2["xeT"][:, dc, th * 512:(th + 1) * 512],
                    start=(dc == 0), stop=(dc == NDC - 1)),
                    waits=W(("att", ld_xe), ("moew", ld_f1),
                            ("ch_scalar", fc1bank_sc[ht % 2] if dc == 0 else 0)),
                    inc="pe" if dc == NDC - 1 else None)
            pe_h = C["pe"]
            fc1bank_sc[ht % 2] = pl.ch(
                "scalar", lambda e, ht=ht, bank=bank, hb=hb: e.activation(
                    hb[:, ht, :], bank, AF.Gelu, bias=f1b_s[:, ht:ht + 1]),
                waits=W(("pe", pe_h), ("moew", f1b_used),
                        ("pe", fc2_done.get(gth - 2, 0))))
        fc1_done[gth] = C["pe"]
        gelu_done[gth] = C["ch_scalar"]

    def fc2_third(ex, th):
        nonlocal st_oe
        gth = ex * NTH + th
        hb = hTs[gth % 2]
        f2w = f2ws[ex % 2]
        ld_f2 = ld_f2_e[ex]
        for st in range(4):
            for half in range(2):
                bank = pb_[half]
                for hc in range(EH // P):
                    pl.op("tensor", lambda e, st=st, hc=hc, half=half, bank=bank,
                          hb=hb: e.matmul(
                        bank[:, :], hb[:, hc, st * P:(st + 1) * P],
                        f2w[:, hc, half * 512:(half + 1) * 512],
                        start=(hc == 0), stop=False),
                        waits=W(("ch_scalar", gelu_done[gth]), ("moew", ld_f2),
                                ("ch_vector", oe_free_v[0] if hc == 0 else 0)),
                        inc=None)
                pl.op("tensor", lambda e, half=half, bank=bank, ex=ex: e.matmul(
                    bank[:, :], ones_bf[:, :],
                    f2b_s[ex % 2][:, half * 512:(half + 1) * 512],
                    start=False, stop=True),
                    waits=W(("moew", ld_f2)),
                    inc="pe" if half == 1 else None)
            pe_o = C["pe"]
            par = st % 2
            for half in range(2):
                pl.ch("vector", lambda e, half=half, par=par: e.tensor_copy(
                    oes[par][:, half * 512:(half + 1) * 512], pb_[half][:, :]),
                    waits=W(("pe", pe_o), ("spillC", oe_st[par])))
            oe_free_v[0] = C["ch_vector"]
            st_oe = pl.op("sync", lambda e, ex=ex, th=th, st=st, par=par: e.dma_start(
                out=oe_d[ex * CLOC + th * 512 + st * P:
                         ex * CLOC + th * 512 + (st + 1) * P, :],
                in_=oes[par][:, :]),
                waits=W(("ch_vector", oe_free_v[0])), inc="spillC", inc_k=16)
            oe_st[par] = st_oe
        fc2_done[gth] = C["pe"]

    oe_free_v = [0]
    ld_xe_e = {}
    ld_f1_e = {}
    ld_f2_e = {}

    def emit_loads(ex):
        nonlocal f1b_used
        prev_fc1 = fc1_done.get(ex * NTH - 1, 0)
        for dc in range(NDC):
            pl.op("sync", lambda e, ex=ex, dc=dc: e.dma_start_transpose(
                M2["xeT"][:, dc, :],
                xe_d[ex * CLOC:(ex + 1) * CLOC, dc * P:(dc + 1) * P]),
                waits=relM + [("pe", prev_fc1)], inc="att", inc_k=16)
        ld_xe_e[ex] = C["att"]
        for dc in range(NDC):
            pl.op("gpsimd", lambda e, ex=ex, dc=dc: e.dma_start(
                out=f1w[:, dc, :], in_=f1w_d[ex, dc * P:(dc + 1) * P, :]),
                waits=relM + [("pe", prev_fc1)], inc="moew", inc_k=16)
        pl.op("gpsimd", lambda e, ex=ex: e.dma_start(
            out=f1b_s[:, :], in_=f1b_d[ex]),
            waits=relM + [("ch_scalar", gelu_done.get(ex * NTH - 1, 0))],
            inc="moew", inc_k=16)
        ld_f1_e[ex] = C["moew"]
        f1b_used = ld_f1_e[ex]
        prev_fc2 = fc2_done.get((ex - 2) * NTH + NTH - 1, 0)
        for hc in range(EH // P):
            pl.op("gpsimd", lambda e, ex=ex, hc=hc: e.dma_start(
                out=f2ws[ex % 2][:, hc, :], in_=f2w_d[ex, hc * P:(hc + 1) * P, :]),
                waits=relM + [("pe", prev_fc2)], inc="moew", inc_k=16)
        pl.op("gpsimd", lambda e, ex=ex: e.dma_start(
            out=f2b_s[ex % 2][:, :], in_=f2b_d[ex]),
            waits=relM + [("pe", fc2_done.get((ex - 2) * NTH + NTH - 1, 0))],
            inc="moew", inc_k=16)
        ld_f2_e[ex] = C["moew"]

    emit_loads(0)
    fc1_third(0, 0)
    pend = [(0, 0)]
    for ex in range(NE):
        for th in range(NTH):
            if (ex, th) == (0, 0):
                continue
            fc1_third(ex, th)
            fc2_third(*pend.pop(0))
            pend.append((ex, th))
            if th == NTH - 1 and ex + 1 < NE:
                emit_loads(ex + 1)
    fc2_third(*pend.pop(0))
    g["oe_st"] = st_oe


def build_p3(pl, nc, g):
    C = pl.cnt
    region = g["region"]
    pb_ = g["pb_"]
    xnT_d = g["xnT_d"]; attn_d = g["attn_d"]; x_d = g["x_d"]; out_d = g["out_d"]
    s1w_d = g["s1w_d"]; s2w_d = g["s2w_d"]; oe_d = g["oe_d"]; h1_d = g["h1_d"]
    ones_bf = g["ones_bf"]; s2b_row = g["s2b_row"]; s1b_s = g["s1b_s"]
    bo_bc = g["bo_bc"]; vch_bo = g["vch_bo"]
    stab0 = g["stab0"]; stab1 = g["stab1"]; w0f = g["w0f"]; w1f = g["w1f"]
    spill_xnT = g["spill_xnT"]; spill_attn = g["spill_attn"]
    gq0 = g["gq0"]; ld0 = g["ld0"]
    oe_done = g.get("oe_st", 0)
    p2_pe = g["p2_pe"]; p2_v = g["p2_v"]; p2_s = g["p2_s"]; p2_gq = g["p2_gq"]

    spill_h1 = g["spill_h1"]
    P3b, _ = region(OFF_B, [("s2w", [P, NHT, D], BF16)])
    P3h, _ = region(OFF_A, [
        ("h1b", [P, NHT, SB3], BF16),
        ("xa0", [P, D], F32),
        ("xa1", [P, D], F32),
        ("at0", [P, D], BF16),
        ("at1", [P, D], BF16),
        ("oa0", [P, D], F32),
        ("oa1", [P, D], F32),
        ("oa2", [P, D], F32),
    ])
    for hcx in range(NHT):
        pl.op("gpsimd", lambda e, hcx=hcx: e.dma_start(
            out=P3b["s2w"][:, hcx, :], in_=s2w_d[hcx * P:(hcx + 1) * P, :]),
            waits=W(("spillH", spill_h1), ("pe", p2_pe), ("gq", p2_gq)),
            inc="moew", inc_k=16)
    s2w_ld = C["moew"]

    xas = [P3h["xa0"], P3h["xa1"]]
    ats = [P3h["at0"], P3h["at1"]]
    g0s = g["g0_t"]
    g1s = g["g1_t"]
    oas = [P3h["oa0"], P3h["oa1"], P3h["oa2"]]

    for t_ in g["g0_t"] + g["g1_t"]:
        pl.ch("vector", lambda e, t_=t_: e.memset(t_[:, :], 0.0),
              waits=W(("spillH", spill_h1)))
    v_gz = C["ch_vector"]

    NJT = SB3 // P
    asm_consumed = [0, 0]
    oa_store = [0, 0, 0]
    peA_mark = p2_pe
    peB_mark = p2_pe
    st_total = 0
    oi = 0
    bank23_free_v = 0
    for sbi in range(NSB3):
        hsem = "wa" if sbi % 2 == 0 else "wb"
        for ht in range(NHT // 2):
            pl.op("sync", lambda e, sbi=sbi, ht=ht: e.dma_start(
                out=P3h["h1b"][:, ht, :], in_=h1_d[sbi, ht]),
                waits=W(("spillH", spill_h1), ("pe", peA_mark)),
                inc=hsem, inc_k=16)
        ld_h1A = C[hsem]
        for ht in range(NHT // 2, NHT):
            pl.op("sync", lambda e, sbi=sbi, ht=ht: e.dma_start(
                out=P3h["h1b"][:, ht, :], in_=h1_d[sbi, ht]),
                waits=W(("spillH", spill_h1), ("pe", peB_mark)),
                inc=hsem, inc_k=16)
        ld_h1B = C[hsem]
        for jt in range(NJT):
            j = sbi * NJT + jt
            apar = j % 2
            pl.op("sync", lambda e, j=j, apar=apar: e.dma_start(
                out=xas[apar][:, :], in_=x_d[j * P:(j + 1) * P, :]),
                waits=W(("ch_vector", asm_consumed[apar]), ("spillH", spill_h1),
                        ("pe", p2_pe)),
                inc="asm", inc_k=16)
            pl.op("sync", lambda e, j=j, apar=apar: e.dma_start(
                out=ats[apar][:, :], in_=attn_d[j]),
                waits=W(("spillB", spill_attn), ("pe", p2_pe)),
                inc="asm", inc_k=16)
            asm_ld = C["asm"]
            gq_g = 0
            if STAGE >= 3:
                pl.op("gpsimd", lambda e, j=j, apar=apar: e.indirect_dma_start(
                    g0s[apar][:, :], None,
                    oe_d[:, :], IndirectOffsetOnAxis(stab0[:, j:j + 1], 0),
                    bounds_check=pl.breg(e), oob_is_err=False),
                    waits=W(("ch_vector", max(asm_consumed[apar], v_gz)),
                            ("spillC", oe_done)),
                    inc="gq", inc_k=16)
                pl.op("gpsimd", lambda e, j=j, apar=apar: e.indirect_dma_start(
                    g1s[apar][:, :], None,
                    oe_d[:, :], IndirectOffsetOnAxis(stab1[:, j:j + 1], 0),
                    bounds_check=pl.breg(e), oob_is_err=False),
                    inc="gq", inc_k=16)
                gq_g = C["gq"]

            for hcx in range(NHT):
                for half in range(2):
                    isA_last = (jt == NJT - 1 and hcx == NHT // 2 - 1 and half == 1)
                    pl.op("tensor", lambda e, jt=jt, hcx=hcx, half=half: e.matmul(
                        pb_[2 + half][:, :], P3h["h1b"][:, hcx, jt * P:(jt + 1) * P],
                        P3b["s2w"][:, hcx, half * 512:(half + 1) * 512],
                        start=(hcx == 0), stop=False),
                        waits=W(("moew", s2w_ld),
                                (hsem, ld_h1A if hcx == 0 else
                                 (ld_h1B if hcx == NHT // 2 else 0)),
                                ("ch_vector", bank23_free_v if hcx == 0 else 0)),
                        inc="pe" if isA_last else None)
                    if isA_last:
                        peA_mark = C["pe"]
            for half in range(2):
                pl.op("tensor", lambda e, half=half: e.matmul(
                    pb_[2 + half][:, :], ones_bf[:, :],
                    s2b_row[:, half * 512:(half + 1) * 512], start=False, stop=True),
                    waits=W(("gq", gq0)), inc="pe")
            pe_sh = C["pe"]
            if jt == NJT - 1:
                peB_mark = pe_sh

            dst = oas[oi % 3]; oi += 1
            for half in range(2):
                pl.ch("vector", lambda e, half=half, apar=apar, dst=dst: e.tensor_add(
                    dst[:, half * 512:(half + 1) * 512], pb_[2 + half][:, :],
                    xas[apar][:, half * 512:(half + 1) * 512]),
                    waits=W(("pe", pe_sh), ("asm", asm_ld),
                            ("stq", oa_store[oas.index(dst)])))
            bank23_free_v = C["ch_vector"]
            cur = dst
            nxt = oas[oi % 3]; oi += 1
            pl.ch("vector", lambda e, cur=cur, nxt=nxt: e.tensor_add(
                nxt[:, :], cur[:, :], bo_bc[:, :]),
                waits=W(("ch_vector", vch_bo), ("stq", oa_store[oas.index(nxt)])))
            cur = nxt
            if STAGE >= 2:
                nxt = oas[oi % 3]; oi += 1
                pl.ch("vector", lambda e, cur=cur, nxt=nxt, apar=apar: e.tensor_add(
                    nxt[:, :], cur[:, :], ats[apar][:, :]),
                    waits=W(("stq", oa_store[oas.index(nxt)])))
                cur = nxt
            if STAGE >= 3:
                nxt = oas[oi % 3]; oi += 1
                pl.ch("vector", lambda e, cur=cur, nxt=nxt, apar=apar, j=j: e.scalar_tensor_tensor(
                    nxt[:, :], g0s[apar][:, :], w0f[:, j:j + 1], cur[:, :],
                    ALU.mult, ALU.add),
                    waits=W(("gq", gq_g), ("stq", oa_store[oas.index(nxt)])))
                cur = nxt
                nxt = oas[oi % 3]; oi += 1
                pl.ch("vector", lambda e, cur=cur, nxt=nxt, apar=apar, j=j: e.scalar_tensor_tensor(
                    nxt[:, :], g1s[apar][:, :], w1f[:, j:j + 1], cur[:, :],
                    ALU.mult, ALU.add),
                    waits=W(("stq", oa_store[oas.index(nxt)])))
                cur = nxt
            v_asm = C["ch_vector"]
            asm_consumed[apar] = v_asm
            st_total = pl.op("sync", lambda e, j=j, cur=cur: e.dma_start(
                out=out_d[j * P:(j + 1) * P, :], in_=cur[:, :]),
                waits=W(("ch_vector", v_asm)), inc="stq", inc_k=16)
            oa_store[oas.index(cur)] = st_total

    pl.op("sync", lambda e: e.nop(), waits=W(("stq", st_total)))


def kernel(L0_attn_x, L0_attn_Wq, L0_attn_Wk, L0_attn_Wv, L0_attn_Wo, L0_attn_bo,
           L0_ffn_proj_w, L0_ffn_proj_b, L0_ffn_expert_bias,
           L0_ffn_fc1_w, L0_ffn_fc1_b, L0_ffn_fc2_w, L0_ffn_fc2_b,
           L0_ffn_sh1_w, L0_ffn_sh1_b, L0_ffn_sh2_w, L0_ffn_sh2_b):
    x = np.ascontiguousarray(np.asarray(L0_attn_x, np.float32))

    idx = np.zeros(D, np.int64)
    for h in range(16):
        for half in range(2):
            for i in range(32):
                idx[h * 64 + half * 32 + i] = h * 64 + 2 * i + half
    wq = np.ascontiguousarray(np.asarray(L0_attn_Wq, np.float32)[:, idx])
    wk = np.ascontiguousarray(np.asarray(L0_attn_Wk, np.float32)[:, idx])
    wv = np.ascontiguousarray(np.asarray(L0_attn_Wv, np.float32))
    wo = np.ascontiguousarray(np.asarray(L0_attn_Wo, np.float32))
    bo = np.asarray(L0_attn_bo, np.float32).reshape(1, D)
    pwt = np.ascontiguousarray(np.asarray(L0_ffn_proj_w, np.float32).T)
    pbv = np.asarray(L0_ffn_proj_b, np.float32).reshape(1, NE)
    eb = np.asarray(L0_ffn_expert_bias, np.float32).reshape(1, NE)
    f1w = np.ascontiguousarray(np.asarray(L0_ffn_fc1_w, np.float32).transpose(0, 2, 1))
    f1b = np.ascontiguousarray(
        np.asarray(L0_ffn_fc1_b, np.float32).reshape(NE, EH // P, P).transpose(0, 2, 1))
    f2w = np.ascontiguousarray(np.asarray(L0_ffn_fc2_w, np.float32).transpose(0, 2, 1))
    f2b = np.ascontiguousarray(np.asarray(L0_ffn_fc2_b, np.float32).reshape(NE, 1, D))
    s1w = np.ascontiguousarray(np.asarray(L0_ffn_sh1_w, np.float32).T)
    s1b = np.ascontiguousarray(np.asarray(L0_ffn_sh1_b, np.float32).reshape(NHT, P).T)
    s2w = np.ascontiguousarray(np.asarray(L0_ffn_sh2_w, np.float32).T)
    s2b = np.asarray(L0_ffn_sh2_b, np.float32).reshape(1, D)

    nc = build()
    shared = dict(wq=wq, wk=wk, wv=wv, wo=wo, bo=bo, pwt=pwt, pb=pbv, eb=eb,
                  f1w=f1w, f1b=f1b, f2w=f2w, f2b=f2b,
                  s1w=s1w, s1b=s1b, s2w=s2w, s2b=s2b)
    in_maps = [dict(shared, x=np.ascontiguousarray(x[c])) for c in range(NCORES)]
    res = run_bass_kernel_spmd(nc, in_maps, core_ids=list(range(NCORES)))
    out = np.stack([res.results[c]["out"] for c in range(NCORES)], axis=0)
    return out.astype(np.float32)


# version 26 (final)
# speedup vs baseline: 1.1608x; 1.0027x over previous
import numpy as np
import ml_dtypes
import concourse.bass as bass
import concourse.mybir as mybir
from concourse.bass import IndirectOffsetOnAxis
from concourse.bass_utils import run_bass_kernel_spmd


F32 = mybir.dt.float32
BF16 = mybir.dt.bfloat16
U32 = mybir.dt.uint32
AF = mybir.ActivationFunctionType
ALU = mybir.AluOpType
AX = mybir.AxisListType

P = 128
D = 1024
S = 4096
NT = S // P
NDC = D // P
NPAIR = 8
DH = 64
NE = 8
EH = 2048
SHID = 4096
CAP = 10240.0
CLOC = 1536
NSLOT = NE * CLOC
HUGE = 1 << 20
NCORES = 8
SB3 = 512
NSB3 = S // SB3
NHT = SHID // P
BISECT_ITERS = 18

STAGE = 3
DEFER_NORM = True

bf16 = ml_dtypes.bfloat16

OFF_A = 0
OFF_B = 64 * 1024
OFF_C = 128 * 1024
ARENA = 160 * 1024


class Plan:
    ENGS = ("sync", "gpsimd", "tensor", "vector", "scalar")

    def __init__(self, nc):
        self.nc = nc
        self.q = {e: [] for e in self.ENGS}
        self.sems = {}
        self.cnt = {}
        self.floors = {}
        self.bounds_reg = [None]

    def breg(self, engine):
        if self.bounds_reg[0] is None:
            self.bounds_reg[0] = engine.to_reg(NSLOT - 1)
        return self.bounds_reg[0]

    def sem(self, name):
        self.sems[name] = self.nc.alloc_semaphore(name)
        self.cnt[name] = 0
        return name

    def op(self, eng, fn, waits=(), inc=None, inc_k=1):
        w = tuple((s, v) for (s, v) in waits if v > 0)
        if inc is not None:
            self.cnt[inc] += inc_k
        after = self.cnt[inc] if inc else None
        sems = self.sems

        floors = self.floors.setdefault(eng, {})

        def emit(engine, fn=fn, w=w, inc=inc, inc_k=inc_k, floors=floors):
            live = [(s, v) for (s, v) in w if floors.get(s, 0) < v]
            for (s, v) in live:
                floors[s] = v
            for (s, v) in live[1:]:
                engine.wait_ge(sems[s], v)
            ins = fn(engine)
            if live:
                ins._wait_ge(sems[live[0][0]], live[0][1])
            if inc is not None:
                ins.then_inc(sems[inc], inc_k)

        self.q[eng].append(emit)
        return after

    def ch(self, eng, fn, waits=()):
        name = "ch_" + eng
        prev = self.cnt[name]
        return self.op(eng, fn, waits=list(waits) + [(name, prev)], inc=name)

    def finalize(self):
        nc = self.nc
        with nc.Block() as block:
            for eng in self.ENGS:
                fns = self.q[eng]
                if not fns:
                    continue

                def body(engine, fns=fns):
                    for f in fns:
                        f(engine)

                getattr(block, eng)(body)


def W(*pairs):
    return list(pairs)


def host_constants():
    c = {}
    c["ident_bf"] = np.eye(P, dtype=np.float32).astype(bf16)
    perm = np.zeros((P, P), np.float32)
    for h in range(2):
        for i in range(32):
            perm[h * 64 + i, h * 64 + 32 + i] = 1.0
            perm[h * 64 + 32 + i, h * 64 + i] = 1.0
    c["p128_bf"] = perm.astype(bf16)
    freqs = 1.0 / (10000.0 ** (np.arange(32, dtype=np.float64) / 32.0))
    ang = np.arange(S, dtype=np.float64)[None, :] * freqs[:, None]
    Cc = np.zeros((P, S), np.float64)
    Sg = np.zeros((P, S), np.float64)
    for l in range(P):
        i = l % 32
        Cc[l] = np.cos(ang[i])
        Sg[l] = np.sin(ang[i]) * (-1.0 if (l % 64) < 32 else 1.0)
    c["cosT"] = Cc.astype(np.float32).astype(bf16)
    c["sinT"] = Sg.astype(np.float32).astype(bf16)
    c["lstrict_bf"] = np.tril(np.ones((P, P), np.float32), -1).astype(bf16)
    dm = np.zeros((4, P, 1024), np.float32)
    for i in range(4):
        for p in range(P):
            keep = (np.arange(512) >= i * P + p).astype(np.float32)
            dm[i, p, 0:512] = keep
            dm[i, p, 512:1024] = keep
    c["dmask"] = dm.astype(bf16)
    c["huge_f"] = (np.float32(HUGE) + np.arange(P, dtype=np.float32))[:, None] * np.ones(
        (1, NT), np.float32)
    c["zer_bf"] = np.zeros((P, D), np.float32).astype(bf16)
    return c


def build():
    nc = bass.Bass("TRN2", target_bir_lowering=False, detect_race_conditions=False)
    pl = Plan(nc)
    C = pl.cnt

    x_d = nc.dram_tensor("x", [S, D], F32, kind="ExternalInput")
    wq_d = nc.dram_tensor("wq", [D, D], F32, kind="ExternalInput")
    wk_d = nc.dram_tensor("wk", [D, D], F32, kind="ExternalInput")
    wv_d = nc.dram_tensor("wv", [D, D], F32, kind="ExternalInput")
    wo_d = nc.dram_tensor("wo", [D, D], F32, kind="ExternalInput")
    bo_d = nc.dram_tensor("bo", [1, D], F32, kind="ExternalInput")
    pwt_d = nc.dram_tensor("pwt", [D, NE], F32, kind="ExternalInput")
    pb_d = nc.dram_tensor("pb", [1, NE], F32, kind="ExternalInput")
    eb_d = nc.dram_tensor("eb", [1, NE], F32, kind="ExternalInput")
    f1w_d = nc.dram_tensor("f1w", [NE, D, EH], F32, kind="ExternalInput")
    f1b_d = nc.dram_tensor("f1b", [NE, P, EH // P], F32, kind="ExternalInput")
    f2w_d = nc.dram_tensor("f2w", [NE, EH, D], F32, kind="ExternalInput")
    f2b_d = nc.dram_tensor("f2b", [NE, 1, D], F32, kind="ExternalInput")
    s1w_d = nc.dram_tensor("s1w", [D, SHID], F32, kind="ExternalInput")
    s1b_d = nc.dram_tensor("s1b", [P, NHT], F32, kind="ExternalInput")
    s2w_d = nc.dram_tensor("s2w", [SHID, D], F32, kind="ExternalInput")
    s2b_d = nc.dram_tensor("s2b", [1, D], F32, kind="ExternalInput")
    out_d = nc.dram_tensor("out", [S, D], F32, kind="ExternalOutput")

    xn_d = nc.dram_tensor("xn_d", [NT, P, D], BF16)
    xnT_d = nc.dram_tensor("xnT_d", [NDC, P, S], BF16)
    attn_d = nc.dram_tensor("attn_d", [NT, P, D], BF16)
    ctxT_d = nc.dram_tensor("ctxT_d", [NPAIR, P, S], BF16)
    h1_d = nc.dram_tensor("h1_d", [NSB3, NHT, P, SB3], BF16)
    xe_d = nc.dram_tensor("xe_d", [NSLOT, D], BF16)
    oe_d = nc.dram_tensor("oe_d", [NSLOT, D], BF16)
    rt_d = nc.dram_tensor("rt_d", [4, S], F32)
    rtall_d = nc.dram_tensor("rtall_d", [4 * NCORES, S], F32)

    hcst = host_constants()
    ident_c = nc.inline_tensor(hcst["ident_bf"], "ident_c")
    p128_c = nc.inline_tensor(hcst["p128_bf"], "p128_c")
    cos_c = nc.inline_tensor(hcst["cosT"], "cos_c")
    sin_c = nc.inline_tensor(hcst["sinT"], "sin_c")
    lstrict_c = nc.inline_tensor(hcst["lstrict_bf"], "lstrict_c")
    huge_c = nc.inline_tensor(hcst["huge_f"], "huge_c")
    dmask_c = nc.inline_tensor(hcst["dmask"].reshape(4 * P, 1024), "dmask_c")
    zer_c = nc.inline_tensor(hcst["zer_bf"], "zer_c")

    for s in ("ch_vector", "ch_scalar", "ch_gpsimd", "pe",
              "ldq", "gq", "stq", "cc", "xa", "xb", "wa", "wb",
              "spillA", "spillB", "spillC", "spillH", "moew", "asm",
              "att", "attb"):
        pl.sem(s)

    sbt = nc.alloc_sbuf_tensor
    ident_s = sbt("ident_s", [P, P], BF16)
    p128_s = sbt("p128_s", [P, P], BF16)
    lstrict_s = sbt("lstrict_s", [P, P], BF16)
    ones_bf = sbt("ones_bf", [1, P], BF16)
    ones_f = sbt("ones_f", [1, P], F32)
    onesc_bf = sbt("onesc_bf", [P, 1], BF16)
    ones_pp = sbt("ones_pp", [P, P], BF16)
    onesc_f = sbt("onesc_f", [P, 1], F32)
    bo_row = sbt("bo_row", [1, D], F32)
    bo_bc = sbt("bo_bc", [P, D], F32)
    s2b_row = sbt("s2b_row", [1, D], BF16)
    s1b_s = sbt("s1b_s", [P, NHT], F32)
    stab0 = sbt("stab0", [P, NT], U32)
    stab1 = sbt("stab1", [P, NT], U32)
    w0f = sbt("w0f", [P, NT], F32)
    w1f = sbt("w1f", [P, NT], F32)
    rl_t = [sbt("rl_t0", [P, D], BF16), sbt("rl_t1", [P, D], BF16)]
    g0_t = [sbt("g0_t0", [P, D], BF16), sbt("g0_t1", [P, D], BF16)]
    g1_t = [sbt("g1_t0", [P, D], BF16), sbt("g1_t1", [P, D], BF16)]

    arena = sbt("arena", [P, ARENA], mybir.dt.uint8)

    def carve(off_bytes, shape, dtype):
        nbytes = int(np.prod(shape[1:])) * mybir.dt.size(dtype)
        assert off_bytes + nbytes <= ARENA, (off_bytes, nbytes)
        v = arena[:, off_bytes:off_bytes + nbytes].bitcast(dtype)
        if shape[0] != P:
            v = v[0:shape[0]]
        if len(shape) > 2:
            pat = " ".join(f"d{i}" for i in range(1, len(shape)))
            v = v.rearrange(f"p ({pat}) -> p {pat}",
                            **{f"d{i}": shape[i] for i in range(1, len(shape))})
        return v

    def region(base, specs):
        out = {}
        off = base
        for name, shape, dtype in specs:
            out[name] = carve(off, shape, dtype)
            off += int(np.prod(shape[1:])) * mybir.dt.size(dtype)
        return out, off

    pbig4 = nc.alloc_psum_tensor("pbig4", [P, 2048], F32)
    pbig = nc.alloc_psum_tensor("pbig", [P, 1536], F32)
    pb_ = [pbig4[:, i * 512:(i + 1) * 512] for i in range(4)]
    pb_.extend([pbig[:, 0:512], pbig[:, 512:1024], pbig[:, 1024:1536]])
    pbt = nc.alloc_psum_tensor("pbt", [P, P], BF16)


    pl.op("sync", lambda e: e.dma_start(out=ident_s[:, :], in_=ident_c[:, :]), inc="ldq", inc_k=16)
    pl.op("sync", lambda e: e.dma_start(out=p128_s[:, :], in_=p128_c[:, :]), inc="ldq", inc_k=16)
    pl.op("sync", lambda e: e.dma_start(out=lstrict_s[:, :], in_=lstrict_c[:, :]), inc="ldq", inc_k=16)
    pl.op("sync", lambda e: e.dma_start(out=bo_row[:, :], in_=bo_d[:, :]), inc="ldq", inc_k=16)
    pl.op("sync", lambda e: e.dma_start(out=s1b_s[:, :], in_=s1b_d[:, :]), inc="ldq", inc_k=16)
    pl.op("gpsimd", lambda e: e.dma_start(out=s2b_row[:, :], in_=s2b_d[:, :]), inc="gq", inc_k=16)
    pl.ch("vector", lambda e: e.memset(ones_bf[:, :], 1.0))
    pl.ch("vector", lambda e: e.memset(ones_f[:, :], 1.0))
    pl.ch("vector", lambda e: e.memset(onesc_bf[:, :], 1.0))
    pl.ch("vector", lambda e: e.memset(ones_pp[:, :], 1.0))
    pl.ch("vector", lambda e: e.memset(onesc_f[:, :], 1.0))
    ld0 = C["ldq"]
    gq0 = C["gq"]
    vch_c = C["ch_vector"]
    v_bo = 0
    for h in range(2):
        pe_bo = pl.op("tensor", lambda e, h=h: e.matmul(
            pb_[6][:, :], ones_f[:, :], bo_row[:, h * 512:(h + 1) * 512],
            start=True, stop=True),
            waits=W(("ldq", ld0), ("ch_vector", vch_c), ("ch_vector", v_bo)), inc="pe")
        v_bo = pl.ch("vector", lambda e, h=h: e.tensor_copy(
            bo_bc[:, h * 512:(h + 1) * 512], pb_[6][:, :]), waits=W(("pe", pe_bo)))
    vch_bo = C["ch_vector"]

    P0c, p0c_end = region(OFF_C, [
        ("xt0", [P, D], F32),
        ("xt1", [P, D], F32),
        ("sqs", [P, D], F32),
        ("lnt", [P, 16], F32),
        ("xn0", [P, D], BF16),
        ("xn1", [P, D], BF16),
    ])
    xnt_bufs = [P0c["xn0"], P0c["xn1"]]
    xts = [P0c["xt0"], P0c["xt1"]]
    lnt = P0c["lnt"]
    xnT = carve(OFF_B, [P, NDC, S], BF16)

    xpar = ["xa", "xb"]
    vdone = [0, 0]
    sdone = [0, 0]
    xn_spill = [0, 0]
    tr_pe = [0, 0]
    vtr = 0
    for j in range(NT):
        par = j % 2
        pl.op("sync", lambda e, j=j, par=par: e.dma_start(
            out=xts[par][:, :], in_=x_d[j * P:(j + 1) * P, :]),
            waits=W(("ch_vector", vdone[par]), ("ch_scalar", sdone[par])),
            inc=xpar[par], inc_k=16)
        ldv = C[xpar[par]]
        ssq = pl.ch("scalar", lambda e, par=par: e.activation(
            P0c["sqs"][:, :], xts[par][:, :], AF.Square, scale=1.0 / 32.0,
            accum_out=lnt[:, 8:9]), waits=W((xpar[par], ldv)))
        sdone[par] = ssq
        pl.ch("vector", lambda e, par=par: e.reduce_sum(
            lnt[:, 0:1], xts[par][:, :], AX.X), waits=W((xpar[par], ldv)))
        pl.ch("vector", lambda e: e.tensor_scalar_mul(lnt[:, 1:2], lnt[:, 0:1], 1.0 / D))
        pl.ch("vector", lambda e: e.tensor_mul(lnt[:, 2:3], lnt[:, 1:2], lnt[:, 1:2]))
        pl.ch("vector", lambda e: e.tensor_sub(lnt[:, 3:4], lnt[:, 8:9], lnt[:, 2:3]),
              waits=W(("ch_scalar", ssq)))
        pl.ch("vector", lambda e: e.tensor_scalar_add(lnt[:, 4:5], lnt[:, 3:4], 1e-5))
        sq = pl.ch("scalar", lambda e: e.sqrt(lnt[:, 5:6], lnt[:, 4:5]),
                   waits=W(("ch_vector", C["ch_vector"])))
        pl.ch("vector", lambda e: e.reciprocal(lnt[:, 6:7], lnt[:, 5:6]),
              waits=W(("ch_scalar", sq)))
        vn = pl.ch("vector", lambda e, j=j, par=par: e.tensor_scalar(
            xnt_bufs[par][:, :], xts[par][:, :], lnt[:, 1:2], lnt[:, 6:7],
            ALU.subtract, ALU.mult),
            waits=W(("spillC", xn_spill[par]), ("pe", tr_pe[par])))
        vdone[par] = vn
        pl.op("sync", lambda e, j=j, par=par: e.dma_start(
            out=xn_d[j], in_=xnt_bufs[par][:, :]),
            waits=W(("ch_vector", vn)), inc="spillC", inc_k=16)
        xn_spill[par] = C["spillC"]
        for dc in range(NDC):
            pt = pl.op("tensor", lambda e, j=j, dc=dc, par=par: e.transpose(
                pbt[:, :], xnt_bufs[par][:, dc * P:(dc + 1) * P], ident_s[:, :]),
                waits=W(("ch_vector", vn), ("ldq", ld0), ("ch_vector", vtr)), inc="pe")
            vtr = pl.ch("vector", lambda e, j=j, dc=dc: e.tensor_copy(
                xnT[:, dc, j * P:(j + 1) * P], pbt[:, :]), waits=W(("pe", pt)))
        tr_pe[par] = C["pe"]
    vch_p0 = C["ch_vector"]
    spill_xn = C["spillC"]
    for dc in range(NDC):
        pl.op("sync", lambda e, dc=dc: e.dma_start(out=xnT_d[dc], in_=xnT[:, dc, :]),
              waits=W(("ch_vector", vch_p0)), inc="spillA", inc_k=16)
    spill_xnT = C["spillA"]

    s1w_s = carve(OFF_A, [P, NDC, SHID], BF16)
    h1c = [g0_t[0][:, 0:SB3], g0_t[1][:, 0:SB3]]
    for dc in range(NDC):
        pl.op("gpsimd", lambda e, dc=dc: e.dma_start(
            out=s1w_s[:, dc, :], in_=s1w_d[dc * P:(dc + 1) * P, :]),
            inc="moew", inc_k=16)
    s1w_ld = C["moew"]
    p3a_st = {"i": 0, "sgels": [0, 0, 0, 0], "spills": [0, 0]}

    def p3a_fill(n):
        for _ in range(n):
            i = p3a_st["i"]
            if i >= NSB3 * NHT:
                return
            p3a_st["i"] = i + 1
            sbi, ht = divmod(i, NHT)
            bank = pb_[i % 4]
            w4 = p3a_st["sgels"][i % 4]
            for dc in range(NDC):
                pl.op("tensor", lambda e, sbi=sbi, ht=ht, dc=dc, bank=bank: e.matmul(
                    bank[:, 0:SB3], s1w_s[:, dc, ht * P:(ht + 1) * P],
                    xnT[:, dc, sbi * SB3:(sbi + 1) * SB3],
                    start=(dc == 0), stop=(dc == NDC - 1)),
                    waits=W(("moew", s1w_ld), ("ch_vector", vch_p0),
                            ("ch_scalar", w4 if dc == 0 else 0)),
                    inc="pe" if dc == NDC - 1 else None)
            pe_h = C["pe"]
            cb = i % 2
            sgel = pl.ch("scalar", lambda e, cb=cb, ht=ht, bank=bank: e.activation(
                h1c[cb][:, :], bank[:, 0:SB3], AF.Gelu, bias=s1b_s[:, ht:ht + 1]),
                waits=W(("pe", pe_h), ("ldq", ld0),
                        ("spillH", p3a_st["spills"][cb])))
            p3a_st["sgels"][i % 4] = sgel
            pl.op("sync", lambda e, sbi=sbi, ht=ht, cb=cb: e.dma_start(
                out=h1_d[sbi, ht], in_=h1c[cb][:, :]),
                waits=W(("ch_scalar", sgel)), inc="spillH", inc_k=16)
            p3a_st["spills"][cb] = C["spillH"]

    env = dict(locals())

    if STAGE >= 3:
        build_routing(pl, nc, env)
    p3a_fill(NSB3 * NHT)
    env["p3a_pe"] = C["pe"]
    env["p3a_sc"] = C["ch_scalar"]
    env["spill_h1"] = C["spillH"]
    env["gq_scat"] = C["gq"]
    env["v_routing"] = C["ch_vector"]

    if STAGE >= 2:
        build_attention(pl, nc, env)
    else:
        vza = pl.ch("vector", lambda e: e.memset(P0c["sqs"][:, :], 0.0),
                    waits=W(("ch_scalar", C["ch_scalar"])))
        ztile = P0c["sqs"].bitcast(BF16)
        for j in range(NT):
            pl.op("sync", lambda e, j=j: e.dma_start(out=attn_d[j], in_=ztile[:, 0:D]),
                  waits=W(("ch_vector", vza)), inc="spillB", inc_k=16)
    env["spill_attn"] = C["spillB"]

    if STAGE >= 3:
        build_moe(pl, nc, env)
    env["p2_pe"] = C["pe"]
    env["p2_v"] = C["ch_vector"]
    env["p2_s"] = C["ch_scalar"]
    env["p2_gq"] = C["gq"]

    build_p3(pl, nc, env)

    pl.finalize()
    return nc


def build_routing(pl, nc, g):
    C = pl.cnt
    carve = g["carve"]
    pb_ = g["pb_"]
    xnT = g["xnT"]; xn_d = g["xn_d"]; xe_d = g["xe_d"]
    pwt_d = g["pwt_d"]; pb_d = g["pb_d"]; eb_d = g["eb_d"]
    rt_d = g["rt_d"]; rtall_d = g["rtall_d"]; huge_c = g["huge_c"]
    ones_bf = g["ones_bf"]; ones_f = g["ones_f"]
    onesc_bf = g["onesc_bf"]; onesc_f = g["onesc_f"]
    lstrict_s = g["lstrict_s"]
    stab0 = g["stab0"]; stab1 = g["stab1"]; w0f = g["w0f"]; w1f = g["w1f"]
    vch_p0 = g["vch_p0"]; spill_xn = g["spill_xn"]
    fill = g.get("p3a_fill") or (lambda n: None)
    zer_c = g["zer_c"]

    R1 = {}
    off = OFF_C
    for name, shape, dtype in [
        ("pwt", [P, NDC, NE], BF16), ("prow", [1, NE], BF16),
        ("ebrow", [1, NE], F32), ("eb_bc", [P, NE], F32),
        ("z", [P, NT, NE], F32), ("zs", [P, NT, NE], F32),
        ("v8", [P, NE], F32), ("i8", [P, NE], U32), ("i8f", [P, NE], F32),
        ("v12", [P, NT, 2], F32), ("e12", [P, NT, 2], U32),
        ("ebr", [P, NT * NE], F32),
        ("w1", [P, NT], F32), ("w2", [P, NT], F32),
        ("e1", [P, NT], F32), ("e2", [P, NT], F32),
        ("wall", [P, 512], F32), ("eall", [P, 512], F32),
        ("mask8", [P, NE, 512], BF16), ("gtmp", [P, 512], F32),
        ("cnts", [P, NE], F32), ("rows", [1, 64], F32),
        ("hibc", [P, NE], F32),
        ("m0", [P, NT], F32), ("m1", [P, NT], F32),
        ("pos0", [P, NT], F32), ("pos1", [P, NT], F32),
        ("acc0a", [P, NT], F32), ("acc0b", [P, NT], F32),
        ("acc1a", [P, NT], F32), ("acc1b", [P, NT], F32),
        ("k0", [P, NT], F32), ("k1", [P, NT], F32),
        ("sc1", [P, NT], F32), ("sc2", [P, NT], F32),
        ("rsum", [P, 4], F32), ("rsbf", [P, 4], BF16),
        ("hugef", [P, NT], F32), ("zrow", [P, NT], F32),
        ("gtu", [1, NE], U32), ("m2u", [P, NT], U32),
    ]:
        R1[name] = carve(off, shape, dtype)
        off += int(np.prod(shape[1:])) * mybir.dt.size(dtype)
    assert off <= ARENA, off

    rel1 = [("ch_vector", vch_p0), ("ch_scalar", C["ch_scalar"])]
    for dc in range(NDC):
        pl.op("gpsimd", lambda e, dc=dc: e.dma_start(
            out=R1["pwt"][:, dc, :], in_=pwt_d[dc * P:(dc + 1) * P, :]),
            waits=rel1, inc="gq", inc_k=16)
    pl.op("gpsimd", lambda e: e.dma_start(out=R1["prow"][:, :], in_=pb_d[:, :]),
          inc="gq", inc_k=16)
    pl.op("sync", lambda e: e.dma_start(out=R1["ebrow"][:, :], in_=eb_d[:, :]),
          waits=rel1, inc="ldq", inc_k=16)
    pl.op("sync", lambda e: e.dma_start(out=R1["hugef"][:, :], in_=huge_c[:, :]),
          waits=rel1, inc="ldq", inc_k=16)
    gq_rt = C["gq"]; ld_rt = C["ldq"]

    peb = pl.op("tensor", lambda e: e.matmul(
        pb_[6][:, 0:NE], ones_f[:, :], R1["ebrow"][:, :], start=True, stop=True),
        waits=W(("ldq", ld_rt), ("ch_vector", g["vch_bo"])), inc="pe")
    v_eb = pl.ch("vector", lambda e: e.tensor_copy(R1["eb_bc"][:, :], pb_[6][:, 0:NE]),
                 waits=W(("pe", peb)))
    for j in range(NT):
        pl.ch("vector", lambda e, j=j: e.tensor_copy(
            R1["ebr"][:, j * NE:(j + 1) * NE], R1["eb_bc"][:, :]))
    v_eb = C["ch_vector"]
    pl.ch("vector", lambda e: e.memset(R1["zrow"][:, :], 0.0))

    for j in range(NT):
        for dc in range(NDC):
            pl.op("tensor", lambda e, j=j, dc=dc: e.matmul(
                pb_[5][:, 0:NE], xnT[:, dc, j * P:(j + 1) * P], R1["pwt"][:, dc, :],
                start=(dc == 0), stop=False),
                waits=W(("gq", gq_rt),
                        ("ch_vector", C["ch_vector"] if dc == 0 else 0)),
                inc=None)
        pz = pl.op("tensor", lambda e: e.matmul(
            pb_[5][:, 0:NE], ones_bf[:, :], R1["prow"][:, :], start=False, stop=True),
            inc="pe")
        pl.ch("vector", lambda e, j=j: e.tensor_copy(R1["z"][:, j, :], pb_[5][:, 0:NE]),
              waits=W(("pe", pz)))
    vz = C["ch_vector"]
    sgm = pl.ch("scalar", lambda e: e.activation(
        R1["zs"].rearrange("p a b -> p (a b)"), R1["z"].rearrange("p a b -> p (a b)"),
        AF.Sigmoid), waits=W(("ch_vector", vz)))
    vadd = pl.ch("vector", lambda e: e.tensor_add(
        R1["zs"].rearrange("p a b -> p (a b)"), R1["zs"].rearrange("p a b -> p (a b)"),
        R1["ebr"][:, :]), waits=W(("ch_scalar", sgm), ("ch_vector", v_eb)))
    pl.ch("vector", lambda e: e.tensor_scalar_max(
        R1["zs"].rearrange("p a b -> p (a b)"),
        R1["zs"].rearrange("p a b -> p (a b)"), 1e-12))
    for j in range(NT):
        pl.ch("vector", lambda e, j=j: e.max(R1["v8"][:, :], R1["zs"][:, j, :]))
        pl.ch("vector", lambda e, j=j: e.max_index(
            R1["i8"][:, :], R1["v8"][:, :], R1["zs"][:, j, :]))
        pl.ch("vector", lambda e, j=j: e.tensor_copy(
            R1["v12"][:, j, :], R1["v8"][:, 0:2]))
        pl.ch("vector", lambda e, j=j: e.tensor_copy(
            R1["e12"][:, j, :], R1["i8"][:, 0:2]))
    pl.ch("vector", lambda e: e.tensor_add(
        R1["w2"][:, :], R1["v12"][:, :, 0], R1["v12"][:, :, 1]))
    pl.ch("vector", lambda e: e.reciprocal(R1["w1"][:, :], R1["w2"][:, :]))
    pl.ch("vector", lambda e: e.tensor_mul(
        R1["w2"][:, :], R1["v12"][:, :, 1], R1["w1"][:, :]))
    pl.ch("vector", lambda e: e.tensor_mul(
        R1["w1"][:, :], R1["v12"][:, :, 0], R1["w1"][:, :]))
    pl.ch("vector", lambda e: e.tensor_copy(R1["e1"][:, :], R1["e12"][:, :, 0]))
    pl.ch("vector", lambda e: e.tensor_copy(R1["e2"][:, :], R1["e12"][:, :, 1]))
    v_top2 = C["ch_vector"]

    for fi, nm in enumerate(("w1", "w2", "e1", "e2")):
        pl.op("gpsimd", lambda e, fi=fi, nm=nm: e.dma_start(
            out=rt_d[fi].rearrange("(p j) -> p j", p=P), in_=R1[nm][:, :]),
            waits=W(("ch_vector", v_top2)), inc="gq", inc_k=16)
    gq_rtw = C["gq"]
    pl.op("gpsimd", lambda e: e.collective_compute(
        "AllGather", ALU.bypass, replica_groups=[list(range(NCORES))],
        ins=[rt_d[:, :]], outs=[rtall_d[:, :]]),
        waits=W(("gq", gq_rtw)), inc="cc", inc_k=1)
    fill(16)
    for cc_ in range(NCORES):
        for sl in range(2):
            col = sl * 256 + cc_ * 32
            pl.op("gpsimd", lambda e, cc_=cc_, sl=sl, col=col: e.dma_start(
                out=R1["wall"][:, col:col + 32],
                in_=rtall_d[4 * cc_ + sl].rearrange("(p j) -> p j", p=P)),
                waits=W(("cc", 1)), inc="ldq", inc_k=16)
            pl.op("gpsimd", lambda e, cc_=cc_, sl=sl, col=col: e.dma_start(
                out=R1["eall"][:, col:col + 32],
                in_=rtall_d[4 * cc_ + 2 + sl].rearrange("(p j) -> p j", p=P)),
                waits=W(("cc", 1)), inc="ldq", inc_k=16)
    ld_we = C["ldq"]

    for ex in range(NE):
        pl.ch("vector", lambda e, ex=ex: e.tensor_single_scalar(
            R1["mask8"][:, ex, :], R1["eall"][:, :], float(ex), ALU.is_equal),
            waits=W(("ldq", ld_we), ("spillC", spill_xn)))
    v_masks = C["ch_vector"]

    R = R1["rows"]
    lo_s, hi_s, lo_a, hi_a = 0, 8, 40, 48
    pl.ch("vector", lambda e: e.memset(R[:, lo_s:lo_s + 8], 0.0))
    pl.ch("vector", lambda e: e.memset(R[:, hi_s:hi_s + 8], 1.0))
    cur = 0
    for it in range(BISECT_ITERS):
        lo0 = lo_s if cur == 0 else lo_a
        hi0 = hi_s if cur == 0 else hi_a
        lo1 = lo_a if cur == 0 else lo_s
        hi1 = hi_a if cur == 0 else hi_s
        pl.ch("vector", lambda e, lo0=lo0, hi0=hi0: e.tensor_add(
            R[:, 16:24], R[:, lo0:lo0 + 8], R[:, hi0:hi0 + 8]))
        vmid = pl.ch("vector", lambda e: e.tensor_scalar_mul(R[:, 16:24], R[:, 16:24], 0.5))
        pm = pl.op("tensor", lambda e: e.matmul(
            pb_[6][:, 0:NE], ones_f[:, :], R[:, 16:24], start=True, stop=True),
            waits=W(("ch_vector", vmid)), inc="pe")
        vbc = pl.ch("vector", lambda e: e.tensor_copy(R1["hibc"][:, :], pb_[6][:, 0:NE]),
                    waits=W(("pe", pm)))
        fill(8)
        for ex in range(NE):
            pl.ch("vector", lambda e, ex=ex: e.tensor_scalar(
                R1["gtmp"][:, :], R1["wall"][:, :], R1["hibc"][:, ex:ex + 1], None,
                ALU.is_ge), waits=W(("ch_vector", max(vbc, v_masks))))
            pl.ch("vector", lambda e, ex=ex: e.tensor_mul(
                R1["gtmp"][:, :], R1["gtmp"][:, :], R1["mask8"][:, ex, :]))
            pl.ch("vector", lambda e, ex=ex: e.reduce_sum(
                R1["cnts"][:, ex:ex + 1], R1["gtmp"][:, :], AX.X))
        vcnt = C["ch_vector"]
        pcm = pl.op("tensor", lambda e: e.matmul(
            pb_[5][0:1, 0:NE], onesc_f[:, :], R1["cnts"][:, :], start=True, stop=True),
            waits=W(("ch_vector", vcnt)), inc="pe")
        pl.ch("vector", lambda e: e.tensor_copy(R[:, 24:32], pb_[5][0:1, 0:NE]),
              waits=W(("pe", pcm)))
        pl.ch("vector", lambda e: e.tensor_single_scalar(
            R1["gtu"][:, :], R[:, 24:32], CAP, ALU.is_gt))
        pl.ch("vector", lambda e, lo0=lo0, lo1=lo1: e.select(
            R[:, lo1:lo1 + 8], R1["gtu"][:, :], R[:, 16:24], R[:, lo0:lo0 + 8],
            add_drain=True))
        pl.ch("vector", lambda e, hi0=hi0, hi1=hi1: e.select(
            R[:, hi1:hi1 + 8], R1["gtu"][:, :], R[:, hi0:hi0 + 8], R[:, 16:24],
            add_drain=True))
        fill(2)
        cur = 1 - cur
    hi_fin = hi_s if cur == 0 else hi_a
    vhi = C["ch_vector"]
    ph = pl.op("tensor", lambda e, hi_fin=hi_fin: e.matmul(
        pb_[6][:, 0:NE], ones_f[:, :], R[:, hi_fin:hi_fin + 8], start=True, stop=True),
        waits=W(("ch_vector", vhi)), inc="pe")
    v_hibc = pl.ch("vector", lambda e: e.tensor_copy(R1["hibc"][:, :], pb_[6][:, 0:NE]),
                   waits=W(("pe", ph)))
    fill(4)

    pl.ch("vector", lambda e: e.tensor_copy(R1["acc0a"][:, :], R1["hugef"][:, :]),
          waits=W(("ldq", ld_rt)))
    pl.ch("vector", lambda e: e.tensor_copy(R1["acc1a"][:, :], R1["hugef"][:, :]))
    pl.ch("vector", lambda e: e.memset(R1["k0"][:, :], 0.0))
    pl.ch("vector", lambda e: e.memset(R1["k1"][:, :], 0.0))
    acc = [[R1["acc0a"], R1["acc0b"]], [R1["acc1a"], R1["acc1b"]]]
    accur = [0, 0]
    for ex in range(NE):
        for sl in range(2):
            wsl = R1["w1"] if sl == 0 else R1["w2"]
            esl = R1["e1"] if sl == 0 else R1["e2"]
            msl = R1["m0"] if sl == 0 else R1["m1"]
            pl.ch("vector", lambda e, esl=esl, ex=ex: e.tensor_single_scalar(
                R1["sc1"][:, :], esl[:, :], float(ex), ALU.is_equal),
                waits=W(("ch_vector", v_hibc)))
            pl.ch("vector", lambda e, wsl=wsl, ex=ex: e.tensor_scalar(
                R1["sc2"][:, :], wsl[:, :], R1["hibc"][:, ex:ex + 1], None,
                ALU.is_ge))
            pl.ch("vector", lambda e, msl=msl: e.tensor_mul(
                msl[:, :], R1["sc1"][:, :], R1["sc2"][:, :]))
        pl.ch("vector", lambda e: e.reduce_sum(R1["rsum"][:, 0:1], R1["m0"][:, :], AX.X))
        pl.ch("vector", lambda e: e.reduce_sum(R1["rsum"][:, 1:2], R1["m1"][:, :], AX.X))
        vrs = pl.ch("vector", lambda e: e.tensor_copy(R1["rsbf"][:, 0:2], R1["rsum"][:, 0:2]))
        pt0 = pl.op("tensor", lambda e: e.matmul(
            pb_[6][:, 0:2], lstrict_s[:, :], R1["rsbf"][:, 0:2], start=True, stop=True),
            waits=W(("ch_vector", vrs)), inc="pe")
        pl.ch("vector", lambda e: e.tensor_copy(R1["rsum"][:, 2:4], pb_[6][:, 0:2]),
              waits=W(("pe", pt0)))
        pt1 = pl.op("tensor", lambda e: e.matmul(
            pb_[6][0:1, 4:5], R1["rsbf"][:, 0:1], onesc_bf[:, :], start=True, stop=True),
            waits=W(("ch_vector", vrs)), inc="pe")
        vt0 = pl.ch("vector", lambda e: e.tensor_copy(R1["rsbf"][0:1, 2:3], pb_[6][0:1, 4:5]),
                    waits=W(("pe", pt1)))
        pt2 = pl.op("tensor", lambda e: e.matmul(
            pb_[6][:, 6:7], ones_bf[:, :], R1["rsbf"][0:1, 2:3], start=True, stop=True),
            waits=W(("ch_vector", vt0)), inc="pe")
        pl.ch("vector", lambda e: e.tensor_copy(R1["rsum"][:, 0:1], pb_[6][:, 6:7]),
              waits=W(("pe", pt2)))
        fill(3)
        for sl in range(2):
            msl = R1["m0"] if sl == 0 else R1["m1"]
            pos = R1["pos0"] if sl == 0 else R1["pos1"]
            pl.ch("vector", lambda e, msl=msl: e.tensor_tensor_scan(
                R1["sc1"][:, :], R1["zrow"][:, :], msl[:, :], 0.0, ALU.add, ALU.add))
            pl.ch("vector", lambda e, msl=msl, pos=pos: e.tensor_sub(
                pos[:, :], R1["sc1"][:, :], msl[:, :]))
            pl.ch("vector", lambda e, pos=pos, sl=sl: e.tensor_scalar_add(
                pos[:, :], pos[:, :], R1["rsum"][:, 2 + sl:3 + sl]))
        pl.ch("vector", lambda e: e.tensor_scalar_add(
            R1["pos1"][:, :], R1["pos1"][:, :], R1["rsum"][:, 0:1]))
        for sl in range(2):
            msl = R1["m0"] if sl == 0 else R1["m1"]
            pos = R1["pos0"] if sl == 0 else R1["pos1"]
            ksl = R1["k0"] if sl == 0 else R1["k1"]
            a = acc[sl]
            cu = accur[sl]
            pl.ch("vector", lambda e, pos=pos: e.tensor_single_scalar(
                R1["sc1"][:, :], pos[:, :], float(CLOC), ALU.is_lt))
            pl.ch("vector", lambda e, msl=msl: e.tensor_mul(
                R1["sc1"][:, :], R1["sc1"][:, :], msl[:, :]))
            pl.ch("vector", lambda e, pos=pos, ex=ex: e.tensor_scalar_add(
                R1["sc2"][:, :], pos[:, :], float(ex * CLOC)))
            pl.ch("vector", lambda e: e.tensor_copy(R1["m2u"][:, :], R1["sc1"][:, :]))
            pl.ch("vector", lambda e, a=a, cu=cu: e.select(
                a[1 - cu][:, :], R1["m2u"][:, :], R1["sc2"][:, :], a[cu][:, :],
                add_drain=True))
            pl.ch("vector", lambda e, ksl=ksl: e.tensor_add(
                ksl[:, :], ksl[:, :], R1["sc1"][:, :]))
            accur[sl] = 1 - cu
    pl.ch("vector", lambda e: e.tensor_copy(stab0[:, :], acc[0][accur[0]][:, :]))
    pl.ch("vector", lambda e: e.tensor_copy(stab1[:, :], acc[1][accur[1]][:, :]))
    pl.ch("vector", lambda e: e.tensor_mul(w0f[:, :], R1["w1"][:, :], R1["k0"][:, :]))
    pl.ch("vector", lambda e: e.tensor_mul(w1f[:, :], R1["w2"][:, :], R1["k1"][:, :]))
    v_stab = C["ch_vector"]

    for b in range(NSLOT // P):
        pl.op("gpsimd", lambda e, b=b: e.dma_start(
            out=xe_d[b * P:(b + 1) * P, :], in_=zer_c[:, :]),
            inc="gq", inc_k=16)
    ld_pref = C["gq"]
    rls = g["rl_t"]
    rl_used = [0, 0]
    for j in range(NT):
        par = j % 2
        rsem = "wa" if par == 0 else "wb"
        pl.op("sync", lambda e, j=j, par=par: e.dma_start(
            out=rls[par][:, :], in_=xn_d[j]),
            waits=W(("spillC", spill_xn), ("gq", rl_used[par])),
            inc=rsem, inc_k=16)
        ld_rl = C[rsem]
        for sl in range(2):
            st = stab0 if sl == 0 else stab1
            pl.op("gpsimd", lambda e, j=j, st=st, par=par: e.indirect_dma_start(
                xe_d[:, :], IndirectOffsetOnAxis(st[:, j:j + 1], 0),
                rls[par][:, :], None,
                bounds_check=pl.breg(e), oob_is_err=False),
                waits=W(("ch_vector", v_stab), ("gq", ld_pref), (rsem, ld_rl)),
                inc="gq", inc_k=16)
        rl_used[par] = C["gq"]


def build_attention(pl, nc, g):
    C = pl.cnt
    carve = g["carve"]
    pb_ = g["pb_"]; pbig4 = g["pbig4"]; pbig = g["pbig"]
    dmask_c = g["dmask_c"]
    xnT = g["xnT"]
    wq_d = g["wq_d"]; wk_d = g["wk_d"]; wv_d = g["wv_d"]; wo_d = g["wo_d"]
    ctxT_d = g["ctxT_d"]; attn_d = g["attn_d"]
    cos_c = g["cos_c"]; sin_c = g["sin_c"]
    p128_s = g["p128_s"]; ones_f = g["ones_f"]; ones_bf = g["ones_bf"]
    ones_pp = g["ones_pp"]
    spill_xn = g["spill_xn"]; vch_p0 = g["vch_p0"]
    v_routing = g["v_routing"]; gq_scat = g["gq_scat"]

    A2 = {}
    offs = {}
    off = OFF_A
    for name, shape, dtype in [
        ("cosT", [P, S], BF16), ("sinT", [P, S], BF16),
        ("qraw0", [P, 512], BF16), ("qraw1", [P, 512], BF16),
        ("q2r", [P, S], BF16), ("k2r", [P, S], BF16),
        ("v2", [P, NT, 132], BF16),
        ("asb0", [P, 1024], BF16), ("asb1", [P, 1024], BF16),
        ("am0", [P, 1024], BF16), ("am1", [P, 1024], BF16),
        ("rt0", [P, 512], BF16), ("rt1", [P, 512], BF16),
        ("rcpb", [P, 1024], BF16),
        ("ctxe", [P, 1024], BF16),
        ("craw", [P, 1024], BF16),
        ("sums", [P, 1024], F32),
    ]:
        A2[name] = carve(off, shape, dtype)
        offs[name] = off
        off += int(np.prod(shape[1:])) * mybir.dt.size(dtype)
    assert off <= OFF_B, off
    amsf = carve(offs["am0"], [P, 1024], F32)
    offc = OFF_C + 18 * 1024
    for name, shape, dtype in [
        ("dmask", [P, 4, 1024], BF16),
        ("wqs", [P, NDC, P], BF16), ("wks", [P, NDC, P], BF16),
        ("wvs", [P, NDC, P], BF16),
    ]:
        A2[name] = carve(offc, shape, dtype)
        offc += int(np.prod(shape[1:])) * mybir.dt.size(dtype)
    assert offc <= ARENA, offc

    relC = [("spillC", spill_xn), ("ch_vector", vch_p0),
            ("ch_vector", v_routing), ("gq", gq_scat)]
    relA = relC + [("pe", g.get("p3a_pe", 0))]
    pl.op("sync", lambda e: e.dma_start(out=A2["cosT"][:, :], in_=cos_c[:, :]),
          waits=relA, inc="ldq", inc_k=16)
    pl.op("sync", lambda e: e.dma_start(out=A2["sinT"][:, :], in_=sin_c[:, :]),
          waits=relA, inc="ldq", inc_k=16)
    pl.op("sync", lambda e: e.dma_start(
        out=A2["dmask"][:, :, :],
        in_=dmask_c[:, :].rearrange("(a p) b -> p a b", p=P)),
        waits=relC, inc="ldq", inc_k=16)
    ld_cs = C["ldq"]
    vone = pl.ch("vector", lambda e: e.memset(A2["v2"][:, :, 64:66], 1.0), waits=relA)
    vone = pl.ch("vector", lambda e: e.memset(A2["v2"][:, :, 130:132], 1.0))
    v_v2ones = C["ch_vector"]

    qraws = [A2["qraw0"], A2["qraw1"]]
    kraws = qraws
    asb = [A2["asb0"], A2["asb1"]]
    ams = [A2["am0"], A2["am1"]]
    rts = [A2["rt0"], A2["rt1"]]
    NBLK = S // 512

    gen_pe_prev = 0
    norm_v_prev = 0
    exp_cons = [g.get("p3a_sc", 0), g.get("p3a_sc", 0)]
    ctx_read = [0, 0]
    pool_read = [0, 0]
    st_ctx = 0
    pending_norm = None
    carry_tails = []
    carry_norm = None
    bank45_free = [0]
    craw_rd_v = [0]
    sums_rd_v = [0]
    norm_pmm_pe = [0]

    for hp in range(NPAIR):
        for dc in range(NDC):
            pl.op("gpsimd", lambda e, hp=hp, dc=dc: e.dma_start(
                out=A2["wqs"][:, dc, :],
                in_=wq_d[dc * P:(dc + 1) * P, hp * P:(hp + 1) * P]),
                waits=relC + [("pe", gen_pe_prev)], inc="att", inc_k=16)
            pl.op("gpsimd", lambda e, hp=hp, dc=dc: e.dma_start(
                out=A2["wks"][:, dc, :],
                in_=wk_d[dc * P:(dc + 1) * P, hp * P:(hp + 1) * P]),
                inc="att", inc_k=16)
            pl.op("gpsimd", lambda e, hp=hp, dc=dc: e.dma_start(
                out=A2["wvs"][:, dc, :],
                in_=wv_d[dc * P:(dc + 1) * P, hp * P:(hp + 1) * P]),
                inc="att", inc_k=16)
        ld_w = C["att"]

        vev_prev = norm_v_prev
        sin_prev = norm_v_prev
        for which, wsl, raws, dst in (
                (0, A2["wqs"], qraws, A2["q2r"]), (1, A2["wks"], kraws, A2["k2r"])):
            for blk in range(NBLK):
                par = blk % 2
                for dc in range(NDC):
                    pl.op("tensor", lambda e, wsl=wsl, dc=dc, blk=blk: e.matmul(
                        pb_[4][:, :], wsl[:, dc, :],
                        xnT[:, dc, blk * 512:(blk + 1) * 512],
                        start=(dc == 0), stop=(dc == NDC - 1)),
                        waits=W(("att", ld_w),
                                ("ch_vector", vev_prev if dc == 0 else 0),
                                ("ch_scalar", bank45_free[0] if dc == 0 else 0)),
                        inc="pe" if dc == NDC - 1 else None)
                pe_q = C["pe"]
                vev = pl.ch("vector", lambda e, raws=raws, par=par: e.tensor_copy(
                    raws[par][:, :], pb_[4][:, :]), waits=W(("pe", pe_q)))
                vev_prev = vev
                pp = pl.op("tensor", lambda e, raws=raws, par=par: e.matmul(
                    pb_[5][:, :], p128_s[:, :], raws[par][:, :],
                    start=True, stop=True),
                    waits=W(("ch_vector", vev), ("ldq", g["ld0"]),
                            ("ch_vector", sin_prev),
                            ("ch_scalar", bank45_free[0])), inc="pe")
                sin_prev = pl.ch("vector", lambda e, blk=blk, par=par: e.tensor_mul(
                    rts[par][:, :], A2["sinT"][:, blk * 512:(blk + 1) * 512],
                    pb_[5][:, :]), waits=W(("pe", pp), ("ldq", ld_cs)))
                pl.ch("vector", lambda e, blk=blk, par=par, raws=raws: e.tensor_mul(
                    rts[1 - par][:, :], A2["cosT"][:, blk * 512:(blk + 1) * 512],
                    raws[par][:, :]))
                pl.ch("vector", lambda e, blk=blk, par=par, dst=dst: e.tensor_add(
                    dst[:, blk * 512:(blk + 1) * 512], rts[par][:, :],
                    rts[1 - par][:, :]))
        for j in range(NT):
            for dc in range(NDC):
                pl.op("tensor", lambda e, j=j, dc=dc: e.matmul(
                    pb_[4][:, 0:P], xnT[:, dc, j * P:(j + 1) * P],
                    A2["wvs"][:, dc, :],
                    start=(dc == 0), stop=(dc == NDC - 1)),
                    waits=W(("att", ld_w),
                            ("ch_vector", C["ch_vector"] if dc == 0 else 0)),
                    inc="pe" if dc == NDC - 1 else None)
            pe_v = C["pe"]
            pl.ch("vector", lambda e, j=j: e.tensor_copy(
                A2["v2"][:, j, 0:64], pb_[4][:, 0:64]),
                waits=W(("pe", pe_v), ("ch_vector", v_v2ones)))
            pl.ch("vector", lambda e, j=j: e.tensor_copy(
                A2["v2"][:, j, 66:130], pb_[4][:, 64:128]))
        v_gen = C["ch_vector"]
        gen_pe_prev = C["pe"]

        for J in range(NBLK):
            csw = [bank45_free[0]]
            ntile = 4 * J + 4
            tinfo = {}

            def emit_ctx(t, ntile=ntile, tinfo=tinfo, csw=csw):
                diag, par, sc_exp, pool_c = tinfo[t]
                for head in range(2):
                    src = (ams[par] if diag else asb[par])[:, head * 512:(head + 1) * 512]
                    bank = (pb_[4], pb_[5])[head]
                    voff = 0 if head == 0 else 66
                    pl.op("tensor", lambda e, t=t, head=head, src=src, bank=bank,
                          voff=voff: e.matmul(
                        bank[0:65, :], A2["v2"][:, t, voff:voff + 65], src,
                        start=(t == 0), stop=(t == ntile - 1)),
                        waits=W(("ch_vector", pool_c) if diag else
                                ("ch_scalar", sc_exp),
                                ("ch_scalar", csw[0] if t == 0 else 0)),
                        inc="pe" if head == 1 else None)
                if diag:
                    pool_read[par] = C["pe"]
                ctx_read[par] = C["pe"]

            for t in range(ntile):
                par = t % 2
                for head in range(2):
                    poff = 64 * head
                    bank = pbig4[:, par * 1024 + head * 512: par * 1024 + (head + 1) * 512]
                    pl.op("tensor", lambda e, hp=hp, t=t, J=J, head=head, bank=bank,
                          poff=poff: e.matmul(
                        bank,
                        A2["k2r"][poff:poff + 64, t * P:(t + 1) * P],
                        A2["q2r"][poff:poff + 64, J * 512:(J + 1) * 512],
                        start=True, stop=True, tile_position=(poff, 0)),
                        waits=W(("ch_scalar", exp_cons[par]),
                                ("ch_vector", v_gen)),
                        inc="pe" if head == 1 else None)
                pm = C["pe"]
                if t == 2 and carry_norm is not None:
                    carry_norm()()
                    carry_norm = None
                if t >= 2:
                    emit_ctx(t - 2)
                exp_cons[par] = pl.ch(
                    "scalar", lambda e, par=par: e.activation(
                        asb[par][:, :], pbig4[:, par * 1024:(par + 1) * 1024],
                        AF.Exp, scale=0.125),
                    waits=W(("pe", pm), ("pe", ctx_read[par])))
                sc_exp = exp_cons[par]
                diag = t >= 4 * J
                if diag:
                    di = t - 4 * J
                    pl.ch("vector", lambda e, par=par, di=di: e.tensor_mul(
                        ams[par][:, :], asb[par][:, :], A2["dmask"][:, di, :]),
                        waits=W(("ch_scalar", sc_exp), ("pe", pool_read[par]),
                                ("ldq", ld_cs)))
                tinfo[t] = (diag, par, sc_exp, C["ch_vector"])
            emit_ctx(ntile - 2)
            emit_ctx(ntile - 1)
            pe_ctx = C["pe"]

            craw_rd = craw_rd_v[0]
            s1 = pl.ch("scalar", lambda e: e.copy(
                A2["craw"][0:64, 0:512], pb_[4][0:64, :]),
                waits=W(("pe", pe_ctx), ("ch_vector", craw_rd)))
            pl.ch("scalar", lambda e: e.copy(
                A2["craw"][0:64, 512:1024], pb_[5][0:64, :]))
            s_free = pl.ch("scalar", lambda e: e.copy(
                A2["sums"][64:65, 0:1024], pbig[64:65, 0:1024]),
                waits=W(("ch_vector", sums_rd_v[0])))
            bank45_free[0] = s_free
            pl.ch("vector", lambda e: e.reciprocal(
                amsf[64:65, 0:512], A2["sums"][64:65, 0:512]),
                waits=W(("ch_scalar", s_free)))
            pl.ch("vector", lambda e: e.reciprocal(
                amsf[64:65, 512:1024], A2["sums"][64:65, 512:1024]))
            rcp_cast = pl.ch("vector", lambda e: e.tensor_copy(
                A2["rcpb"][64:65, 0:1024], amsf[64:65, 0:1024]),
                waits=W(("pe", norm_pmm_pe[0])))
            sums_rd_v[0] = rcp_cast

            def make_norm(rcp_cast=rcp_cast, hp=hp, J=J, s1=s1):
                def norm(csw_next=None):
                    nonlocal st_ctx, norm_v_prev
                    st_prev = st_ctx
                    tt = 0
                    for head in range(2):
                        pmm = pl.op("tensor", lambda e, head=head: e.matmul(
                            pb_[6][:, :], ones_pp[64:65, :],
                            A2["rcpb"][64:65, head * 512:(head + 1) * 512],
                            start=True, stop=True),
                            waits=W(("ch_vector", rcp_cast), ("ch_vector", tt)),
                            inc="pe")
                        tt = pl.ch("vector", lambda e, head=head: e.tensor_mul(
                            A2["ctxe"][0:64, head * 512:(head + 1) * 512],
                            A2["craw"][0:64, head * 512:(head + 1) * 512],
                            pb_[6][0:64, :]),
                            waits=W(("pe", pmm), ("attb", st_prev)))
                    v_norm = C["ch_vector"]
                    craw_rd_v[0] = v_norm
                    norm_pmm_pe[0] = C["pe"]
                    st_ctx = pl.op("sync", lambda e, hp=hp, J=J: e.dma_start(
                        out=ctxT_d[hp, 0:64, J * 512:(J + 1) * 512],
                        in_=A2["ctxe"][0:64, 0:512]),
                        waits=W(("ch_vector", v_norm)), inc="attb", inc_k=16)
                    st_ctx = pl.op("sync", lambda e, hp=hp, J=J: e.dma_start(
                        out=ctxT_d[hp, 64:128, J * 512:(J + 1) * 512],
                        in_=A2["ctxe"][0:64, 512:1024]),
                        inc="attb", inc_k=16)
                    norm_v_prev = v_norm
                    if csw_next is not None:
                        pass
                return norm

            carry_norm = make_norm
        if carry_norm is not None:
            carry_norm()()
            carry_norm = None
    ctx_stored = st_ctx
    g["pe_attn_jend"] = C["pe"]
    if STAGE >= 3:
        xeT0 = carve(OFF_C, [P, NDC, CLOC], BF16)
        for dc in range(NDC):
            pl.op("sync", lambda e, dc=dc: e.dma_start_transpose(
                xeT0[:, dc, :], g["xe_d"][0:CLOC, dc * P:(dc + 1) * P]),
                waits=W(("gq", gq_scat), ("ch_vector", C["ch_vector"])),
                inc="att", inc_k=16)
        g["ld_xe0"] = C["att"]

    off = OFF_A + 16 * 1024
    O2 = {}
    for name, shape, dtype in [
        ("woS", [P, NDC, D], BF16),
        ("ctxJ0", [P, NPAIR, 512], BF16),
        ("ctxJ1", [P, NPAIR, 512], BF16),
        ("ae0", [P, D], BF16),
        ("ae1", [P, D], BF16),
    ]:
        O2[name] = carve(off, shape, dtype)
        off += int(np.prod(shape[1:])) * mybir.dt.size(dtype)
    assert off <= OFF_B

    relO = [("pe", C["pe"]), ("ch_vector", C["ch_vector"]),
            ("ch_scalar", C["ch_scalar"]), ("ch_gpsimd", C["ch_gpsimd"])]
    for dc in range(NDC):
        pl.op("gpsimd", lambda e, dc=dc: e.dma_start(
            out=O2["woS"][:, dc, :], in_=wo_d[dc * P:(dc + 1) * P, :]),
            waits=relO, inc="att", inc_k=16)
    ld_wo = C["att"]
    ctxJ = [O2["ctxJ0"], O2["ctxJ1"]]
    aes = [O2["ae0"], O2["ae1"]]
    ctxJ_pe = [0, 0]
    ae_st = [0, 0]
    for J in range(NBLK):
        par = J % 2
        cjsem = "xa" if par == 0 else "xb"
        for pr in range(NPAIR):
            pl.op("sync", lambda e, J=J, pr=pr, par=par: e.dma_start(
                out=ctxJ[par][:, pr, :], in_=ctxT_d[pr, :, J * 512:(J + 1) * 512]),
                waits=relO + [("attb", ctx_stored), ("pe", ctxJ_pe[par])],
                inc=cjsem, inc_k=16)
        ld_cj = C[cjsem]
        for st in range(4):
            j = J * 4 + st
            for half in range(2):
                bank = pb_[half]
                for pr in range(NPAIR):
                    pl.op("tensor", lambda e, st=st, pr=pr, half=half, par=par,
                          bank=bank: e.matmul(
                        bank[:, :], ctxJ[par][:, pr, st * P:(st + 1) * P],
                        O2["woS"][:, pr, half * 512:(half + 1) * 512],
                        start=(pr == 0), stop=(pr == NPAIR - 1)),
                        waits=W(("att", ld_wo), (cjsem, ld_cj),
                                ("ch_vector", C["ch_vector"] if pr == 0 else 0)),
                        inc="pe" if pr == NPAIR - 1 else None)
            pe_o = C["pe"]
            apar = j % 2
            for half in range(2):
                pl.ch("vector", lambda e, half=half, apar=apar: e.tensor_copy(
                    aes[apar][:, half * 512:(half + 1) * 512], pb_[half][:, :]),
                    waits=W(("pe", pe_o), ("spillB", ae_st[apar])))
            v_ae = C["ch_vector"]
            pl.op("sync", lambda e, j=j, apar=apar: e.dma_start(
                out=attn_d[j], in_=aes[apar][:, :]),
                waits=W(("ch_vector", v_ae)), inc="spillB", inc_k=16)
            ae_st[apar] = C["spillB"]
        ctxJ_pe[par] = C["pe"]


def build_moe(pl, nc, g):
    C = pl.cnt
    carve = g["carve"]
    pb_ = g["pb_"]; pbig = g["pbig"]
    xe_d = g["xe_d"]; oe_d = g["oe_d"]
    f1w_d = g["f1w_d"]; f1b_d = g["f1b_d"]; f2w_d = g["f2w_d"]; f2b_d = g["f2b_d"]
    ones_bf = g["ones_bf"]
    gq_scat = g["gq_scat"]

    f2ws = [carve(OFF_A, [P, EH // P, D], BF16),
            carve(OFF_A + 32 * 1024, [P, EH // P, D], BF16)]
    hTs = [carve(OFF_B, [P, EH // P, 512], BF16),
           carve(OFF_B + 16 * 1024, [P, EH // P, 512], BF16)]
    f1w = carve(OFF_B + 32 * 1024, [P, NDC, EH], BF16)
    M2 = {"xeT": carve(OFF_C, [P, NDC, CLOC], BF16)}
    sbt = nc.alloc_sbuf_tensor
    oes = [sbt("moe_oe0", [P, D], BF16), sbt("moe_oe1", [P, D], BF16)]
    f1b_s = sbt("moe_f1b", [P, EH // P], F32)
    f2b_s = [sbt("moe_f2b0", [1, D], BF16), sbt("moe_f2b1", [1, D], BF16)]

    relM = [("pe", C["pe"]), ("ch_vector", C["ch_vector"]),
            ("ch_scalar", C["ch_scalar"]), ("ch_gpsimd", C["ch_gpsimd"]),
            ("gq", gq_scat)]
    NTH = CLOC // 512
    fc1_done = {}
    gelu_done = {}
    fc2_done = {}
    f1b_used = 0
    oe_st = [0, 0]
    st_oe = 0
    fc1bank_sc = [0, 0]

    def fc1_third(ex, th):
        gth = ex * NTH + th
        hb = hTs[gth % 2]
        ld_xe = ld_xe_e[ex]
        ld_f1 = ld_f1_e[ex]
        for ht in range(EH // P):
            bank = pbig[:, (ht % 2) * 512:(ht % 2 + 1) * 512]
            for dc in range(NDC):
                pl.op("tensor", lambda e, ht=ht, dc=dc, th=th, bank=bank: e.matmul(
                    bank, f1w[:, dc, ht * P:(ht + 1) * P],
                    M2["xeT"][:, dc, th * 512:(th + 1) * 512],
                    start=(dc == 0), stop=(dc == NDC - 1)),
                    waits=W(("att", ld_xe), ("moew", ld_f1),
                            ("ch_scalar", fc1bank_sc[ht % 2] if dc == 0 else 0)),
                    inc="pe" if dc == NDC - 1 else None)
            pe_h = C["pe"]
            fc1bank_sc[ht % 2] = pl.ch(
                "scalar", lambda e, ht=ht, bank=bank, hb=hb: e.activation(
                    hb[:, ht, :], bank, AF.Gelu, bias=f1b_s[:, ht:ht + 1]),
                waits=W(("pe", pe_h), ("moew", f1b_used),
                        ("pe", fc2_done.get(gth - 2, 0))))
        fc1_done[gth] = C["pe"]
        gelu_done[gth] = C["ch_scalar"]

    def fc2_third(ex, th):
        nonlocal st_oe
        gth = ex * NTH + th
        hb = hTs[gth % 2]
        f2w = f2ws[ex % 2]
        ld_f2 = ld_f2_e[ex]
        for st in range(4):
            for half in range(2):
                bank = pb_[half]
                for hc in range(EH // P):
                    pl.op("tensor", lambda e, st=st, hc=hc, half=half, bank=bank,
                          hb=hb: e.matmul(
                        bank[:, :], hb[:, hc, st * P:(st + 1) * P],
                        f2w[:, hc, half * 512:(half + 1) * 512],
                        start=(hc == 0), stop=False),
                        waits=W(("ch_scalar", gelu_done[gth]), ("moew", ld_f2),
                                ("ch_vector", oe_free_v[0] if hc == 0 else 0)),
                        inc=None)
                pl.op("tensor", lambda e, half=half, bank=bank, ex=ex: e.matmul(
                    bank[:, :], ones_bf[:, :],
                    f2b_s[ex % 2][:, half * 512:(half + 1) * 512],
                    start=False, stop=True),
                    waits=W(("moew", ld_f2)),
                    inc="pe" if half == 1 else None)
            pe_o = C["pe"]
            par = st % 2
            for half in range(2):
                pl.ch("vector", lambda e, half=half, par=par: e.tensor_copy(
                    oes[par][:, half * 512:(half + 1) * 512], pb_[half][:, :]),
                    waits=W(("pe", pe_o), ("spillC", oe_st[par])))
            oe_free_v[0] = C["ch_vector"]
            st_oe = pl.op("sync", lambda e, ex=ex, th=th, st=st, par=par: e.dma_start(
                out=oe_d[ex * CLOC + th * 512 + st * P:
                         ex * CLOC + th * 512 + (st + 1) * P, :],
                in_=oes[par][:, :]),
                waits=W(("ch_vector", oe_free_v[0])), inc="spillC", inc_k=16)
            oe_st[par] = st_oe
        fc2_done[gth] = C["pe"]

    oe_free_v = [0]
    ld_xe_e = {}
    ld_f1_e = {}
    ld_f2_e = {}

    relM0_f1 = [("gq", gq_scat), ("pe", g["pe_attn_jend"])]

    def emit_loads(ex):
        nonlocal f1b_used
        prev_fc1 = fc1_done.get(ex * NTH - 1, 0)
        if ex == 0:
            ld_xe_e[0] = g["ld_xe0"]
        else:
            for dc in range(NDC):
                pl.op("sync", lambda e, ex=ex, dc=dc: e.dma_start_transpose(
                    M2["xeT"][:, dc, :],
                    xe_d[ex * CLOC:(ex + 1) * CLOC, dc * P:(dc + 1) * P]),
                    waits=relM + [("pe", prev_fc1)], inc="att", inc_k=16)
            ld_xe_e[ex] = C["att"]
        for dc in range(NDC):
            pl.op("gpsimd", lambda e, ex=ex, dc=dc: e.dma_start(
                out=f1w[:, dc, :], in_=f1w_d[ex, dc * P:(dc + 1) * P, :]),
                waits=(relM0_f1 if ex == 0 else relM + [("pe", prev_fc1)]),
                inc="moew", inc_k=16)
        pl.op("gpsimd", lambda e, ex=ex: e.dma_start(
            out=f1b_s[:, :], in_=f1b_d[ex]),
            waits=(relM0_f1 if ex == 0 else
                   relM + [("ch_scalar", gelu_done.get(ex * NTH - 1, 0))]),
            inc="moew", inc_k=16)
        ld_f1_e[ex] = C["moew"]
        f1b_used = ld_f1_e[ex]
        prev_fc2 = fc2_done.get((ex - 2) * NTH + NTH - 1, 0)
        for hc in range(EH // P):
            pl.op("gpsimd", lambda e, ex=ex, hc=hc: e.dma_start(
                out=f2ws[ex % 2][:, hc, :], in_=f2w_d[ex, hc * P:(hc + 1) * P, :]),
                waits=relM + [("pe", prev_fc2)], inc="moew", inc_k=16)
        pl.op("gpsimd", lambda e, ex=ex: e.dma_start(
            out=f2b_s[ex % 2][:, :], in_=f2b_d[ex]),
            waits=relM + [("pe", fc2_done.get((ex - 2) * NTH + NTH - 1, 0))],
            inc="moew", inc_k=16)
        ld_f2_e[ex] = C["moew"]

    emit_loads(0)
    fc1_third(0, 0)
    pend = [(0, 0)]
    for ex in range(NE):
        for th in range(NTH):
            if (ex, th) == (0, 0):
                continue
            fc1_third(ex, th)
            fc2_third(*pend.pop(0))
            pend.append((ex, th))
            if th == NTH - 1 and ex + 1 < NE:
                emit_loads(ex + 1)
    fc2_third(*pend.pop(0))
    g["oe_st"] = st_oe


def build_p3(pl, nc, g):
    C = pl.cnt
    region = g["region"]
    pb_ = g["pb_"]
    xnT_d = g["xnT_d"]; attn_d = g["attn_d"]; x_d = g["x_d"]; out_d = g["out_d"]
    s1w_d = g["s1w_d"]; s2w_d = g["s2w_d"]; oe_d = g["oe_d"]; h1_d = g["h1_d"]
    ones_bf = g["ones_bf"]; s2b_row = g["s2b_row"]; s1b_s = g["s1b_s"]
    bo_bc = g["bo_bc"]; vch_bo = g["vch_bo"]
    stab0 = g["stab0"]; stab1 = g["stab1"]; w0f = g["w0f"]; w1f = g["w1f"]
    spill_xnT = g["spill_xnT"]; spill_attn = g["spill_attn"]
    gq0 = g["gq0"]; ld0 = g["ld0"]
    oe_done = g.get("oe_st", 0)
    p2_pe = g["p2_pe"]; p2_v = g["p2_v"]; p2_s = g["p2_s"]; p2_gq = g["p2_gq"]

    spill_h1 = g["spill_h1"]
    P3b, _ = region(OFF_B, [("s2w", [P, NHT, D], BF16)])
    P3h, _ = region(OFF_A, [
        ("h1b", [P, NHT, SB3], BF16),
        ("xa0", [P, D], F32),
        ("xa1", [P, D], F32),
        ("at0", [P, D], BF16),
        ("at1", [P, D], BF16),
        ("oa0", [P, D], F32),
        ("oa1", [P, D], F32),
        ("oa2", [P, D], F32),
    ])
    for hcx in range(NHT):
        pl.op("gpsimd", lambda e, hcx=hcx: e.dma_start(
            out=P3b["s2w"][:, hcx, :], in_=s2w_d[hcx * P:(hcx + 1) * P, :]),
            waits=W(("spillH", spill_h1), ("pe", p2_pe), ("gq", p2_gq)),
            inc="moew", inc_k=16)
    s2w_ld = C["moew"]

    xas = [P3h["xa0"], P3h["xa1"]]
    ats = [P3h["at0"], P3h["at1"]]
    g0s = g["g0_t"]
    g1s = g["g1_t"]
    oas = [P3h["oa0"], P3h["oa1"], P3h["oa2"]]

    for t_ in g["g0_t"] + g["g1_t"]:
        pl.ch("vector", lambda e, t_=t_: e.memset(t_[:, :], 0.0),
              waits=W(("spillH", spill_h1)))
    v_gz = C["ch_vector"]

    NJT = SB3 // P
    asm_consumed = [0, 0]
    oa_store = [0, 0, 0]
    peA_mark = p2_pe
    peB_mark = p2_pe
    st_total = 0
    oi = 0
    bank23_free_v = 0
    for sbi in range(NSB3):
        hsem = "wa" if sbi % 2 == 0 else "wb"
        for ht in range(NHT // 2):
            pl.op("sync", lambda e, sbi=sbi, ht=ht: e.dma_start(
                out=P3h["h1b"][:, ht, :], in_=h1_d[sbi, ht]),
                waits=W(("spillH", spill_h1), ("pe", peA_mark)),
                inc=hsem, inc_k=16)
        ld_h1A = C[hsem]
        for ht in range(NHT // 2, NHT):
            pl.op("sync", lambda e, sbi=sbi, ht=ht: e.dma_start(
                out=P3h["h1b"][:, ht, :], in_=h1_d[sbi, ht]),
                waits=W(("spillH", spill_h1), ("pe", peB_mark)),
                inc=hsem, inc_k=16)
        ld_h1B = C[hsem]
        for jt in range(NJT):
            j = sbi * NJT + jt
            apar = j % 2
            pl.op("sync", lambda e, j=j, apar=apar: e.dma_start(
                out=xas[apar][:, :], in_=x_d[j * P:(j + 1) * P, :]),
                waits=W(("ch_vector", asm_consumed[apar]), ("spillH", spill_h1),
                        ("pe", p2_pe)),
                inc="asm", inc_k=16)
            pl.op("sync", lambda e, j=j, apar=apar: e.dma_start(
                out=ats[apar][:, :], in_=attn_d[j]),
                waits=W(("spillB", spill_attn), ("pe", p2_pe)),
                inc="asm", inc_k=16)
            asm_ld = C["asm"]
            gq_g = 0
            if STAGE >= 3:
                pl.op("gpsimd", lambda e, j=j, apar=apar: e.indirect_dma_start(
                    g0s[apar][:, :], None,
                    oe_d[:, :], IndirectOffsetOnAxis(stab0[:, j:j + 1], 0),
                    bounds_check=pl.breg(e), oob_is_err=False),
                    waits=W(("ch_vector", max(asm_consumed[apar], v_gz)),
                            ("spillC", oe_done)),
                    inc="gq", inc_k=16)
                pl.op("gpsimd", lambda e, j=j, apar=apar: e.indirect_dma_start(
                    g1s[apar][:, :], None,
                    oe_d[:, :], IndirectOffsetOnAxis(stab1[:, j:j + 1], 0),
                    bounds_check=pl.breg(e), oob_is_err=False),
                    inc="gq", inc_k=16)
                gq_g = C["gq"]

            for hcx in range(NHT):
                for half in range(2):
                    isA_last = (jt == NJT - 1 and hcx == NHT // 2 - 1 and half == 1)
                    pl.op("tensor", lambda e, jt=jt, hcx=hcx, half=half: e.matmul(
                        pb_[2 + half][:, :], P3h["h1b"][:, hcx, jt * P:(jt + 1) * P],
                        P3b["s2w"][:, hcx, half * 512:(half + 1) * 512],
                        start=(hcx == 0), stop=False),
                        waits=W(("moew", s2w_ld),
                                (hsem, ld_h1A if hcx == 0 else
                                 (ld_h1B if hcx == NHT // 2 else 0)),
                                ("ch_vector", bank23_free_v if hcx == 0 else 0)),
                        inc="pe" if isA_last else None)
                    if isA_last:
                        peA_mark = C["pe"]
            for half in range(2):
                pl.op("tensor", lambda e, half=half: e.matmul(
                    pb_[2 + half][:, :], ones_bf[:, :],
                    s2b_row[:, half * 512:(half + 1) * 512], start=False, stop=True),
                    waits=W(("gq", gq0)), inc="pe")
            pe_sh = C["pe"]
            if jt == NJT - 1:
                peB_mark = pe_sh

            dst = oas[oi % 3]; oi += 1
            for half in range(2):
                pl.ch("vector", lambda e, half=half, apar=apar, dst=dst: e.tensor_add(
                    dst[:, half * 512:(half + 1) * 512], pb_[2 + half][:, :],
                    xas[apar][:, half * 512:(half + 1) * 512]),
                    waits=W(("pe", pe_sh), ("asm", asm_ld),
                            ("stq", oa_store[oas.index(dst)])))
            bank23_free_v = C["ch_vector"]
            cur = dst
            nxt = oas[oi % 3]; oi += 1
            pl.ch("vector", lambda e, cur=cur, nxt=nxt: e.tensor_add(
                nxt[:, :], cur[:, :], bo_bc[:, :]),
                waits=W(("ch_vector", vch_bo), ("stq", oa_store[oas.index(nxt)])))
            cur = nxt
            if STAGE >= 2:
                nxt = oas[oi % 3]; oi += 1
                pl.ch("vector", lambda e, cur=cur, nxt=nxt, apar=apar: e.tensor_add(
                    nxt[:, :], cur[:, :], ats[apar][:, :]),
                    waits=W(("stq", oa_store[oas.index(nxt)])))
                cur = nxt
            if STAGE >= 3:
                nxt = oas[oi % 3]; oi += 1
                pl.ch("vector", lambda e, cur=cur, nxt=nxt, apar=apar, j=j: e.scalar_tensor_tensor(
                    nxt[:, :], g0s[apar][:, :], w0f[:, j:j + 1], cur[:, :],
                    ALU.mult, ALU.add),
                    waits=W(("gq", gq_g), ("stq", oa_store[oas.index(nxt)])))
                cur = nxt
                nxt = oas[oi % 3]; oi += 1
                pl.ch("vector", lambda e, cur=cur, nxt=nxt, apar=apar, j=j: e.scalar_tensor_tensor(
                    nxt[:, :], g1s[apar][:, :], w1f[:, j:j + 1], cur[:, :],
                    ALU.mult, ALU.add),
                    waits=W(("stq", oa_store[oas.index(nxt)])))
                cur = nxt
            v_asm = C["ch_vector"]
            asm_consumed[apar] = v_asm
            st_total = pl.op("sync", lambda e, j=j, cur=cur: e.dma_start(
                out=out_d[j * P:(j + 1) * P, :], in_=cur[:, :]),
                waits=W(("ch_vector", v_asm)), inc="stq", inc_k=16)
            oa_store[oas.index(cur)] = st_total

    pl.op("sync", lambda e: e.nop(), waits=W(("stq", st_total)))


def kernel(L0_attn_x, L0_attn_Wq, L0_attn_Wk, L0_attn_Wv, L0_attn_Wo, L0_attn_bo,
           L0_ffn_proj_w, L0_ffn_proj_b, L0_ffn_expert_bias,
           L0_ffn_fc1_w, L0_ffn_fc1_b, L0_ffn_fc2_w, L0_ffn_fc2_b,
           L0_ffn_sh1_w, L0_ffn_sh1_b, L0_ffn_sh2_w, L0_ffn_sh2_b):
    x = np.ascontiguousarray(np.asarray(L0_attn_x, np.float32))

    idx = np.zeros(D, np.int64)
    for h in range(16):
        for half in range(2):
            for i in range(32):
                idx[h * 64 + half * 32 + i] = h * 64 + 2 * i + half
    wq = np.ascontiguousarray(np.asarray(L0_attn_Wq, np.float32)[:, idx])
    wk = np.ascontiguousarray(np.asarray(L0_attn_Wk, np.float32)[:, idx])
    wv = np.ascontiguousarray(np.asarray(L0_attn_Wv, np.float32))
    wo = np.ascontiguousarray(np.asarray(L0_attn_Wo, np.float32))
    bo = np.asarray(L0_attn_bo, np.float32).reshape(1, D)
    pwt = np.ascontiguousarray(np.asarray(L0_ffn_proj_w, np.float32).T)
    pbv = np.asarray(L0_ffn_proj_b, np.float32).reshape(1, NE)
    eb = np.asarray(L0_ffn_expert_bias, np.float32).reshape(1, NE)
    f1w = np.ascontiguousarray(np.asarray(L0_ffn_fc1_w, np.float32).transpose(0, 2, 1))
    f1b = np.ascontiguousarray(
        np.asarray(L0_ffn_fc1_b, np.float32).reshape(NE, EH // P, P).transpose(0, 2, 1))
    f2w = np.ascontiguousarray(np.asarray(L0_ffn_fc2_w, np.float32).transpose(0, 2, 1))
    f2b = np.ascontiguousarray(np.asarray(L0_ffn_fc2_b, np.float32).reshape(NE, 1, D))
    s1w = np.ascontiguousarray(np.asarray(L0_ffn_sh1_w, np.float32).T)
    s1b = np.ascontiguousarray(np.asarray(L0_ffn_sh1_b, np.float32).reshape(NHT, P).T)
    s2w = np.ascontiguousarray(np.asarray(L0_ffn_sh2_w, np.float32).T)
    s2b = np.asarray(L0_ffn_sh2_b, np.float32).reshape(1, D)

    nc = build()
    shared = dict(wq=wq, wk=wk, wv=wv, wo=wo, bo=bo, pwt=pwt, pb=pbv, eb=eb,
                  f1w=f1w, f1b=f1b, f2w=f2w, f2b=f2b,
                  s1w=s1w, s1b=s1b, s2w=s2w, s2b=s2b)
    in_maps = [dict(shared, x=np.ascontiguousarray(x[c])) for c in range(NCORES)]
    res = run_bass_kernel_spmd(nc, in_maps, core_ids=list(range(NCORES)))
    out = np.stack([res.results[c]["out"] for c in range(NCORES)], axis=0)
    return out.astype(np.float32)
